# version 02
# baseline (speedup 1.0000x reference)
import math
from contextlib import ExitStack

import numpy as np
import concourse.bass as bass
import concourse.mybir as mybir
from concourse.bass import IndirectOffsetOnAxis
from concourse.bass_utils import run_bass_kernel_spmd

F32 = mybir.dt.float32
BF16 = mybir.dt.bfloat16
F16 = mybir.dt.float16
I32 = mybir.dt.int32
U32 = mybir.dt.uint32
AF = mybir.ActivationFunctionType
ALU = mybir.AluOpType
AX = mybir.AxisListType

NCORES = 8
TPC = 2048
TSEG = 512
D = 1024
ATT_T = 4096
PI = float(np.pi)

ENGS = ("sync", "vector", "scalar", "tensor", "gpsimd")


class Prog:
    def __init__(self, nc):
        self.nc = nc
        self.ops = {e: [] for e in ENGS}
        self.sem_counts = {}

    def sem(self, name):
        if name not in self.sem_counts:
            self.sem_counts[name] = 0
        return name

    def count(self, name):
        return self.sem_counts[self.sem(name)]

    def bump(self, name, n):
        self.sem_counts[self.sem(name)] += n
        return self.sem_counts[name]

    def ins(self, eng, fn, waits=(), inc=None):
        if inc is not None:
            self.bump(inc[0], inc[1])
        self.ops[eng].append((fn, [(self.sem(s), v) for s, v in waits], inc))

    def barrier(self, eng, waits, sem=None):
        self.ins(eng, lambda e, s: e.engine_nop(), waits=waits,
                 inc=(sem, 1) if sem else None)

    def run(self, block, stack):
        handles = {n: stack.enter_context(self.nc.semaphore(n))
                   for n in self.sem_counts}

        def mk(name):
            def body(eng):
                for fn, waits, inc in self.ops[name]:
                    for (s, v) in waits:
                        if v > 0:
                            eng.wait_ge(handles[s], v)
                    ins = fn(eng, handles)
                    if inc is not None:
                        s, n = inc
                        assert ins is not None
                        ins.then_inc(handles[s], n)
            return body

        for e in ENGS:
            if self.ops[e]:
                getattr(block, e)(mk(e))



_CONSTS = None


def host_constants():
    global _CONSTS
    if _CONSTS is None:
        c = {}
        c["c_ones_col"] = np.ones((128, 1), np.float32)
        c["c_ones_row"] = np.ones((1, 128), np.float32)
        c["c_iota_col"] = np.arange(128, dtype=np.float32).reshape(128, 1)
        c["c_ident"] = np.eye(128, dtype=np.float32)
        T = ATT_T
        pos = np.arange(T, dtype=np.float32)
        inv = np.exp(np.arange(0, 64, 2, dtype=np.float32) * (-np.log(10000.0) / 64))
        ang = pos[None, :] * inv[:, None]
        c["c_rope_sin"] = np.sin(ang).astype(np.float16).astype(np.float32)
        c["c_rope_cos"] = np.cos(ang).astype(np.float16).astype(np.float32)
        c["c_iota_T"] = pos.reshape(1, T)
        ii = np.arange(128)[:, None]
        xx = np.arange(1024)[None, :]
        c["c_causal_big"] = np.where(ii - xx + 511 >= 0, 0.0, -4e30).astype(np.float32)
        for ne in (8, 16, 32):
            m = np.zeros((ne, ne * 128), np.float32)
            for e in range(ne):
                m[e, 128 * e:128 * (e + 1)] = 1.0
            c[f"c_sel{ne}"] = m
        c["c_lt128"] = (np.arange(128)[:, None] <
                        np.arange(128)[None, :]).astype(np.float32)
        _CONSTS = c
    return _CONSTS


class _Scope:

    def __init__(self, ctx):
        self.ctx = ctx
        self.stack = ExitStack()

    def __enter__(self):
        return self

    def __exit__(self, *a):
        return self.stack.__exit__(*a)

    def sb(self, name, shape, dtype):
        self.ctx.uid += 1
        return self.stack.enter_context(
            self.ctx.nc.sbuf_tensor(f"{name}_{self.ctx.uid}", shape, dtype))

    def ps(self, name, shape, dtype=F32):
        self.ctx.uid += 1
        return self.stack.enter_context(
            self.ctx.nc.psum_tensor(f"{name}_{self.ctx.uid}", shape, dtype))

    def close(self):
        self.stack.close()


class Ctx:
    def __init__(self, nc, prog, stack, T=TPC):
        self.nc = nc
        self.prog = prog
        self.stack = stack
        self.T = T
        self.uid = 0
        self.cst = {}
        self.cin = {}

    def sb(self, name, shape, dtype):
        self.uid += 1
        return self.stack.enter_context(
            self.nc.sbuf_tensor(f"{name}_{self.uid}", shape, dtype))

    def ps(self, name, shape, dtype=F32):
        self.uid += 1
        return self.stack.enter_context(
            self.nc.psum_tensor(f"{name}_{self.uid}", shape, dtype))

    def scope(self):
        s = _Scope(self)
        s.stack.__enter__()
        return s

    def load_const_after(self, name, waits, dtype=F32):
        hc = host_constants()
        bf = name.endswith("@bf")
        base = name[:-3] if bf else name
        arr = hc[base]
        if base not in self.cin:
            self.cin[base] = self.nc.dram_tensor(
                base, list(arr.shape), F32, kind="ExternalInput").ap()
        ap = self.cin[base]
        t = self.sb(name.replace("@", "_"), list(arr.shape), BF16 if bf else dtype)
        self.cst[name] = t
        self.prog.ins("gpsimd", lambda e, s, t=t, ap=ap: e.dma_start(
            out=t[:], in_=ap[:, :]), waits=waits, inc=("c_ld", 16))
        return t

    def use_consts(self, names):
        hc = host_constants()
        p = self.prog
        for n in names:
            if n in self.cst:
                continue
            bf = n.endswith("@bf")
            base = n[:-3] if bf else n
            arr = hc[base]
            if base not in self.cin:
                self.cin[base] = self.nc.dram_tensor(
                    base, list(arr.shape), F32, kind="ExternalInput").ap()
            ap = self.cin[base]
            t = self.sb(n.replace("@", "_"), list(arr.shape), BF16 if bf else F32)
            self.cst[n] = t
            eng = "gpsimd" if bf else "sync"
            p.ins(eng, lambda e, s, t=t, ap=ap: e.dma_start(out=t[:], in_=ap[:, :]),
                  inc=("c_ld", 16))
        return self.cst


def const_inmap(ctx):
    hc = host_constants()
    return {n: hc[n] for n in ctx.cin}




def _tt_b(e, out_ap, in_ap, small_ap, op):
    a, b = bass.broadcast_tensor_aps(in_ap, small_ap)
    return e.tensor_tensor(out_ap, a, b, op)

def emit_load_x(ctx, xT_ap, partials=None):
    p, T = ctx.prog, ctx.T
    xf = [ctx.sb(f"xf{k}", [128, T], F32) for k in range(8)]
    ld = p.sem("x_ld")
    for k in range(8):
        p.ins("sync", lambda e, s, k=k: e.dma_start(
            out=xf[k][:], in_=xT_ap[128 * k:128 * (k + 1), :]), inc=(ld, 16))
    if partials is None:
        return xf, [(ld, p.count(ld)), ("c_ld", p.count("c_ld"))]
    rv = p.sem("x_rdy")
    np_ = len(partials)
    pt = [ctx.sb(f"part{i}", [128, T], F32) for i in range(np_)]
    pl = p.sem("p_ld")
    for k in range(8):
        for i, pap in enumerate(partials):
            w = [(rv, np_ * k)] if (k > 0 and i == 0) else []
            p.ins("sync", lambda e, s, k=k, i=i, pap=pap: e.dma_start(
                out=pt[i][:], in_=pap[128 * k:128 * (k + 1), :]), waits=w, inc=(pl, 16))
        for i in range(np_):
            w = [(pl, 16 * np_ * (k + 1)), (ld, 16 * (k + 1))] if i == 0 else []
            p.ins("vector", lambda e, s, k=k, i=i: e.tensor_add(
                xf[k][:], xf[k][:], pt[i][:]), waits=w, inc=(rv, 1))
    return xf, [(rv, p.count(rv)), ("c_ld", p.count("c_ld"))]


def emit_ln(ctx, xf, x_ln, tag, ready):
    p, T = ctx.prog, ctx.T
    cst = ctx.cst
    nseg = T // TSEG
    keep = ctx.__dict__.get("keep_ln_stats")
    if keep:
        mu = ctx.sb(f"{tag}_mu", [1, T], F32)
        rstd = ctx.sb(f"{tag}_rstd", [1, T], F32)
    sc = ctx.scope()
    sq = sc.sb(f"{tag}_sq", [128, TSEG], F32)
    stat_q = sc.sb(f"{tag}_q", [1, T], F32)
    if not keep:
        mu = sc.sb(f"{tag}_mu", [1, T], F32)
        rstd = sc.sb(f"{tag}_rstd", [1, T], F32)
    ps_s = sc.ps(f"{tag}_pss", [1, TSEG])
    ps_q = sc.ps(f"{tag}_psq", [1, TSEG])
    ps_b = sc.ps(f"{tag}_psb", [128, TSEG])
    ps_b2 = sc.ps(f"{tag}_psb2", [128, TSEG])
    sv = p.sem(f"{tag}_v")
    ss = p.sem(f"{tag}_s")
    st = p.sem(f"{tag}_t")

    sq2b = sc.sb(f"{tag}_sq2b", [128, TSEG], F32)
    for g in range(nseg):
        sl = slice(g * TSEG, (g + 1) * TSEG)
        for k in range(8):
            b = sq if k % 2 == 0 else sq2b
            w = [(st, max(0, p.count(st) - 1))] if not (g == 0 and k <= 1) else (
                list(ready) if g == 0 else [(sv, p.count(sv))])
            p.ins("scalar", lambda e, s, k=k, sl=sl, b=b: e.activation(
                b[:], xf[k][:, sl], AF.Square), waits=w, inc=(ss, 1))
            w2 = []
            if g == 0 and k == 0:
                w2 = list(ready)
            elif k == 0:
                w2 = [(sv, p.count(sv))]
            p.ins("tensor", lambda e, s, k=k, sl=sl: e.matmul(
                ps_s[:, :], cst["c_ones_col"][:, :], xf[k][:, sl],
                start=(k == 0), stop=(k == 7)), waits=w2)
            p.ins("tensor", lambda e, s, k=k, b=b: e.matmul(
                ps_q[:, :], cst["c_ones_col"][:, :], b[:, :],
                start=(k == 0), stop=(k == 7)),
                waits=[(ss, p.count(ss))], inc=(st, 1))
        p.ins("vector", lambda e, s, sl=sl: e.tensor_scalar_mul(
            mu[:, sl], ps_s[:, :], 1.0 / D), waits=[(st, p.count(st))])
        p.ins("vector", lambda e, s, sl=sl: e.tensor_scalar_mul(
            stat_q[:, sl], ps_q[:, :], 1.0 / D), inc=(sv, 1))
    p.ins("vector", lambda e, s: e.tensor_mul(rstd[:], mu[:], mu[:]),
          waits=[(sv, p.count(sv))])
    p.ins("vector", lambda e, s: e.tensor_sub(stat_q[:], stat_q[:], rstd[:]))
    p.ins("vector", lambda e, s: e.tensor_scalar_add(stat_q[:], stat_q[:], 1e-5),
          inc=(sv, 1))
    p.ins("scalar", lambda e, s: e.activation(stat_q[:], stat_q[:], AF.Sqrt),
          waits=[(sv, p.count(sv))], inc=(ss, 1))
    p.ins("vector", lambda e, s: e.reciprocal(rstd[:], stat_q[:]),
          waits=[(ss, p.count(ss))], inc=(sv, 1))
    p.ins("vector", lambda e, s: e.engine_nop(), waits=[(sv, p.count(sv))])

    sq2 = sc.sb(f"{tag}_sq2", [128, TSEG], F32)
    for g in range(nseg):
        sl = slice(g * TSEG, (g + 1) * TSEG)
        w = [(sv, p.count(sv))]
        p.ins("tensor", lambda e, s, sl=sl: e.matmul(
            ps_b[:, :], cst["c_ones_row"][:, :], mu[:, sl], start=True, stop=True),
            waits=w)
        p.ins("tensor", lambda e, s, sl=sl: e.matmul(
            ps_b2[:, :], cst["c_ones_row"][:, :], rstd[:, sl], start=True, stop=True),
            inc=(st, 1))
        for k in range(8):
            w = [(st, p.count(st))] if k == 0 else []
            p.ins("vector", lambda e, s, k=k, sl=sl: e.tensor_sub(
                sq2[:, :], xf[k][:, sl], ps_b[:, :]), waits=w)
            p.ins("vector", lambda e, s, k=k, sl=sl: e.tensor_mul(
                x_ln[k][:, sl], sq2[:, :], ps_b2[:, :]),
                inc=(sv, 1) if k == 7 else None)
    sc.close()
    return [(sv, p.count(sv))], (mu if ctx.__dict__.get("keep_ln_stats") else None), (rstd if ctx.__dict__.get("keep_ln_stats") else None)


def emit_topk_gates(ctx, tag, probs, G_, E_, K, waits):
    p = ctx.prog
    gates = ctx.sb(f"{tag}_gates", [128, G_ * E_], F32)
    thr = ctx.sb(f"{tag}_thr", [128, G_], F32)
    ssum = ctx.sb(f"{tag}_gsum", [128, G_], F32)
    rsum = ctx.sb(f"{tag}_rsum", [128, G_], F32)
    keep = ctx.sb(f"{tag}_keep", [128, G_ * E_], F32)
    a3 = lambda t: t[:].rearrange("p (g e) -> p g e", e=E_)
    sv = p.sem(f"{tag}_v")

    def step(fn, w=()):
        p.ins("vector", fn, waits=list(w) + [(sv, p.count(sv))], inc=(sv, 1))

    step(lambda e, s: e.tensor_copy(keep[:], probs[:]), waits)
    for it in range(K - 1):
        step(lambda e, s: e.tensor_reduce(thr[:], a3(probs), axis=AX.X, op=ALU.max))
        step(lambda e, s: _tt_b(e, a3(gates), a3(probs), thr[:, :, None], ALU.is_ge))
        step(lambda e, s: e.scalar_tensor_tensor(
            probs[:], gates[:], -1e30, probs[:], op0=ALU.mult, op1=ALU.add))
    step(lambda e, s: e.tensor_reduce(thr[:], a3(probs), axis=AX.X, op=ALU.max))
    step(lambda e, s: _tt_b(e, a3(gates), a3(keep), thr[:, :, None], ALU.is_ge))
    step(lambda e, s: e.tensor_mul(gates[:], gates[:], keep[:]))
    step(lambda e, s: e.tensor_reduce(ssum[:], a3(gates), axis=AX.X, op=ALU.add))
    step(lambda e, s: e.reciprocal(rsum[:], ssum[:]))
    step(lambda e, s: _tt_b(e, a3(gates), a3(gates), rsum[:, :, None], ALU.mult))
    return gates, [(sv, p.count(sv))]


def emit_gate_rows(ctx, tag, gates, E_, waits):
    p, T = ctx.prog, ctx.T
    ntc = T // 128
    GT = ctx.sb(f"{tag}_GT", [E_, T], BF16)
    sc_ = ctx.scope()
    pt = sc_.ps(f"{tag}_pt", [E_, 128])
    st = p.sem(f"{tag}_t")
    sv = p.sem(f"{tag}_v")
    for tc in range(ntc):
        w = list(waits) if tc == 0 else [(sv, p.count(sv))]
        p.ins("tensor", lambda e, s, tc=tc: e.transpose(
            pt[:, :], gates[:, E_ * tc:E_ * (tc + 1)], ctx.cst["c_ident"][:, :]),
            waits=w, inc=(st, 1))
        p.ins("vector", lambda e, s, tc=tc: e.tensor_copy(
            GT[:, 128 * tc:128 * (tc + 1)], pt[:, :]),
            waits=[(st, p.count(st))], inc=(sv, 1))
    sc_.close()
    return GT, [(sv, p.count(sv))]


def emit_store_residual(ctx, tag, acc, xT_ap, partials, coef, yT_ap, waits):
    p, T = ctx.prog, ctx.T
    xre = ctx.sb(f"{tag}_xre", [128, T], F32)
    prt = [ctx.sb(f"{tag}_prt{i}", [128, T], F32)
           for i in range(len(partials or ()))]
    srl = p.sem(f"{tag}_rl")
    srv = p.sem(f"{tag}_rv")
    sro = p.sem(f"{tag}_ro")
    nl = 1 + len(partials or ())
    for k in range(8):
        w = list(waits) if k == 0 else [(srv, p.count(srv)), (sro, p.count(sro))]
        p.ins("sync", lambda e, s, k=k: e.dma_start(
            out=xre[:], in_=xT_ap[128 * k:128 * (k + 1), :]), waits=w, inc=(srl, 16))
        for i, pap in enumerate(partials or ()):
            p.ins("sync", lambda e, s, k=k, pap=pap, i=i: e.dma_start(
                out=prt[i][:], in_=pap[128 * k:128 * (k + 1), :]), inc=(srl, 16))
        p.ins("vector", lambda e, s, k=k: e.scalar_tensor_tensor(
            xre[:], acc[k][:], coef, xre[:], op0=ALU.mult, op1=ALU.add),
            waits=[(srl, p.count(srl))])
        for i in range(len(partials or ())):
            p.ins("vector", lambda e, s, k=k, i=i: e.tensor_add(
                xre[:], xre[:], prt[i][:]))
        p.barrier("vector", [], sem=srv)
        p.ins("gpsimd", lambda e, s, k=k: e.dma_start(
            out=yT_ap[128 * k:128 * (k + 1), :], in_=xre[:]),
            waits=[(srv, p.count(srv))], inc=(sro, 16))
    p.barrier("gpsimd", [(sro, p.count(sro))])



def emit_l0ffn(ctx, xT_ap, partials, waps, yT_ap, coef=0.5):
    p, T = ctx.prog, ctx.T
    E, H = 8, 2048
    nseg = T // TSEG
    ntc = T // 128
    nhc = H // 128
    cst = ctx.use_consts(["c_ones_col", "c_ones_row", "c_ident", "c_sel8@bf"])

    xf, ready = emit_load_x(ctx, xT_ap, partials)
    x_ln = [ctx.sb(f"xln{k}", [128, T], BF16) for k in range(8)]
    ln_done, _, _ = emit_ln(ctx, xf, x_ln, "l0ln", ready)

    wg_bf = ctx.sb("l0_wg", [128, 8 * 8], BF16)
    wld = p.sem("l0_wgld")
    p.ins("gpsimd", lambda e, s: e.dma_start(
        out=wg_bf[:].rearrange("p (k e) -> p k e", e=8),
        in_=waps["wg"].rearrange("(k p) e -> p k e", p=128)),
        waits=ln_done, inc=(wld, 16))
    ps_r = ctx.ps("l0_psr", [128, ntc * 8])
    sr = p.sem("l0_r")
    for tc in range(ntc):
        for k in range(8):
            w = ln_done + [(wld, 16)] if (tc == 0 and k == 0) else []
            p.ins("tensor", lambda e, s, tc=tc, k=k: e.matmul(
                ps_r[:, 8 * tc:8 * (tc + 1)],
                x_ln[k][:, 128 * tc:128 * (tc + 1)], wg_bf[:, 8 * k:8 * (k + 1)],
                start=(tc == 0 and k == 0), stop=(tc == ntc - 1 and k == 7)),
                waits=w, inc=(sr, 1) if (tc == ntc - 1 and k == 7) else None)

    r_exp = ctx.sb("l0_rexp", [128, ntc * 8], F32)
    r_m = ctx.sb("l0_rm", [128, ntc], F32)
    p.ins("vector", lambda e, s: e.tensor_reduce(
        r_m[:], ps_r[:].rearrange("p (g e) -> p g e", e=8), axis=AX.X, op=ALU.max,
        negate=True), waits=[(sr, p.count(sr))], inc=(p.sem("l0_rv"), 1))
    p.ins("vector", lambda e, s: _tt_b(
        e, r_exp[:].rearrange("p (g e) -> p g e", e=8),
        ps_r[:].rearrange("p (g e) -> p g e", e=8), r_m[:, :, None],
        ALU.add), waits=[("l0_rv", p.count("l0_rv"))], inc=(p.sem("l0_rv"), 1))
    p.ins("scalar", lambda e, s: e.activation(r_exp[:], r_exp[:], AF.Exp),
          waits=[("l0_rv", p.count("l0_rv"))], inc=(p.sem("l0_rs"), 1))
    gates, g_done = emit_topk_gates(ctx, "l0g", r_exp, ntc, 8, 2,
                                    [("l0_rs", p.count("l0_rs"))])
    GT, gt_done = emit_gate_rows(ctx, "l0gt", gates, 8, g_done)

    b2_bf = ctx.sb("l0_b2bf", [8, D], BF16)
    xsc = ctx.scope()
    w1_buf = xsc.sb("l0_w1", [128, 8 * H], BF16)
    w2_buf = xsc.sb("l0_w2", [128, nhc * D], BF16)
    b1_buf = xsc.sb("l0_b1", [128, nhc], F32)
    h_buf = xsc.sb("l0_h", [128, nhc * TSEG], BF16)
    gb_sb = xsc.sb("l0_gb", [128, TSEG], F32)
    sq_acc = xsc.sb("l0_sqacc", [128, TSEG], F32)
    acc = xf
    h_ps = [ctx.ps(f"l0_hps{i}", [128, TSEG]) for i in range(2)]
    y_ps = [ctx.ps(f"l0_yps{i}", [128, TSEG]) for i in range(2)]
    gb_ps = ctx.ps("l0_gbps", [128, TSEG])
    swl = p.sem("l0_wl")
    she = p.sem("l0_he")
    sht = p.sem("l0_ht")
    syt = p.sem("l0_yt")
    sgv = p.sem("l0_gv")
    sgt = p.sem("l0_gtp")

    def load_weights(eidx):
        p.ins("gpsimd", lambda e, s: e.engine_nop(),
              waits=[(syt, p.count(syt))] + (ln_done if eidx == 0 else []))
        p.ins("gpsimd", lambda e, s, eidx=eidx: e.dma_start(
            out=w1_buf[:].rearrange("p (k h) -> p k h", k=8),
            in_=waps["w1"][eidx].rearrange("(k p) h -> p k h", p=128)), inc=(swl, 16))
        p.ins("gpsimd", lambda e, s, eidx=eidx: e.dma_start(
            out=w2_buf[:].rearrange("p (k d) -> p k d", k=nhc),
            in_=waps["w2"][eidx].rearrange("(k p) d -> p k d", p=128)), inc=(swl, 16))
        p.ins("gpsimd", lambda e, s, eidx=eidx: e.dma_start(
            out=b1_buf[:],
            in_=waps["b1"][eidx]), inc=(swl, 16))

    h_free = {}
    y_free = {}
    gb_free = 0
    for e_i in range(E):
        load_weights(e_i)
        w_ready = p.count(swl)
        for g in range(nseg):
            sl = slice(g * TSEG, (g + 1) * TSEG)
            w = [(sgv, gb_free)]
            if e_i == 0 and g == 0:
                w += gt_done
            p.ins("tensor", lambda e, s, e_i=e_i, sl=sl: e.matmul(
                gb_ps[:, :], cst["c_sel8@bf"][:, 128 * e_i:128 * (e_i + 1)], GT[0:8, sl],
                start=True, stop=True), waits=w, inc=(sgt, 1))
            p.ins("vector", lambda e, s: e.tensor_copy(gb_sb[:], gb_ps[:, :]),
                  waits=[(sgt, p.count(sgt))], inc=(sgv, 1))
            for hc in range(nhc):
                for k in range(8):
                    w = []
                    if g == 0 and hc == 0 and k == 0:
                        w = [(swl, w_ready)]
                    if k == 0 and (hc % 2) in h_free:
                        w = w + [(she, h_free[hc % 2])]
                    p.ins("tensor", lambda e, s, hc=hc, k=k, sl=sl: e.matmul(
                        h_ps[hc % 2][:, :],
                        w1_buf[:, H * k + 128 * hc:H * k + 128 * (hc + 1)],
                        x_ln[k][:, sl], start=(k == 0), stop=(k == 7)),
                        waits=w, inc=(sht, 1) if k == 7 else None)
                w = [(sht, p.count(sht))]
                if hc == 0:
                    w += [(syt, p.count(syt))]
                p.ins("scalar", lambda e, s, hc=hc: e.activation(
                    h_buf[:, TSEG * hc:TSEG * (hc + 1)], h_ps[hc % 2][:, :],
                    AF.Relu, bias=b1_buf[:, hc:hc + 1]),
                    waits=w, inc=(she, 1))
                h_free[hc % 2] = p.count(she)
            for dc in range(8):
                for hc in range(nhc):
                    w = [(she, p.count(she))] if (dc == 0 and hc == 0) else []
                    if hc == 0 and (dc % 2) in y_free:
                        w = w + [(sgv, y_free[dc % 2])]
                    p.ins("tensor", lambda e, s, dc=dc, hc=hc: e.matmul(
                        y_ps[dc % 2][:, :],
                        w2_buf[:, D * hc + 128 * dc:D * hc + 128 * (dc + 1)],
                        h_buf[:, TSEG * hc:TSEG * (hc + 1)],
                        start=(hc == 0), stop=(hc == nhc - 1)),
                        waits=w, inc=(syt, 1) if hc == nhc - 1 else None)
                if e_i == 0:
                    p.ins("vector", lambda e, s, dc=dc, sl=sl: e.tensor_mul(
                        acc[dc][:, sl], y_ps[dc % 2][:, :], gb_sb[:]),
                        waits=[(syt, p.count(syt))], inc=(sgv, 1))
                else:
                    p.ins("vector", lambda e, s, dc=dc: e.tensor_mul(
                        sq_acc[:], y_ps[dc % 2][:, :], gb_sb[:]),
                        waits=[(syt, p.count(syt))])
                    p.ins("vector", lambda e, s, dc=dc, sl=sl: e.tensor_add(
                        acc[dc][:, sl], acc[dc][:, sl], sq_acc[:]), inc=(sgv, 1))
                y_free[dc % 2] = p.count(sgv)
            gb_free = p.count(sgv)

    xsc.close()
    b2_ps = ctx.ps("l0_b2ps", [128, TSEG])
    p.ins("gpsimd", lambda e, s: e.dma_start(out=b2_bf[:], in_=waps["b2"][:, :]),
          waits=ln_done, inc=(p.sem("l0_b2l"), 16))
    sbt = p.sem("l0_bt")
    for g in range(nseg):
        sl = slice(g * TSEG, (g + 1) * TSEG)
        for dc in range(8):
            w = [("l0_b2l", 16)] if (g == 0 and dc == 0) else []
            w = w + [(sgv, p.count(sgv))]
            p.ins("tensor", lambda e, s, dc=dc, sl=sl: e.matmul(
                b2_ps[:, :], b2_bf[:, 128 * dc:128 * (dc + 1)], GT[0:8, sl],
                start=True, stop=True), waits=w, inc=(sbt, 1))
            p.ins("vector", lambda e, s, dc=dc, sl=sl: e.tensor_add(
                acc[dc][:, sl], acc[dc][:, sl], b2_ps[:, :]),
                waits=[(sbt, p.count(sbt))], inc=(sgv, 1))

    emit_store_residual(ctx, "l0res", acc, xT_ap, partials, coef, yT_ap,
                        [(sgv, p.count(sgv))])


def build_l0ffn_launch(with_partials=False):
    nc = bass.Bass("TRN2", target_bir_lowering=False)
    T = TPC
    xT = nc.dram_tensor("xT", [D, T], F32, kind="ExternalInput").ap()
    partials = None
    if with_partials:
        partials = [nc.dram_tensor(f"p{i}", [D, T], F32, kind="ExternalInput").ap()
                    for i in range(2)]
    waps = {
        "wg": nc.dram_tensor("wg", [D, 8], F32, kind="ExternalInput").ap(),
        "w1": nc.dram_tensor("w1", [8, D, 2048], F32, kind="ExternalInput").ap(),
        "b1": nc.dram_tensor("b1", [8, 128, 16], F32, kind="ExternalInput").ap(),
        "w2": nc.dram_tensor("w2", [8, 2048, D], F32, kind="ExternalInput").ap(),
        "b2": nc.dram_tensor("b2", [8, D], F32, kind="ExternalInput").ap(),
    }
    yT = nc.dram_tensor("yT", [D, T], F32, kind="ExternalOutput").ap()
    prog = Prog(nc)
    with ExitStack() as stack:
        ctx = Ctx(nc, prog, stack, T)
        emit_l0ffn(ctx, xT, partials, waps, yT)
        block = stack.enter_context(nc.Block())
        prog.run(block, stack)
    return nc, ctx


_PROGS = {}


def _get_prog(tag, fn):
    if tag not in _PROGS:
        _PROGS[tag] = fn()
    return _PROGS[tag]


def kernel(**inputs):
    from concourse.bass_utils import run_bass_kernel_spmd
    x = np.asarray(inputs["L0_attn_x"], np.float32)
    B, S, Dm = x.shape
    TG = B * S
    TL = TG // 8
    cc = np.ascontiguousarray
    yg = cc(x.reshape(TG, Dm).T)
    zpart = np.zeros((Dm, TL), np.float32)

    def run8(prog, ims):
        nc, ctx = prog
        for im in ims:
            im.update(const_inmap(ctx))
        return run_bass_kernel_spmd(nc, ims, core_ids=list(range(8)))

    def ffn_pass(prog, wmap, yg, parts=None):
        ims = []
        for c in range(8):
            im = {"xT": cc(yg[:, TL * c:TL * (c + 1)])}
            im.update(wmap)
            if parts is None:
                im["p0"] = zpart
                im["p1"] = zpart
            else:
                pa, pb = parts
                b, hf = c // 2, c % 2
                im["p0"] = cc(pa[b][:, TL * hf:TL * hf + TL])
                im["p1"] = cc(pb[b][:, TL * hf:TL * hf + TL])
            ims.append(im)
        res = run8(prog, ims)
        return np.concatenate([res.results[c]["yT"] for c in range(8)], axis=1)

    def attn_pass(prog, wmap_fn, yg):
        ims = []
        for c in range(8):
            b, hf = c // 2, c % 2
            im = {"xT": cc(yg[:, S * b:S * (b + 1)])}
            im.update(wmap_fn(hf))
            ims.append(im)
        res = run8(prog, ims)
        pa = [res.results[2 * b]["yT"] for b in range(B)]
        pb = [res.results[2 * b + 1]["yT"] for b in range(B)]
        return pa, pb

    p_l0f = _get_prog("l0f", lambda: build_l0ffn_launch(True))
    w_l0f = {"wg": cc(inputs["L0_ffn_Wg"]), "w1": cc(inputs["L0_ffn_W1"]),
             "b1": cc(np.asarray(inputs["L0_ffn_b1"]).reshape(8, 16, 128)
                      .transpose(0, 2, 1)),
             "w2": cc(inputs["L0_ffn_W2"]), "b2": cc(inputs["L0_ffn_b2"])}
    yg = ffn_pass(p_l0f, w_l0f, yg)

    p_l0a = _get_prog("l0a", lambda: build_l0attn_launch(8))
    Wq0, Wk0 = np.asarray(inputs["L0_attn_Wq"]), np.asarray(inputs["L0_attn_Wk"])
    Wv0, Wo0 = np.asarray(inputs["L0_attn_Wv"]), np.asarray(inputs["L0_attn_Wo"])
    ph0 = np.asarray(inputs["L0_attn_q_phase_angle"], np.float32)
    ls0 = np.asarray(inputs["L0_attn_alibi_log_slopes"], np.float32)
    perm = np.concatenate([np.arange(0, 64, 2), np.arange(1, 64, 2)])

    def w_l0a(hf):
        heads = list(range(8 * hf, 8 * hf + 8))
        rows = np.concatenate([np.arange(64 * h, 64 * h + 64) for h in heads])
        return {"wqT": cc(np.concatenate([Wq0[:, 64 * h + perm] for h in heads],
                                         axis=1)),
                "wkT": cc(np.concatenate([Wk0[:, 64 * kv + perm]
                                          for kv in (2 * hf, 2 * hf + 1)], axis=1)),
                "wvT": cc(Wv0[:, 128 * hf:128 * hf + 128]),
                "woR": cc(Wo0[rows, :]),
                "phase": cc(ph0[None, 8 * hf:8 * hf + 8]),
                "slopes": cc(np.kron(ls0[8 * hf:8 * hf + 8],
                                     np.r_[1.0, np.zeros(7)]
                                     ).astype(np.float32)[None, :])}
    parts = attn_pass(p_l0a, w_l0a, yg)
    yg = ffn_pass(p_l0f, w_l0f, yg, parts)

    p_l1f = _get_prog("l1f", lambda: build_l1ffn_launch(True))
    w_l1f = {"gate_wT": cc(np.asarray(inputs["L1_ffn_gate_w"]).T),
             "corr_bias": cc(np.tile(np.asarray(inputs["L1_ffn_correction_bias"],
                                                np.float32), (128, 16))),
             "Wg": cc(inputs["L1_ffn_Wg"]), "Wu": cc(inputs["L1_ffn_Wu"]),
             "Wd": cc(inputs["L1_ffn_Wd"]), "Wg_s": cc(inputs["L1_ffn_Wg_s"]),
             "Wu_s": cc(inputs["L1_ffn_Wu_s"]), "Wd_s": cc(inputs["L1_ffn_Wd_s"])}
    yg = ffn_pass(p_l1f, w_l1f, yg)

    p_l1a = _get_prog("l1a", lambda: build_l1attn_launch(8))
    f32 = lambda k: np.asarray(inputs[k], np.float32)
    Wq1, bq1 = f32("L1_attn_Wq"), f32("L1_attn_bq")
    Wk1, bk1 = f32("L1_attn_Wk"), f32("L1_attn_bk")
    Wv1, bv1 = f32("L1_attn_Wv"), f32("L1_attn_bv")
    Wo1, bo1 = f32("L1_attn_Wo"), f32("L1_attn_bo")

    def w_l1a(hf):
        sl = slice(512 * hf, 512 * hf + 512)
        return {"wqT": cc(Wq1[:, sl]), "bq": cc(bq1[None, sl]),
                "wkT": cc(Wk1[:, sl]), "bk": cc(bk1[None, sl]),
                "wvT": cc(Wv1[:, sl]), "bv": cc(bv1[None, sl]),
                "woR": cc(Wo1[sl, :]),
                "boC": (cc(bo1.reshape(8, 128).T.astype(np.float32)) if hf == 0
                        else np.zeros((128, 8), np.float32))}
    parts = attn_pass(p_l1a, w_l1a, yg)
    yg = ffn_pass(p_l1f, w_l1f, yg, parts)

    p_l2c = _get_prog("l2c", lambda: build_l2cnt_launch(True))
    p_l2f = _get_prog("l2f", lambda: build_l2ffn_launch(True))
    w_router = cc(inputs["L2_ffn_w_router"])
    w_l2f = {"w_router": w_router, "w_gate": cc(inputs["L2_ffn_w_gate"]),
             "w_up": cc(inputs["L2_ffn_w_up"]),
             "w_down": cc(inputs["L2_ffn_w_down"])}

    def l2_pass(yg, parts):
        ims = []
        for c in range(8):
            im = {"xT": cc(yg[:, TL * c:TL * (c + 1)]), "w_router": w_router}
            if parts is None:
                im["p0"] = zpart
                im["p1"] = zpart
            else:
                pa, pb = parts
                b, hf = c // 2, c % 2
                im["p0"] = cc(pa[b][:, TL * hf:TL * hf + TL])
                im["p1"] = cc(pb[b][:, TL * hf:TL * hf + TL])
            ims.append(im)
        res = run8(p_l2c, [dict(im) for im in ims])
        counts_all = np.stack([res.results[c]["cnt"][0] for c in range(8)])
        ims2 = []
        for c in range(8):
            im = dict(ims[c])
            im.update(w_l2f)
            im["counts_all"] = cc(counts_all)
            pm = np.zeros((8, 1), np.float32)
            pm[:c] = 1.0
            im["prevmask"] = pm
            ims2.append(im)
        res2 = run8(p_l2f, ims2)
        return np.concatenate([res2.results[c]["yT"] for c in range(8)], axis=1)

    yg = l2_pass(yg, None)

    p_l2a = _get_prog("l2a", lambda: build_l2attn_launch(8))
    wq2, wk2 = np.asarray(inputs["L2_attn_wq"]), np.asarray(inputs["L2_attn_wk"])
    wv2, wo2 = np.asarray(inputs["L2_attn_wv"]), np.asarray(inputs["L2_attn_wo"])
    rf2 = np.asarray(inputs["L2_attn_rf"])

    def w_l2a(hf):
        rows = np.arange(512 * hf, 512 * hf + 512)
        return {"wqT": cc(wq2[rows, :].T), "wkT": cc(wk2[rows, :].T),
                "wvT": cc(wv2[rows, :].T), "woR": cc(wo2.T[rows, :]),
                "rfT": cc(rf2.T)}
    parts = attn_pass(p_l2a, w_l2a, yg)
    yg = l2_pass(yg, parts)

    p_l3f = _get_prog("l3f", lambda: build_l3ffn_launch(True))
    tr = lambda a: cc(np.swapaxes(np.asarray(a), -1, -2))
    w_l3f = {"gate_wT": tr(inputs["L3_ffn_gate_w"]),
             "w1T": tr(inputs["L3_ffn_w1"]), "w3T": tr(inputs["L3_ffn_w3"]),
             "w2T": tr(inputs["L3_ffn_w2"]), "sw1T": tr(inputs["L3_ffn_sw1"]),
             "sw3T": tr(inputs["L3_ffn_sw3"]), "sw2T": tr(inputs["L3_ffn_sw2"])}
    yg = ffn_pass(p_l3f, w_l3f, yg)

    p_l3a = _get_prog("l3a", lambda: build_l3attn_launch(8))
    Wq3, bq3 = np.asarray(inputs["L3_attn_Wq"]), np.asarray(inputs["L3_attn_bq"])
    Wk3, bk3 = np.asarray(inputs["L3_attn_Wk"]), np.asarray(inputs["L3_attn_bk"])
    Wv3, bv3 = np.asarray(inputs["L3_attn_Wv"]), np.asarray(inputs["L3_attn_bv"])
    Wo3, bo3 = np.asarray(inputs["L3_attn_Wo"]), np.asarray(inputs["L3_attn_bo"])
    idx3 = np.asarray(inputs["L3_attn_idx_k"], np.int32)

    def w_l3a(hf):
        sl = slice(512 * hf, 512 * hf + 512)
        return {"wqT": cc(Wq3.T[:, sl]), "bq": cc(bq3[None, sl]),
                "wkT": cc(Wk3.T[:, sl]), "bk": cc(bk3[None, sl]),
                "wvT": cc(Wv3.T[:, sl]), "bv": cc(bv3[None, sl]),
                "woR": cc(Wo3.T[sl, :]),
                "idx": cc(idx3[None, :]),
                "boC": (cc(bo3.reshape(8, 128).T.astype(np.float32)) if hf == 0
                        else np.zeros((128, 8), np.float32))}
    parts = attn_pass(p_l3a, w_l3a, yg)
    yg = ffn_pass(p_l3f, w_l3f, yg, parts)

    return np.ascontiguousarray(yg.T).reshape(B, S, Dm)


def build_ln_debug_launch():
    nc = bass.Bass("TRN2", target_bir_lowering=False)
    T = TPC
    xT = nc.dram_tensor("xT", [D, T], F32, kind="ExternalInput").ap()
    yT = nc.dram_tensor("yT", [D, T], F32, kind="ExternalOutput").ap()
    st_o = nc.dram_tensor("st_o", [2, T], F32, kind="ExternalOutput").ap()
    prog = Prog(nc)
    with ExitStack() as stack:
        ctx = Ctx(nc, prog, stack, T)
        ctx.keep_ln_stats = True
        p = prog
        ctx.use_consts(["c_ones_col", "c_ones_row"])
        xf, ready = emit_load_x(ctx, xT, None)
        x_ln = [ctx.sb(f"xln{k}", [128, T], BF16) for k in range(8)]
        ln_done, mu, rstd = emit_ln(ctx, xf, x_ln, "ln", ready)
        so = p.sem("out")
        p.ins("gpsimd", lambda e, s: e.dma_start(out=st_o[0:1, :], in_=mu[:]),
              waits=ln_done, inc=(so, 16))
        p.ins("gpsimd", lambda e, s: e.dma_start(out=st_o[1:2, :], in_=rstd[:]),
              inc=(so, 16))
        for k in range(8):
            p.ins("gpsimd", lambda e, s, k=k: e.dma_start(
                out=yT[128 * k:128 * (k + 1), :], in_=x_ln[k][:]), inc=(so, 16))
        p.barrier("gpsimd", [(so, p.count(so))])
        block = stack.enter_context(nc.Block())
        prog.run(block, stack)
    return nc, ctx


def build_l0router_debug_launch():
    nc = bass.Bass("TRN2", target_bir_lowering=False)
    T = TPC
    xT = nc.dram_tensor("xT", [D, T], F32, kind="ExternalInput").ap()
    wg_ap = nc.dram_tensor("wg", [D, 8], F32, kind="ExternalInput").ap()
    gt_o = nc.dram_tensor("gt_o", [8, T], F32, kind="ExternalOutput").ap()
    g_tok_o = nc.dram_tensor("g_tok_o", [128, (T // 128) * 8], F32, kind="ExternalOutput").ap()
    dbg_keep = nc.dram_tensor("dbg_keep", [128, (T // 128) * 8], F32, kind="ExternalOutput").ap()
    dbg_thr = nc.dram_tensor("dbg_thr", [128, T // 128], F32, kind="ExternalOutput").ap()
    dbg_ssum = nc.dram_tensor("dbg_ssum", [128, T // 128], F32, kind="ExternalOutput").ap()
    dbg_rsum = nc.dram_tensor("dbg_rsum", [128, T // 128], F32, kind="ExternalOutput").ap()
    prog = Prog(nc)
    with ExitStack() as stack:
        ctx = Ctx(nc, prog, stack, T)
        p = prog
        ntc = T // 128
        cst = ctx.use_consts(["c_ones_col", "c_ones_row", "c_ident"])
        xf, ready = emit_load_x(ctx, xT, None)
        x_ln = [ctx.sb(f"xln{k}", [128, T], BF16) for k in range(8)]
        ln_done, _, _ = emit_ln(ctx, xf, x_ln, "l0ln", ready)
        wg_bf = ctx.sb("l0_wg", [128, 8 * 8], BF16)
        wld = p.sem("l0_wgld")
        p.ins("gpsimd", lambda e, s: e.dma_start(
            out=wg_bf[:].rearrange("p (k e) -> p k e", e=8),
            in_=wg_ap.rearrange("(k p) e -> p k e", p=128)),
            waits=ln_done, inc=(wld, 16))
        ps_r = ctx.ps("l0_psr", [128, ntc * 8])
        sr = p.sem("l0_r")
        for tc in range(ntc):
            for k in range(8):
                w = ln_done + [(wld, 16)] if (tc == 0 and k == 0) else []
                p.ins("tensor", lambda e, s, tc=tc, k=k: e.matmul(
                    ps_r[:, 8 * tc:8 * (tc + 1)],
                    x_ln[k][:, 128 * tc:128 * (tc + 1)], wg_bf[:, 8 * k:8 * (k + 1)],
                    start=(tc == 0 and k == 0), stop=(tc == ntc - 1 and k == 7)),
                    waits=w, inc=(sr, 1) if (tc == ntc - 1 and k == 7) else None)
        r_exp = ctx.sb("l0_rexp", [128, ntc * 8], F32)
        r_m = ctx.sb("l0_rm", [128, ntc], F32)
        p.ins("vector", lambda e, s: e.tensor_reduce(
            r_m[:], ps_r[:].rearrange("p (g e) -> p g e", e=8), axis=AX.X, op=ALU.max,
            negate=True), waits=[(sr, p.count(sr))])
        p.ins("vector", lambda e, s: _tt_b(
            e, r_exp[:].rearrange("p (g e) -> p g e", e=8),
            ps_r[:].rearrange("p (g e) -> p g e", e=8), r_m[:, :, None],
            ALU.add), inc=(p.sem("l0_rv"), 1))
        p.ins("scalar", lambda e, s: e.activation(r_exp[:], r_exp[:], AF.Exp),
              waits=[("l0_rv", p.count("l0_rv"))], inc=(p.sem("l0_rs"), 1))
        gates, g_done = emit_topk_gates(ctx, "l0g", r_exp, ntc, 8, 2,
                                        [("l0_rs", p.count("l0_rs"))])
        GT, gt_done = emit_gate_rows(ctx, "l0gt", gates, 8, g_done)
        so = p.sem("out")
        p.ins("gpsimd", lambda e, s: e.dma_start(out=gt_o[:, :], in_=GT[:]),
              waits=gt_done, inc=(so, 16))
        p.ins("gpsimd", lambda e, s: e.dma_start(out=g_tok_o[:, :], in_=gates[:]),
              inc=(so, 16))
        for nm, ap in (("dbg_keep", dbg_keep), ("dbg_thr", dbg_thr),
                       ("dbg_ssum", dbg_ssum), ("dbg_rsum", dbg_rsum)):
            p.ins("gpsimd", lambda e, s, ap=ap: e.dma_start(
                out=ap[0:1, :], in_=ctx.cst["c_ident"][0:1, 0:ap.shape[1]]), inc=(so, 16))
        p.barrier("gpsimd", [(so, p.count(so))])
        block = stack.enter_context(nc.Block())
        prog.run(block, stack)
    return nc, ctx



def emit_glu_moe(ctx, tag, x_ln, GT, nE, experts, I, acc, waits_gates, sel_const,
                 seed_from=None):
    p, T = ctx.prog, ctx.T
    nseg = T // TSEG
    nic = I // 128
    cst = ctx.cst
    msc = ctx.scope()
    wg_buf = msc.sb(f"{tag}_wg", [128, 8 * I], BF16)
    wu_buf = msc.sb(f"{tag}_wu", [128, 8 * I], BF16)
    wd_buf = msc.sb(f"{tag}_wd", [128, nic * D], BF16)
    s_buf = [msc.sb(f"{tag}_s{i}", [128, TSEG], F32) for i in range(2)]
    act_buf = msc.sb(f"{tag}_act", [128, nic * TSEG], BF16)
    gb_sb = msc.sb(f"{tag}_gb", [128, TSEG], F32)
    g_ps = [ctx.ps(f"{tag}_gps{i}", [128, TSEG]) for i in range(2)]
    u_ps = [ctx.ps(f"{tag}_ups{i}", [128, TSEG]) for i in range(2)]
    y_ps = [ctx.ps(f"{tag}_yps{i}", [128, TSEG]) for i in range(2)]
    gb_ps = ctx.ps(f"{tag}_gbps", [128, TSEG])
    swl = p.sem(f"{tag}_wl")
    sgt = p.sem(f"{tag}_gt")
    sse = p.sem(f"{tag}_se")
    sae = p.sem(f"{tag}_ae")
    syt = p.sem(f"{tag}_yt")
    sav = p.sem(f"{tag}_av")
    sgb = p.sem(f"{tag}_gb")

    first_write = [True] * 8
    if seed_from is not None:
        for k in range(8):
            p.ins("vector", lambda e, s, k=k: e.tensor_copy(
                acc[k][:], seed_from[k][:]),
                waits=(list(waits_gates) if k == 0 else []), inc=(sav, 1))
        first_write = [False] * 8
    g_free = {}
    y_free = {}
    gb_free = 0
    act_free = 0
    for ei, ex in enumerate(experts):
        p.ins("gpsimd", lambda e, s: e.engine_nop(),
              waits=[(syt, p.count(syt))] + (list(waits_gates) if ei == 0 else []))
        p.ins("gpsimd", lambda e, s, ex=ex: e.dma_start(
            out=wg_buf[:].rearrange("p (k i) -> p k i", k=8),
            in_=ex["wg"].rearrange("(k p) i -> p k i", p=128)), inc=(swl, 16))
        p.ins("gpsimd", lambda e, s, ex=ex: e.dma_start(
            out=wu_buf[:].rearrange("p (k i) -> p k i", k=8),
            in_=ex["wu"].rearrange("(k p) i -> p k i", p=128)), inc=(swl, 16))
        p.ins("gpsimd", lambda e, s, ex=ex: e.dma_start(
            out=wd_buf[:].rearrange("p (k d) -> p k d", k=nic),
            in_=ex["wd"].rearrange("(k p) d -> p k d", p=128)), inc=(swl, 16))
        w_ready = p.count(swl)
        for g in range(nseg):
            sl = slice(g * TSEG, (g + 1) * TSEG)
            gate = ex.get("gate")
            if gate is not None:
                p.ins("tensor", lambda e, s, gate=gate, sl=sl: e.matmul(
                    gb_ps[:, :], cst[sel_const][:, 128 * gate:128 * (gate + 1)],
                    GT[0:nE, sl], start=True, stop=True),
                    waits=[(sav, gb_free)] +
                          (list(waits_gates) if (ei == 0 and g == 0) else []),
                    inc=(sgb, 1))
                p.ins("vector", lambda e, s: e.tensor_copy(gb_sb[:], gb_ps[:, :]),
                      waits=[(sgb, p.count(sgb))], inc=(sav, 1))
            for ic in range(nic):
                for k in range(8):
                    w = [(swl, w_ready)] if (g == 0 and ic == 0 and k == 0) else []
                    if k == 0 and (ic % 2) in g_free:
                        w = w + [(sse, g_free[ic % 2])]
                    p.ins("tensor", lambda e, s, ic=ic, k=k, sl=sl: e.matmul(
                        g_ps[ic % 2][:, :],
                        wg_buf[:, I * k + 128 * ic:I * k + 128 * (ic + 1)],
                        x_ln[k][:, sl], start=(k == 0), stop=(k == 7)), waits=w)
                    p.ins("tensor", lambda e, s, ic=ic, k=k, sl=sl: e.matmul(
                        u_ps[ic % 2][:, :],
                        wu_buf[:, I * k + 128 * ic:I * k + 128 * (ic + 1)],
                        x_ln[k][:, sl], start=(k == 0), stop=(k == 7)),
                        inc=(sgt, 1) if k == 7 else None)
                w = [(sgt, p.count(sgt))]
                if (ic % 2) in g_free:
                    pass
                p.ins("scalar", lambda e, s, ic=ic: e.activation(
                    s_buf[ic % 2][:], g_ps[ic % 2][:, :], AF.Silu),
                    waits=w, inc=(sse, 1))
                w2 = [(sse, p.count(sse))]
                if ic == 0:
                    w2 += [(syt, act_free)]
                    if gate is not None:
                        w2 += [(sav, p.count(sav))]
                p.ins("vector", lambda e, s, ic=ic: e.tensor_mul(
                    s_buf[ic % 2][:], s_buf[ic % 2][:], u_ps[ic % 2][:, :]),
                    waits=w2)
                if gate is not None:
                    p.ins("vector", lambda e, s, ic=ic: e.tensor_mul(
                        act_buf[:, TSEG * ic:TSEG * (ic + 1)], s_buf[ic % 2][:],
                        gb_sb[:]), inc=(sae, 1))
                else:
                    p.ins("vector", lambda e, s, ic=ic: e.tensor_copy(
                        act_buf[:, TSEG * ic:TSEG * (ic + 1)], s_buf[ic % 2][:]),
                        inc=(sae, 1))
                g_free[ic % 2] = p.count(sse)
            for dc in range(8):
                for ic in range(nic):
                    w = [(sae, p.count(sae))] if (dc == 0 and ic == 0) else []
                    if ic == 0 and (dc % 2) in y_free:
                        w = w + [(sav, y_free[dc % 2])]
                    p.ins("tensor", lambda e, s, dc=dc, ic=ic: e.matmul(
                        y_ps[dc % 2][:, :],
                        wd_buf[:, D * ic + 128 * dc:D * ic + 128 * (dc + 1)],
                        act_buf[:, TSEG * ic:TSEG * (ic + 1)],
                        start=(ic == 0), stop=(ic == nic - 1)),
                        waits=w, inc=(syt, 1) if ic == nic - 1 else None)
                if first_write[dc]:
                    p.ins("vector", lambda e, s, dc=dc, sl=sl: e.tensor_copy(
                        acc[dc][:, sl], y_ps[dc % 2][:, :]),
                        waits=[(syt, p.count(syt))], inc=(sav, 1))
                else:
                    p.ins("vector", lambda e, s, dc=dc, sl=sl: e.tensor_add(
                        acc[dc][:, sl], acc[dc][:, sl], y_ps[dc % 2][:, :]),
                        waits=[(syt, p.count(syt))], inc=(sav, 1))
                y_free[dc % 2] = p.count(sav)
            act_free = p.count(syt)
            gb_free = p.count(sav)
        first_write = [False] * 8
    msc.close()
    return [(sav, p.count(sav))]


def _emit_small_chain(p, sem):
    def step(fn, w=()):
        p.ins("vector", fn, waits=list(w) + [(sem, p.count(sem))], inc=(sem, 1))
    return step


def emit_l1ffn(ctx, xT_ap, partials, waps, yT_ap, coef=0.5):
    p, T = ctx.prog, ctx.T
    E, I, NG = 8, 512, 4
    ntc = T // 128
    cst = ctx.use_consts(["c_ones_col", "c_ones_row", "c_ident", "c_sel8@bf"])

    xf, ready = emit_load_x(ctx, xT_ap, partials)
    x_ln = [ctx.sb(f"xln{k}", [128, T], BF16) for k in range(8)]
    ln_done, _, _ = emit_ln(ctx, xf, x_ln, "l1ln", ready)

    gwT_bf = ctx.sb("l1_gw", [128, 8 * 8], BF16)
    rsc = ctx.scope()
    wld = p.sem("l1_wgld")
    p.ins("gpsimd", lambda e, s: e.dma_start(
        out=gwT_bf[:].rearrange("p (k e) -> p k e", e=8),
        in_=waps["gate_wT"].rearrange("(k p) e -> p k e", p=128)),
        waits=ln_done, inc=(wld, 16))
    ps_r = rsc.ps("l1_psr", [128, ntc * 8])
    sr = p.sem("l1_r")
    for tc in range(ntc):
        for k in range(8):
            w = ln_done + [(wld, 16)] if (tc == 0 and k == 0) else []
            p.ins("tensor", lambda e, s, tc=tc, k=k: e.matmul(
                ps_r[:, 8 * tc:8 * (tc + 1)],
                x_ln[k][:, 128 * tc:128 * (tc + 1)], gwT_bf[:, 8 * k:8 * (k + 1)],
                start=(tc == 0 and k == 0), stop=(tc == ntc - 1 and k == 7)),
                waits=w, inc=(sr, 1) if (tc == ntc - 1 and k == 7) else None)

    cb_b = ctx.sb("l1_cbb", [128, 128], F32)
    p.ins("sync", lambda e, s: e.dma_start(out=cb_b[:], in_=waps["corr_bias"]),
          waits=ln_done, inc=(p.sem("l1_cbl"), 16))
    p.barrier("vector", [("l1_cbl", 16)], sem=p.sem("l1_cbv"))

    scores = ctx.sb("l1_sc", [128, ntc * 8], F32)
    sc_c = ctx.sb("l1_scc", [128, ntc * 8], F32)
    gsum = ctx.sb("l1_gs", [128, ntc * 4], F32)
    gthr = ctx.sb("l1_gthr", [128, ntc], F32)
    gmask = ctx.sb("l1_gm", [128, ntc * 4], F32)
    sfc = ctx.sb("l1_sfc", [128, ntc * 8], F32)
    sel = ctx.sb("l1_sel", [128, ntc * 8], F32)
    eq = ctx.sb("l1_eq", [128, ntc * 8], F32)
    pre = ctx.sb("l1_pre", [128, ntc * 8], F32)
    pre2 = ctx.sb("l1_pre2", [128, ntc * 8], F32)
    thr1 = ctx.sb("l1_t1", [128, ntc], F32)
    wsum = ctx.sb("l1_ws", [128, ntc], F32)
    rws = ctx.sb("l1_rws", [128, ntc], F32)
    a38 = lambda t: t[:].rearrange("p (g e) -> p g e", e=8)
    a34 = lambda t: t[:].rearrange("p (g e) -> p g e", e=4)
    sv = p.sem("l1_rv")
    step = _emit_small_chain(p, sv)

    p.ins("scalar", lambda e, s: e.activation(scores[:], ps_r[:], AF.Sigmoid),
          waits=[(sr, p.count(sr))], inc=(p.sem("l1_rs"), 1))
    step(lambda e, s: e.tensor_add(sc_c[:], scores[:], cb_b[:]),
         [("l1_rs", 1), ("l1_cbv", 1)])
    pair = lambda t, j: t[:].rearrange("p (x two) -> p x two", two=2)[:, :, j]
    step(lambda e, s: e.tensor_tensor(gsum[:], pair(sc_c, 0), pair(sc_c, 1), ALU.add))
    step(lambda e, s: e.tensor_reduce(gthr[:], a34(gsum), axis=AX.X, op=ALU.max))
    step(lambda e, s: _tt_b(e, a34(gmask), a34(gsum), gthr[:, :, None], ALU.is_ge))
    step(lambda e, s: e.scalar_tensor_tensor(
        gsum[:], gmask[:], -1e30, gsum[:], op0=ALU.mult, op1=ALU.add))
    step(lambda e, s: e.tensor_reduce(gthr[:], a34(gsum), axis=AX.X, op=ALU.max))
    step(lambda e, s: e.tensor_tensor(gsum[:], pair(sc_c, 0), pair(sc_c, 1), ALU.add))
    step(lambda e, s: _tt_b(e, a34(gmask), a34(gsum), gthr[:, :, None], ALU.is_ge))
    step(lambda e, s: e.tensor_copy(pair(sfc, 0), gmask[:]))
    step(lambda e, s: e.tensor_copy(pair(sfc, 1), gmask[:]))
    step(lambda e, s: e.tensor_mul(sfc[:], sfc[:], sc_c[:]))
    step(lambda e, s: e.memset(sel[:], 0.0))
    for it in range(4):
        step(lambda e, s: e.tensor_reduce(thr1[:], a38(sfc), axis=AX.X, op=ALU.max))
        step(lambda e, s: _tt_b(e, a38(eq), a38(sfc), thr1[:, :, None], ALU.is_ge))
        step(lambda e, s: e.tensor_copy(pre[:], eq[:]))
        cur, oth = pre, pre2
        for sh in (1, 2, 4):
            step(lambda e, s, cur=cur, oth=oth, sh=sh: e.tensor_copy(
                a38(oth)[:, :, 0:sh], a38(cur)[:, :, 0:sh]))
            step(lambda e, s, cur=cur, oth=oth, sh=sh: e.tensor_tensor(
                a38(oth)[:, :, sh:8], a38(cur)[:, :, sh:8], a38(cur)[:, :, 0:8 - sh],
                ALU.add))
            cur, oth = oth, cur
        step(lambda e, s, cur=cur: e.tensor_scalar(
            pre[:], cur[:], 1.0, None, op0=ALU.is_le))
        step(lambda e, s: e.tensor_mul(eq[:], eq[:], pre[:]))
        step(lambda e, s: e.tensor_add(sel[:], sel[:], eq[:]))
        step(lambda e, s: e.scalar_tensor_tensor(
            sfc[:], eq[:], -1e30, sfc[:], op0=ALU.mult, op1=ALU.add))
    step(lambda e, s: e.tensor_mul(sel[:], sel[:], scores[:]))
    step(lambda e, s: e.tensor_reduce(wsum[:], a38(sel), axis=AX.X, op=ALU.add))
    step(lambda e, s: e.tensor_scalar_add(wsum[:], wsum[:], 1e-20))
    step(lambda e, s: e.reciprocal(rws[:], wsum[:]))
    step(lambda e, s: _tt_b(e, a38(sel), a38(sel), rws[:, :, None], ALU.mult))
    step(lambda e, s: e.tensor_scalar_mul(sel[:], sel[:], 2.5))
    rsc.close()
    ctx.l1_gates_tok = sel
    ctx.l1_dbg = dict(sc_c=cb_b, gsum=gsum, gmask=gmask, sfc=sfc, scores=scores)
    GT, gt_done = emit_gate_rows(ctx, "l1gt", sel, 8, [(sv, p.count(sv))])
    ctx.l1_GT = GT
    ctx.l1_gt_done = gt_done

    acc = xf
    experts = [dict(wg=waps["Wg"][e_], wu=waps["Wu"][e_], wd=waps["Wd"][e_],
                    gate=e_) for e_ in range(E)]
    experts.append(dict(wg=waps["Wg_s"], wu=waps["Wu_s"], wd=waps["Wd_s"],
                        gate=None))
    moe_done = emit_glu_moe(ctx, "l1x", x_ln, GT, 8, experts, I, acc,
                            gt_done, "c_sel8@bf")
    emit_store_residual(ctx, "l1res", acc, xT_ap, partials, coef, yT_ap, moe_done)


def emit_l3ffn(ctx, xT_ap, partials, waps, yT_ap, coef=0.5):
    p, T = ctx.prog, ctx.T
    E, I = 16, 512
    ntc = T // 128
    cst = ctx.use_consts(["c_ones_col", "c_ones_row", "c_ident", "c_sel16@bf"])

    xf, ready = emit_load_x(ctx, xT_ap, partials)
    x_ln = [ctx.sb(f"xln{k}", [128, T], BF16) for k in range(8)]
    ln_done, _, _ = emit_ln(ctx, xf, x_ln, "l3ln", ready)

    gwT_bf = ctx.sb("l3_gw", [128, 8 * 16], BF16)
    wld = p.sem("l3_wgld")
    p.ins("gpsimd", lambda e, s: e.dma_start(
        out=gwT_bf[:].rearrange("p (k e) -> p k e", e=16),
        in_=waps["gate_wT"].rearrange("(k p) e -> p k e", p=128)),
        waits=ln_done, inc=(wld, 16))
    rsc3 = ctx.scope()
    ps_r = rsc3.ps("l3_psr", [128, ntc * 16])
    sr = p.sem("l3_r")
    for tc in range(ntc):
        for k in range(8):
            w = ln_done + [(wld, 16)] if (tc == 0 and k == 0) else []
            p.ins("tensor", lambda e, s, tc=tc, k=k: e.matmul(
                ps_r[:, 16 * tc:16 * (tc + 1)],
                x_ln[k][:, 128 * tc:128 * (tc + 1)], gwT_bf[:, 16 * k:16 * (k + 1)],
                start=(tc == 0 and k == 0), stop=(tc == ntc - 1 and k == 7)),
                waits=w, inc=(sr, 1) if (tc == ntc - 1 and k == 7) else None)
    r_exp = ctx.sb("l3_rexp", [128, ntc * 16], F32)
    r_m = ctx.sb("l3_rm", [128, ntc], F32)
    p.ins("vector", lambda e, s: e.tensor_reduce(
        r_m[:], ps_r[:].rearrange("p (g e) -> p g e", e=16), axis=AX.X, op=ALU.max,
        negate=True), waits=[(sr, p.count(sr))], inc=(p.sem("l3_rv"), 1))
    p.ins("vector", lambda e, s: _tt_b(
        e, r_exp[:].rearrange("p (g e) -> p g e", e=16),
        ps_r[:].rearrange("p (g e) -> p g e", e=16), r_m[:, :, None],
        ALU.add), waits=[("l3_rv", p.count("l3_rv"))], inc=(p.sem("l3_rv"), 1))
    p.ins("scalar", lambda e, s: e.activation(r_exp[:], r_exp[:], AF.Exp),
          waits=[("l3_rv", p.count("l3_rv"))], inc=(p.sem("l3_rs"), 1))
    gates, g_done = emit_topk_gates(ctx, "l3g", r_exp, ntc, 16, 4,
                                    [("l3_rs", p.count("l3_rs"))])
    rsc3.close()
    GT, gt_done = emit_gate_rows(ctx, "l3gt", gates, 16, g_done)

    acc = xf
    experts = [dict(wg=waps["w1T"][e_], wu=waps["w3T"][e_], wd=waps["w2T"][e_],
                    gate=e_) for e_ in range(E)]
    experts.append(dict(wg=waps["sw1T"], wu=waps["sw3T"], wd=waps["sw2T"],
                        gate=None))
    moe_done = emit_glu_moe(ctx, "l3x", x_ln, GT, 16, experts, I, acc,
                            gt_done, "c_sel16@bf")
    emit_store_residual(ctx, "l3res", acc, xT_ap, partials, coef, yT_ap, moe_done)


def build_l1ffn_launch(with_partials=False):
    nc = bass.Bass("TRN2", target_bir_lowering=False)
    T = TPC
    xT = nc.dram_tensor("xT", [D, T], F32, kind="ExternalInput").ap()
    partials = None
    if with_partials:
        partials = [nc.dram_tensor(f"p{i}", [D, T], F32, kind="ExternalInput").ap()
                    for i in range(2)]
    waps = {
        "gate_wT": nc.dram_tensor("gate_wT", [D, 8], F32, kind="ExternalInput").ap(),
        "corr_bias": nc.dram_tensor("corr_bias", [128, 128], F32, kind="ExternalInput").ap(),
        "Wg": nc.dram_tensor("Wg", [8, D, 512], F32, kind="ExternalInput").ap(),
        "Wu": nc.dram_tensor("Wu", [8, D, 512], F32, kind="ExternalInput").ap(),
        "Wd": nc.dram_tensor("Wd", [8, 512, D], F32, kind="ExternalInput").ap(),
        "Wg_s": nc.dram_tensor("Wg_s", [D, 512], F32, kind="ExternalInput").ap(),
        "Wu_s": nc.dram_tensor("Wu_s", [D, 512], F32, kind="ExternalInput").ap(),
        "Wd_s": nc.dram_tensor("Wd_s", [512, D], F32, kind="ExternalInput").ap(),
    }
    yT = nc.dram_tensor("yT", [D, T], F32, kind="ExternalOutput").ap()
    prog = Prog(nc)
    with ExitStack() as stack:
        ctx = Ctx(nc, prog, stack, T)
        emit_l1ffn(ctx, xT, partials, waps, yT)
        block = stack.enter_context(nc.Block())
        prog.run(block, stack)
    return nc, ctx


def build_l3ffn_launch(with_partials=False):
    nc = bass.Bass("TRN2", target_bir_lowering=False)
    T = TPC
    xT = nc.dram_tensor("xT", [D, T], F32, kind="ExternalInput").ap()
    partials = None
    if with_partials:
        partials = [nc.dram_tensor(f"p{i}", [D, T], F32, kind="ExternalInput").ap()
                    for i in range(2)]
    waps = {
        "gate_wT": nc.dram_tensor("gate_wT", [D, 16], F32, kind="ExternalInput").ap(),
        "w1T": nc.dram_tensor("w1T", [16, D, 512], F32, kind="ExternalInput").ap(),
        "w3T": nc.dram_tensor("w3T", [16, D, 512], F32, kind="ExternalInput").ap(),
        "w2T": nc.dram_tensor("w2T", [16, 512, D], F32, kind="ExternalInput").ap(),
        "sw1T": nc.dram_tensor("sw1T", [D, 512], F32, kind="ExternalInput").ap(),
        "sw3T": nc.dram_tensor("sw3T", [D, 512], F32, kind="ExternalInput").ap(),
        "sw2T": nc.dram_tensor("sw2T", [512, D], F32, kind="ExternalInput").ap(),
    }
    yT = nc.dram_tensor("yT", [D, T], F32, kind="ExternalOutput").ap()
    prog = Prog(nc)
    with ExitStack() as stack:
        ctx = Ctx(nc, prog, stack, T)
        emit_l3ffn(ctx, xT, partials, waps, yT)
        block = stack.enter_context(nc.Block())
        prog.run(block, stack)
    return nc, ctx




def emit_l2router(ctx, x_ln, waps, ln_done, rsc, tag):
    p, T = ctx.prog, ctx.T
    E = 32
    ntc = T // 128
    wr_buf = ctx.sb(f"{tag}_wr", [128, 8 * E], BF16)
    wld = p.sem(f"{tag}_wrld")
    p.ins("gpsimd", lambda e, s: e.dma_start(
        out=wr_buf[:].rearrange("p (k e) -> p k e", e=E),
        in_=waps["w_router"].rearrange("(k p) e -> p k e", p=128)),
        waits=ln_done, inc=(wld, 16))
    ps_r = rsc.ps(f"{tag}_psr", [128, ntc * E])
    sr = p.sem(f"{tag}_r")
    for tc in range(ntc):
        for k in range(8):
            w = ln_done + [(wld, 16)] if (tc == 0 and k == 0) else []
            p.ins("tensor", lambda e, s, tc=tc, k=k: e.matmul(
                ps_r[:, E * tc:E * (tc + 1)],
                x_ln[k][:, 128 * tc:128 * (tc + 1)], wr_buf[:, E * k:E * (k + 1)],
                start=(tc == 0 and k == 0), stop=(tc == ntc - 1 and k == 7)),
                waits=w, inc=(sr, 1) if (tc == ntc - 1 and k == 7) else None)
    logit = ctx.sb(f"{tag}_lg", [128, ntc * E], F32)
    lmax = ctx.sb(f"{tag}_lm", [128, ntc], F32)
    expd = ctx.sb(f"{tag}_ex", [128, ntc * E], F32)
    ssum = ctx.sb(f"{tag}_ssm", [128, ntc], F32)
    gcol = ctx.sb(f"{tag}_gc", [128, ntc], F32)
    oh = ctx.sb(f"{tag}_oh", [128, ntc * E], F32)
    ohb = ctx.sb(f"{tag}_ohb", [128, ntc * E], BF16)
    a3 = lambda t: t[:].rearrange("p (g e) -> p g e", e=E)
    sv = p.sem(f"{tag}_rv")
    step = _emit_small_chain(p, sv)
    p.ins("vector", lambda e, s: e.tensor_copy(logit[:], ps_r[:]),
          waits=[(sr, p.count(sr)), (sv, p.count(sv))], inc=(sv, 1))
    step(lambda e, s: e.tensor_reduce(lmax[:], a3(logit), axis=AX.X, op=ALU.max))
    step(lambda e, s: _tt_b(e, a3(expd), a3(logit), lmax[:, :, None], ALU.subtract))
    p.ins("scalar", lambda e, s: e.activation(expd[:], expd[:], AF.Exp),
          waits=[(sv, p.count(sv))], inc=(p.sem(f"{tag}_rs"), 1))
    step(lambda e, s: e.tensor_reduce(ssum[:], a3(expd), axis=AX.X, op=ALU.add),
         [(f"{tag}_rs", 1)])
    step(lambda e, s: e.reciprocal(gcol[:], ssum[:]))
    step(lambda e, s: _tt_b(e, a3(oh), a3(logit), lmax[:, :, None], ALU.is_ge))
    step(lambda e, s: e.tensor_copy(ohb[:], oh[:]))
    return dict(oh=oh, ohb=ohb, gcol=gcol, sv=sv, waits=[(sv, p.count(sv))])


def emit_l2cnt(ctx, xT_ap, partials, waps, cnt_ap):
    p, T = ctx.prog, ctx.T
    E = 32
    ntc = T // 128
    cst = ctx.use_consts(["c_ones_col", "c_ones_row"])
    xf, ready = emit_load_x(ctx, xT_ap, partials)
    x_ln = [ctx.sb(f"xln{k}", [128, T], BF16) for k in range(8)]
    ln_done, _, _ = emit_ln(ctx, xf, x_ln, "l2cln", ready)
    rsc = ctx.scope()
    r = emit_l2router(ctx, x_ln, waps, ln_done, rsc, "l2c")
    cnt_ps = rsc.ps("l2c_cps", [1, E])
    st = p.sem("l2c_t")
    for tc in range(ntc):
        w = list(r["waits"]) if tc == 0 else []
        p.ins("tensor", lambda e, s, tc=tc: e.matmul(
            cnt_ps[:, :], cst["c_ones_col"][:, :], r["oh"][:, E * tc:E * (tc + 1)],
            start=(tc == 0), stop=(tc == ntc - 1)),
            waits=w, inc=(st, 1) if tc == ntc - 1 else None)
    cnt_sb = ctx.sb("l2c_sb", [1, E], F32)
    p.ins("vector", lambda e, s: e.tensor_copy(cnt_sb[:], cnt_ps[:, :]),
          waits=[(st, 1)], inc=(p.sem("l2c_v"), 1))
    so = p.sem("l2c_o")
    p.ins("gpsimd", lambda e, s: e.dma_start(out=cnt_ap[:, :], in_=cnt_sb[:]),
          waits=[("l2c_v", 1)], inc=(so, 16))
    p.barrier("gpsimd", [(so, 16)])
    rsc.close()


def emit_l2ffn(ctx, xT_ap, partials, waps, yT_ap, coef=0.5):
    p, T = ctx.prog, ctx.T
    E, I = 32, 1024
    CAP = 640.0
    ntc = T // 128
    cst = ctx.use_consts(["c_ones_col", "c_ones_row", "c_ident",
                          "c_sel32@bf", "c_lt128@bf"])
    xf, ready = emit_load_x(ctx, xT_ap, partials)
    x_ln = [ctx.sb(f"xln{k}", [128, T], BF16) for k in range(8)]
    ln_done, _, _ = emit_ln(ctx, xf, x_ln, "l2ln", ready)

    run = ctx.sb("l2_run", [1, E], F32)
    possb = ctx.sb("l2_pos", [128, E], F32)
    accep = ctx.sb("l2_acp", [128, E], F32)
    gtok = ctx.sb("l2_gtk", [128, ntc * E], F32)
    pm_b = ctx.sb("l2_pm", [8, 1], F32)
    ca_b = ctx.sb("l2_ca", [8, E], F32)

    rsc = ctx.scope()
    r = emit_l2router(ctx, x_ln, waps, ln_done, rsc, "l2m")
    ld2 = p.sem("l2_bld")
    p.ins("sync", lambda e, s: e.dma_start(out=pm_b[:], in_=waps["prevmask"]),
          waits=ln_done, inc=(ld2, 16))
    p.ins("sync", lambda e, s: e.dma_start(out=ca_b[:], in_=waps["counts_all"]),
          waits=ln_done, inc=(ld2, 16))
    base_ps = rsc.ps("l2_bps", [1, E])
    pos_ps = rsc.ps("l2_pps", [128, E])
    big_ps = rsc.ps("l2_gps", [128, E])
    tot_ps = rsc.ps("l2_tps", [1, E])
    st = p.sem("l2_at")
    sv2 = p.sem("l2_av")
    step = _emit_small_chain(p, sv2)
    p.ins("tensor", lambda e, s: e.matmul(
        base_ps[:, :], pm_b[:, :], ca_b[:, :], start=True, stop=True),
        waits=[(ld2, 32)], inc=(st, 1))
    p.ins("vector", lambda e, s: e.tensor_copy(run[:], base_ps[:, :]),
          waits=[(st, p.count(st)), (sv2, p.count(sv2))], inc=(sv2, 1))
    for tc in range(ntc):
        w = (list(r["waits"]) if tc == 0 else []) + [(sv2, p.count(sv2))]
        p.ins("tensor", lambda e, s, tc=tc: e.matmul(
            pos_ps[:, :], cst["c_lt128@bf"][:, :], r["ohb"][:, E * tc:E * (tc + 1)],
            start=True, stop=True), waits=w, inc=(st, 1))
        p.ins("tensor", lambda e, s: e.matmul(
            big_ps[:, :], cst["c_ones_row"][:, :], run[:, :],
            start=True, stop=True), inc=(st, 1))
        p.ins("tensor", lambda e, s, tc=tc: e.matmul(
            tot_ps[:, :], cst["c_ones_col"][:, :], r["oh"][:, E * tc:E * (tc + 1)],
            start=True, stop=True), inc=(st, 1))
        stc = p.count(st)
        step(lambda e, s: e.tensor_copy(possb[:], pos_ps[:, :]), [(st, stc)])
        step(lambda e, s: e.tensor_add(possb[:], possb[:], big_ps[:, :]))
        step(lambda e, s: e.tensor_scalar(
            accep[:], possb[:], CAP - 0.5, None, op0=ALU.is_le))
        step(lambda e, s, tc=tc: e.tensor_mul(
            gtok[:, E * tc:E * (tc + 1)], r["oh"][:, E * tc:E * (tc + 1)],
            accep[:]))
        step(lambda e, s, tc=tc: e.tensor_scalar(
            gtok[:, E * tc:E * (tc + 1)], gtok[:, E * tc:E * (tc + 1)],
            r["gcol"][:, tc:tc + 1], None, op0=ALU.mult))
        step(lambda e, s: e.tensor_add(run[:], run[:], tot_ps[:, :]))
    rsc.close()
    GT, gt_done = emit_gate_rows(ctx, "l2gt", gtok, E, [(sv2, p.count(sv2))])

    acc = xf
    experts = [dict(wg=waps["w_gate"][e_], wu=waps["w_up"][e_],
                    wd=waps["w_down"][e_], gate=e_) for e_ in range(E)]
    moe_done = emit_glu_moe(ctx, "l2x", x_ln, GT, E, experts, I, acc,
                            gt_done, "c_sel32@bf", seed_from=x_ln)
    emit_store_residual(ctx, "l2res", acc, xT_ap, partials, coef, yT_ap, moe_done)


def build_l2cnt_launch(with_partials=False):
    nc = bass.Bass("TRN2", target_bir_lowering=False)
    T = TPC
    xT = nc.dram_tensor("xT", [D, T], F32, kind="ExternalInput").ap()
    partials = None
    if with_partials:
        partials = [nc.dram_tensor(f"p{i}", [D, T], F32, kind="ExternalInput").ap()
                    for i in range(2)]
    waps = {
        "w_router": nc.dram_tensor("w_router", [D, 32], F32,
                                   kind="ExternalInput").ap(),
    }
    cnt = nc.dram_tensor("cnt", [1, 32], F32, kind="ExternalOutput").ap()
    prog = Prog(nc)
    with ExitStack() as stack:
        ctx = Ctx(nc, prog, stack, T)
        emit_l2cnt(ctx, xT, partials, waps, cnt)
        block = stack.enter_context(nc.Block())
        prog.run(block, stack)
    return nc, ctx


def build_l2ffn_launch(with_partials=False):
    nc = bass.Bass("TRN2", target_bir_lowering=False)
    T = TPC
    xT = nc.dram_tensor("xT", [D, T], F32, kind="ExternalInput").ap()
    partials = None
    if with_partials:
        partials = [nc.dram_tensor(f"p{i}", [D, T], F32, kind="ExternalInput").ap()
                    for i in range(2)]
    waps = {
        "w_router": nc.dram_tensor("w_router", [D, 32], F32,
                                   kind="ExternalInput").ap(),
        "counts_all": nc.dram_tensor("counts_all", [8, 32], F32,
                                     kind="ExternalInput").ap(),
        "prevmask": nc.dram_tensor("prevmask", [8, 1], F32,
                                   kind="ExternalInput").ap(),
        "w_gate": nc.dram_tensor("w_gate", [32, D, D], F32,
                                 kind="ExternalInput").ap(),
        "w_up": nc.dram_tensor("w_up", [32, D, D], F32,
                               kind="ExternalInput").ap(),
        "w_down": nc.dram_tensor("w_down", [32, D, D], F32,
                                 kind="ExternalInput").ap(),
    }
    yT = nc.dram_tensor("yT", [D, T], F32, kind="ExternalOutput").ap()
    prog = Prog(nc)
    with ExitStack() as stack:
        ctx = Ctx(nc, prog, stack, T)
        emit_l2ffn(ctx, xT, partials, waps, yT)
        block = stack.enter_context(nc.Block())
        prog.run(block, stack)
    return nc, ctx





def emit_ln_stream(ctx, xT_ap, x_ln, tag, ready):
    p, T = ctx.prog, ctx.T
    cst = ctx.cst
    nseg = T // TSEG
    sc = ctx.scope()
    xs = [[sc.sb(f"{tag}_x{b}_{k}", [128, TSEG], F32) for k in range(8)]
          for b in range(2)]
    sq = [sc.sb(f"{tag}_sq{b}", [128, TSEG], F32) for b in range(2)]
    r_mu = sc.sb(f"{tag}_mu", [1, TSEG], F32)
    r_q = sc.sb(f"{tag}_q", [1, TSEG], F32)
    r_rs = sc.sb(f"{tag}_rs", [1, TSEG], F32)
    ps_s = sc.ps(f"{tag}_pss", [1, TSEG])
    ps_q = sc.ps(f"{tag}_psq", [1, TSEG])
    ps_b = sc.ps(f"{tag}_psb", [128, TSEG])
    ps_b2 = sc.ps(f"{tag}_psb2", [128, TSEG])
    sl_ = p.sem(f"{tag}_l")
    sv = p.sem(f"{tag}_v")
    ss = p.sem(f"{tag}_s")
    st = p.sem(f"{tag}_t")
    x_free = {}
    for g in range(nseg):
        b = g % 2
        sl = slice(g * TSEG, (g + 1) * TSEG)
        for k in range(8):
            w = [(sv, x_free.get(b, 0))] if k == 0 else []
            if g == 0 and k == 0:
                w = w + list(ready)
            p.ins("sync", lambda e, s, k=k, sl=sl, b=b: e.dma_start(
                out=xs[b][k][:], in_=xT_ap[128 * k:128 * (k + 1), sl]),
                waits=w, inc=(sl_, 16))
        ld_tgt = p.count(sl_)
        for k in range(8):
            w = [(sl_, ld_tgt)] if k == 0 else []
            wsq = [(sl_, ld_tgt)] if k == 0 else [(st, p.count(st))]
            p.ins("scalar", lambda e, s, k=k, b=b: e.activation(
                sq[b][:, :], xs[b][k][:, :], AF.Square), waits=wsq, inc=(ss, 1))
            p.ins("tensor", lambda e, s, k=k, b=b: e.matmul(
                ps_s[:, :], cst["c_ones_col"][:, :], xs[b][k][:, :],
                start=(k == 0), stop=(k == 7)), waits=w)
            p.ins("tensor", lambda e, s, k=k, b=b: e.matmul(
                ps_q[:, :], cst["c_ones_col"][:, :], sq[b][:, :],
                start=(k == 0), stop=(k == 7)),
                waits=[(ss, p.count(ss))], inc=(st, 1))
        p.ins("vector", lambda e, s: e.tensor_scalar_mul(
            r_mu[:], ps_s[:, :], 1.0 / D), waits=[(st, p.count(st))])
        p.ins("vector", lambda e, s: e.tensor_scalar_mul(
            r_q[:], ps_q[:, :], 1.0 / D))
        p.ins("vector", lambda e, s: e.tensor_mul(r_rs[:], r_mu[:], r_mu[:]))
        p.ins("vector", lambda e, s: e.tensor_sub(r_q[:], r_q[:], r_rs[:]))
        p.ins("vector", lambda e, s: e.tensor_scalar_add(r_q[:], r_q[:], 1e-5),
              inc=(sv, 1))
        p.ins("scalar", lambda e, s: e.activation(r_q[:], r_q[:], AF.Sqrt),
              waits=[(sv, p.count(sv))], inc=(ss, 1))
        p.ins("vector", lambda e, s: e.reciprocal(r_rs[:], r_q[:]),
              waits=[(ss, p.count(ss))], inc=(sv, 1))
        p.ins("tensor", lambda e, s: e.matmul(
            ps_b[:, :], cst["c_ones_row"][:, :], r_mu[:, :], start=True, stop=True),
            waits=[(sv, p.count(sv))])
        p.ins("tensor", lambda e, s: e.matmul(
            ps_b2[:, :], cst["c_ones_row"][:, :], r_rs[:, :], start=True, stop=True),
            inc=(st, 1))
        for k in range(8):
            w = [(st, p.count(st))] if k == 0 else []
            p.ins("vector", lambda e, s, k=k, b=b: e.tensor_sub(
                sq[b][:, :], xs[b][k][:, :], ps_b[:, :]), waits=w)
            p.ins("vector", lambda e, s, k=k, b=b, sl=sl: e.tensor_mul(
                x_ln[k][:, sl], sq[b][:, :], ps_b2[:, :]),
                inc=(sv, 1) if k == 7 else None)
        x_free[b] = p.count(sv)
    sc.close()
    return [(sv, p.count(sv))]


def emit_l1attn(ctx, xT_ap, waps, yT_ap, NH=8):
    nc, p, T = ctx.nc, ctx.prog, ctx.T
    assert T == ATT_T
    nseg = T // TSEG
    ntc = T // 128
    cst = ctx.use_consts(["c_ones_col", "c_ones_row", "c_ident"])

    x_ln = [ctx.sb(f"xln{k}", [128, T], BF16) for k in range(8)]
    ready = [("c_ld", p.count("c_ld"))]
    ln_done = emit_ln_stream(ctx, xT_ap, x_ln, "a1ln", ready)

    ctx.load_const_after("c_rope_sin@bf", ln_done)
    ctx.load_const_after("c_rope_cos@bf", ln_done)
    cst = ctx.cst
    DH = 64 * NH
    wq_b = ctx.sb("a1_wq", [128, 8 * DH], BF16)
    wk_b = ctx.sb("a1_wk", [128, 8 * DH], BF16)
    wv_b = ctx.sb("a1_wv", [128, 8 * DH], BF16)
    bq_b = ctx.sb("a1_bq", [1, DH], BF16)
    bk_b = ctx.sb("a1_bk", [1, DH], BF16)
    bv_b = ctx.sb("a1_bv", [1, DH], BF16)
    ident_bf = ctx.sb("a1_idbf", [128, 128], BF16)
    ones_seg = ctx.sb("a1_ones", [1, TSEG], BF16)
    wl = p.sem("a1_wl")
    for nm, buf in (("wqT", wq_b), ("wkT", wk_b), ("wvT", wv_b),
                    ("bq", bq_b), ("bk", bk_b), ("bv", bv_b)):
        if nm.startswith("w"):
            p.ins("gpsimd", lambda e, s, nm=nm, buf=buf: e.dma_start(
                out=buf[:].rearrange("p (k h) -> p k h", k=8),
                in_=waps[nm].rearrange("(k p) h -> p k h", p=128)),
                waits=ln_done, inc=(wl, 16))
        else:
            p.ins("gpsimd", lambda e, s, nm=nm, buf=buf: e.dma_start(
                out=buf[:], in_=waps[nm]), waits=ln_done, inc=(wl, 16))
    sv = p.sem("a1_v")
    p.ins("vector", lambda e, s: e.memset(ones_seg[:], 1.0),
          waits=ln_done, inc=(sv, 1))
    p.ins("vector", lambda e, s: e.tensor_copy(ident_bf[:], cst["c_ident"][:, :]),
          inc=(sv, 1))

    outb = ctx.sb("a1_outb", [128, TSEG], F32)
    outc = ctx.sb("a1_outc", [128, TSEG], F32)
    ctxT = [ctx.sb(f"a1_ctx{i}", [128, T], BF16) for i in range(NH // 2)]
    hd_sc = ctx.scope()
    Qa = hd_sc.sb("a1_Qa", [65, T], BF16)
    Ka = hd_sc.sb("a1_Ka", [65, T], BF16)
    VaT = [hd_sc.sb(f"a1_VaT{i}", [64, TSEG], BF16) for i in range(2)]
    Va = hd_sc.sb("a1_Va", [128, ntc * 72], BF16)
    q2r = hd_sc.sb("a1_q2", [1, T], BF16)
    k2r = hd_sc.sb("a1_k2", [1, T], BF16)
    sqs = [hd_sc.sb(f"a1_sqs{i}", [64, TSEG], F32) for i in range(2)]
    rsc4 = [hd_sc.sb(f"a1_r4_{i}", [32, TSEG], F32) for i in range(4)]
    kmax = hd_sc.sb("a1_km", [1, 1], F32)
    P_b = [hd_sc.sb(f"a1_P{i}", [128, TSEG], BF16) for i in range(2)]
    drow = hd_sc.sb("a1_dr", [1, TSEG], F32)
    drec = hd_sc.sb("a1_drec", [1, TSEG], F32)

    ps_sc = ctx.scope()
    pr_ps = [ps_sc.ps(f"a1_pr{i}", [64, TSEG]) for i in range(2)]
    S_ps = [ps_sc.ps(f"a1_S{i}", [128, TSEG]) for i in range(2)]
    acc_ps = ps_sc.ps("a1_acc", [65, TSEG])
    bc_ps = ps_sc.ps("a1_bc", [64, TSEG])
    row_ps = ps_sc.ps("a1_row", [1, TSEG])
    vt_ps = ps_sc.ps("a1_vt", [128, 64], BF16)

    st = p.sem("a1_t")
    ss = p.sem("a1_s")
    sg = p.sem("a1_g")

    W_READY = p.count(wl)
    pr_free = {}
    S_free = {}
    P_free = {}
    sq_free = {}

    def proj_qk(h, wbuf, bbuf, dst):
        for g in range(nseg):
            sl = slice(g * TSEG, (g + 1) * TSEG)
            b = g % 2
            for k in range(8):
                w = []
                if k == 0:
                    w = [(sv, pr_free.get(b, 0))]
                    if h == 0 and g == 0:
                        w += [(wl, W_READY), (sv, 2)]
                p.ins("tensor", lambda e, s, k=k, sl=sl, b=b, h=h, wbuf=wbuf: e.matmul(
                    pr_ps[b][:, :], wbuf[:, DH * k + 64 * h:DH * k + 64 * (h + 1)],
                    x_ln[k][:, sl], start=(k == 0), stop=False), waits=w)
            p.ins("tensor", lambda e, s, h=h, b=b, bbuf=bbuf: e.matmul(
                pr_ps[b][:, :], bbuf[0:1, 64 * h:64 * (h + 1)], ones_seg[0:1, :],
                start=False, stop=True), inc=(st, 1))
            w = [(st, p.count(st)), (st, sq_free.get(0, 0))]
            p.ins("vector", lambda e, s, sl=sl, b=b: e.tensor_mul(
                rsc4[0][:, :], cst["c_rope_sin@bf"][:, sl], pr_ps[b][32:64, :]),
                waits=w)
            p.ins("vector", lambda e, s, sl=sl, b=b: e.tensor_mul(
                rsc4[1][:, :], cst["c_rope_sin@bf"][:, sl], pr_ps[b][0:32, :]))
            p.ins("vector", lambda e, s, sl=sl, b=b: e.tensor_mul(
                rsc4[2][:, :], cst["c_rope_cos@bf"][:, sl], pr_ps[b][0:32, :]))
            p.ins("vector", lambda e, s, sl=sl, b=b: e.tensor_mul(
                rsc4[3][:, :], cst["c_rope_cos@bf"][:, sl], pr_ps[b][32:64, :]))
            p.ins("vector", lambda e, s, sl=sl, dst=dst: e.tensor_sub(
                dst[0:32, sl], rsc4[2][:, :], rsc4[0][:, :]))
            p.ins("vector", lambda e, s, sl=sl, dst=dst: e.tensor_add(
                dst[32:64, sl], rsc4[1][:, :], rsc4[3][:, :]), inc=(sv, 1))
            pr_free[b] = p.count(sv)

    def proj_v(h):
        for g in range(nseg):
            sl = slice(g * TSEG, (g + 1) * TSEG)
            b = g % 2
            for k in range(8):
                w = [(sv, pr_free.get(b, 0))] if k == 0 else []
                p.ins("tensor", lambda e, s, k=k, sl=sl, b=b, h=h: e.matmul(
                    pr_ps[b][:, :], wv_b[:, DH * k + 64 * h:DH * k + 64 * (h + 1)],
                    x_ln[k][:, sl], start=(k == 0), stop=False), waits=w)
            p.ins("tensor", lambda e, s, h=h, b=b: e.matmul(
                pr_ps[b][:, :], bv_b[0:1, 64 * h:64 * (h + 1)], ones_seg[0:1, :],
                start=False, stop=True), inc=(st, 1))
            p.ins("vector", lambda e, s, b=b: e.tensor_copy(
                VaT[b][:, :], pr_ps[b][:, :]), waits=[(st, p.count(st))], inc=(sv, 1))
            pr_free[b] = p.count(sv)
            vdone = p.count(sv)
            for j in range(4):
                tc = 4 * g + j
                w = [(sv, vdone)] if j == 0 else [(sv, p.count(sv))]
                p.ins("tensor", lambda e, s, j=j, b=b: e.transpose(
                    vt_ps[:, :], VaT[b][:, 128 * j:128 * (j + 1)],
                    ident_bf[0:64, 0:64]), waits=w, inc=(st, 1))
                p.ins("vector", lambda e, s, tc=tc: e.tensor_copy(
                    Va[:].rearrange("p (c w) -> p c w", w=72)[:, tc, 0:64], vt_ps[:, :]),
                    waits=[(st, p.count(st))], inc=(sv, 1))
        p.ins("vector", lambda e, s: e.memset(
            Va[:].rearrange("p (c w) -> p c w", w=72)[:, :, 64:65], 1.0), inc=(sv, 1))

    def norms(h):
        rows_done = p.count(sv)
        for which, src_t, dst_r in (("q", Qa, q2r), ("k", Ka, k2r)):
            for g in range(nseg):
                sl = slice(g * TSEG, (g + 1) * TSEG)
                b = g % 2
                w = [(sv, rows_done)] if (which == "q" and g == 0) else []
                if (b in sq_free):
                    w = w + [(st, sq_free[b])]
                p.ins("scalar", lambda e, s, sl=sl, b=b, src_t=src_t: e.activation(
                    sqs[b][:, :], src_t[0:64, sl], AF.Square), waits=w, inc=(ss, 1))
                p.ins("tensor", lambda e, s, b=b: e.matmul(
                    row_ps[:, :], cst["c_ones_col"][0:64, :], sqs[b][:, :],
                    start=True, stop=True),
                    waits=[(ss, p.count(ss)), (sv, p.count(sv))], inc=(st, 1))
                sq_free[b] = p.count(st)
                p.ins("vector", lambda e, s, sl=sl, dst_r=dst_r: e.tensor_copy(
                    dst_r[:, sl], row_ps[:, :]), waits=[(st, p.count(st))],
                    inc=(sv, 1))
        p.ins("scalar", lambda e, s: e.activation(q2r[:], q2r[:], AF.Sqrt),
              waits=[(sv, p.count(sv))], inc=(ss, 1))
        p.ins("scalar", lambda e, s: e.activation(k2r[:], k2r[:], AF.Sqrt),
              inc=(ss, 1))
        p.ins("vector", lambda e, s: e.reduce_max(kmax[:], k2r[:], axis=AX.X),
              waits=[(ss, p.count(ss))], inc=(sv, 1))
        p.ins("vector", lambda e, s: e.tensor_scalar(
            Qa[64:65, :], q2r[:], kmax[0:1, 0:1], -1.0, op0=ALU.mult, op1=ALU.mult),
            waits=[(sv, p.count(sv))], inc=(sv, 1))
        p.ins("vector", lambda e, s: e.memset(Ka[64:65, :], 1.0), inc=(sv, 1))

    def attend(h):
        prep_done = p.count(sv)
        for qi in range(nseg):
            qsl = slice(qi * TSEG, (qi + 1) * TSEG)
            nk = (qi + 1) * (TSEG // 128)
            for ki in range(nk):
                b = ki % 2
                ksl = slice(ki * 128, (ki + 1) * 128)
                w = [(ss, S_free.get(b, 0))]
                if qi == 0 and ki == 0:
                    w += [(sv, prep_done)]
                p.ins("tensor", lambda e, s, ksl=ksl, qsl=qsl, b=b: e.matmul(
                    S_ps[b][:, :], Ka[0:65, ksl], Qa[0:65, qsl],
                    start=True, stop=True), waits=w, inc=(st, 1))
                w2 = [(st, p.count(st)), (st, P_free.get(b, 0))]
                p.ins("scalar", lambda e, s, b=b: e.activation(
                    P_b[b][:], S_ps[b][:, :], AF.Exp, scale=0.125),
                    waits=w2, inc=(ss, 1))
                S_free[b] = p.count(ss)
                diag = (ki * 128 + 127 >= qi * TSEG)
                if diag:
                    p.ins("gpsimd", lambda e, s, b=b, ki=ki, qi=qi: e.affine_select(
                        P_b[b][:], P_b[b][:], [[1, TSEG]], ALU.is_ge, 0.0,
                        base=qi * TSEG - ki * 128, channel_multiplier=-1),
                        waits=[(ss, p.count(ss))], inc=(sg, 1))
                    avw = [(sg, p.count(sg))]
                else:
                    avw = [(ss, p.count(ss))]
                p.ins("tensor", lambda e, s, ki=ki, b=b, nk=nk: e.matmul(
                    acc_ps[:, :],
                    Va[:].rearrange("p (c w) -> p c w", w=72)[:, ki, 0:65],
                    P_b[b][:], start=(ki == 0), stop=(ki == nk - 1)),
                    waits=avw, inc=(st, 1) if ki == nk - 1 else None)
                P_free[b] = p.count(st)
            p.ins("vector", lambda e, s: e.tensor_copy(drow[:], acc_ps[64:65, :]),
                  waits=[(st, p.count(st))], inc=(sv, 1))
            p.ins("vector", lambda e, s: e.reciprocal(drec[:], drow[:]),
                  waits=[(sv, p.count(sv))], inc=(sv, 1))
            p.ins("tensor", lambda e, s: e.matmul(
                bc_ps[:, :], cst["c_ones_row"][0:1, 0:64], drec[:, :],
                start=True, stop=True), waits=[(sv, p.count(sv))], inc=(st, 1))
            p.ins("scalar", lambda e, s: e.activation(
                outb[0:64, :], acc_ps[0:64, :], AF.Copy),
                waits=[(st, p.count(st))], inc=(ss, 1))
            p.ins("vector", lambda e, s, h=h, qsl=qsl: e.tensor_mul(
                ctxT[h // 2][64 * (h % 2):64 * (h % 2) + 64, qsl],
                outb[0:64, :], bc_ps[:, :]),
                waits=[(ss, p.count(ss))], inc=(sv, 1))

    for h in range(NH):
        proj_qk(h, wq_b, bq_b, Qa)
        proj_qk(h, wk_b, bk_b, Ka)
        proj_v(h)
        norms(h)
        attend(h)

    ps_sc.close()
    hd_sc.close()
    attn_all_done = p.count(sv)
    wo_b = ctx.sb("a1_wo", [128, (DH // 128) * D], BF16)
    boC = ctx.sb("a1_boC", [128, 8], F32)
    p.ins("gpsimd", lambda e, s: e.dma_start(
        out=wo_b[:].rearrange("p (k d) -> p k d", k=DH // 128),
        in_=waps["woR"].rearrange("(k p) d -> p k d", p=128)),
        waits=[("a1_v", attn_all_done)], inc=(wl, 16))
    p.ins("sync", lambda e, s: e.dma_start(out=boC[:], in_=waps["boC"]),
          waits=[("a1_v", attn_all_done)], inc=(wl, 16))
    wo_rdy = p.count(wl)
    y_ps = [ctx.ps(f"a1_y{i}", [128, TSEG]) for i in range(2)]
    so = p.sem("a1_o")
    ctx_done = p.count(sv)
    ob = [outb, outc]
    y_free2 = {}
    o_free = {}
    for g in range(nseg):
        sl = slice(g * TSEG, (g + 1) * TSEG)
        for dc in range(8):
            i = g * 8 + dc
            bk_ = i % 2
            for kc in range(DH // 128):
                w = []
                if kc == 0:
                    w = [(sv, y_free2.get(bk_, 0))]
                    if i == 0:
                        w += [(wl, wo_rdy), (sv, ctx_done)]
                p.ins("tensor", lambda e, s, kc=kc, dc=dc, sl=sl, bk_=bk_: e.matmul(
                    y_ps[bk_][:, :], wo_b[:, D * kc + 128 * dc:D * kc + 128 * (dc + 1)],
                    ctxT[kc][:, sl], start=(kc == 0), stop=(kc == DH // 128 - 1)),
                    waits=w, inc=(st, 1) if kc == DH // 128 - 1 else None)
            p.ins("vector", lambda e, s, bk_=bk_, dc=dc: e.tensor_scalar(
                ob[bk_][:], y_ps[bk_][:, :], boC[:, dc:dc + 1], None, op0=ALU.add),
                waits=[(st, p.count(st)), (so, o_free.get(bk_, 0))], inc=(sv, 1))
            y_free2[bk_] = p.count(sv)
            p.ins("gpsimd", lambda e, s, dc=dc, sl=sl, bk_=bk_: e.dma_start(
                out=yT_ap[128 * dc:128 * (dc + 1), sl], in_=ob[bk_][:]),
                waits=[(sv, p.count(sv))], inc=(so, 16))
            o_free[bk_] = p.count(so)
    p.barrier("gpsimd", [(so, p.count(so))])


def build_l1attn_launch(NH=8):
    nc = bass.Bass("TRN2", target_bir_lowering=False)
    T = ATT_T
    DH = 64 * NH
    xT = nc.dram_tensor("xT", [D, T], F32, kind="ExternalInput").ap()
    waps = {
        "wqT": nc.dram_tensor("wqT", [D, DH], F32, kind="ExternalInput").ap(),
        "bq": nc.dram_tensor("bq", [1, DH], F32, kind="ExternalInput").ap(),
        "wkT": nc.dram_tensor("wkT", [D, DH], F32, kind="ExternalInput").ap(),
        "bk": nc.dram_tensor("bk", [1, DH], F32, kind="ExternalInput").ap(),
        "wvT": nc.dram_tensor("wvT", [D, DH], F32, kind="ExternalInput").ap(),
        "bv": nc.dram_tensor("bv", [1, DH], F32, kind="ExternalInput").ap(),
        "woR": nc.dram_tensor("woR", [DH, D], F32, kind="ExternalInput").ap(),
        "boC": nc.dram_tensor("boC", [128, 8], F32, kind="ExternalInput").ap(),
    }
    yT = nc.dram_tensor("yT", [D, T], F32, kind="ExternalOutput").ap()
    prog = Prog(nc)
    with ExitStack() as stack:
        ctx = Ctx(nc, prog, stack, T)
        emit_l1attn(ctx, xT, waps, yT, NH)
        block = stack.enter_context(nc.Block())
        prog.run(block, stack)
    return nc, ctx



def emit_l0attn(ctx, xT_ap, waps, yT_ap, NH=8):
    nc, p, T = ctx.nc, ctx.prog, ctx.T
    assert T == ATT_T
    nseg = T // TSEG
    ntc = T // 128
    cst = ctx.use_consts(["c_ones_col", "c_ones_row", "c_ident", "c_iota_T",
                          "c_causal_big@bf"])

    x_ln = [ctx.sb(f"xln{k}", [128, T], BF16) for k in range(8)]
    ready = [("c_ld", p.count("c_ld"))]
    ln_done = emit_ln_stream(ctx, xT_ap, x_ln, "a0ln", ready)

    DH = 64 * NH
    wq_b = ctx.sb("a0_wq", [128, 4 * DH], BF16)
    wk_b = ctx.sb("a0_wk", [128, 4 * 128], BF16)
    wv_b = ctx.sb("a0_wv", [128, 4 * 128], BF16)
    ph_row = ctx.sb("a0_ph", [1, NH], F32)
    ph2_row = ctx.sb("a0_ph2", [1, NH], F32)
    cs_row = ctx.sb("a0_cs", [1, 2 * NH], F32)
    lsl_row = ctx.sb("a0_lsl", [1, 8 * NH], F32)
    sl_row = ctx.sb("a0_sl", [1, 8 * NH], F32)
    wl = p.sem("a0_wl")
    for nm, buf, kk in (("wqT", wq_b, 4), ("wkT", wk_b, 4), ("wvT", wv_b, 4)):
        p.ins("gpsimd", lambda e, s, nm=nm, buf=buf, kk=kk: e.dma_start(
            out=buf[:].rearrange("p (k h) -> p k h", k=kk),
            in_=waps[nm].rearrange("(k p) h -> p k h", p=128)),
            waits=ln_done, inc=(wl, 16))
    p.ins("sync", lambda e, s: e.dma_start(out=ph_row[:], in_=waps["phase"]),
          waits=ln_done, inc=(wl, 16))
    p.ins("sync", lambda e, s: e.dma_start(out=lsl_row[:], in_=waps["slopes"]),
          inc=(wl, 16))
    ss = p.sem("a0_s")
    sv = p.sem("a0_v")
    p.ins("scalar", lambda e, s: e.activation(
        cs_row[0:1, 0:NH], ph_row[:], AF.Sin, bias=PI / 2.0 if False else 0.0),
        waits=[(wl, p.count(wl))], inc=(ss, 1))
    p.ins("vector", lambda e, s: e.tensor_scalar_add(
        ph2_row[:], ph_row[:], PI / 2.0),
        waits=[(wl, p.count(wl))], inc=(sv, 1))
    p.ins("scalar", lambda e, s: e.activation(
        cs_row[0:1, NH:2 * NH], ph2_row[:], AF.Sin),
        waits=[(sv, p.count(sv))], inc=(ss, 1))
    p.ins("scalar", lambda e, s: e.activation(sl_row[:], lsl_row[:], AF.Exp),
          inc=(ss, 1))
    trig_done = [(ss, p.count(ss))]

    outb = ctx.sb("a0_outb", [128, TSEG], F32)
    outc = outb
    ctxT = [ctx.sb(f"a0_ctx{i}", [128, T], BF16) for i in range(NH // 2)]
    ctx.a0_ctxT0 = ctxT
    ident_bf = ctx.sb("a0_idbf", [64, 64], BF16)
    hd_sc = ctx.scope()
    Qh = hd_sc.sb("a0_Qh", [64, T], BF16)
    Kh = [hd_sc.sb("a0_Kh", [64, T], BF16)] * 2
    qaug = hd_sc.sb("a0_qaug", [1, T], F32)
    kaug = hd_sc.sb("a0_kaug", [1, T], F32)
    onesr = hd_sc.sb("a0_onesr", [1, TSEG], F32)
    VaT = [hd_sc.sb(f"a0_VaT{i}", [64, TSEG], BF16) for i in range(2)]
    Va = [hd_sc.sb("a0_Va", [128, ntc * 68], BF16)] * 2
    Mp = hd_sc.sb("a0_Mp", [128, 32], F32)
    nseg_row = hd_sc.sb("a0_nr", [1, TSEG], F32)
    rot = [hd_sc.sb(f"a0_rot{i}", [32, TSEG], F32) for i in range(4)]
    csc = hd_sc.sb("a0_csc", [32, 2], F32)
    P_b = [hd_sc.sb(f"a0_P{i}", [128, TSEG], BF16) for i in range(2)]
    drec = hd_sc.sb("a0_drec", [1, TSEG], F32)
    drow = nseg_row
    mtmp = hd_sc.sb("a0_mtmp", [128, 2], F32)

    ps_sc = ctx.scope()
    pr_ps = [ps_sc.ps(f"a0_pr{i}", [64, TSEG]) for i in range(2)]
    S_ps = [ps_sc.ps(f"a0_S{i}", [128, TSEG]) for i in range(2)]
    acc_ps = ps_sc.ps("a0_acc", [65, TSEG])
    bc_ps = ps_sc.ps("a0_bc", [64, TSEG])
    row_ps = ps_sc.ps("a0_row", [1, TSEG])
    vt_ps = ps_sc.ps("a0_vt", [128, 64], BF16)

    st = p.sem("a0_t")
    sg = p.sem("a0_g")
    p.ins("vector", lambda e, s: e.tensor_copy(ident_bf[:], cst["c_ident"][0:64, 0:64]),
          waits=ln_done, inc=(sv, 1))

    pr_free = {}
    S_free = {}
    S_free_a = {}
    P_free = {}
    sq_free = {}
    W_READY = p.count(wl)

    def proj_kv(kv):
        for g in range(nseg):
            sl = slice(g * TSEG, (g + 1) * TSEG)
            b = g % 2
            for k in range(4):
                w = [(sv, pr_free.get(b, 0))]
                if kv == 0 and g == 0 and k == 0:
                    w += [(wl, W_READY), (sv, 2)]
                p.ins("tensor", lambda e, s, k=k, sl=sl, b=b, kv=kv: e.matmul(
                    pr_ps[b][:, :], wk_b[:, 128 * k + 64 * kv:128 * k + 64 * (kv + 1)],
                    x_ln[4 + k][:, sl], start=(k == 0), stop=(k == 3)),
                    waits=w if k == 0 else [], inc=(st, 1) if k == 3 else None)
            p.ins("vector", lambda e, s, sl=sl, b=b, kv=kv: e.tensor_copy(
                Kh[kv][:, sl], pr_ps[b][:, :]), waits=[(st, p.count(st))], inc=(sv, 1))
            pr_free[b] = p.count(sv)
        for g in range(nseg):
            sl = slice(g * TSEG, (g + 1) * TSEG)
            b = g % 2
            for k in range(4):
                w = [(sv, pr_free.get(b, 0))] if k == 0 else []
                p.ins("tensor", lambda e, s, k=k, sl=sl, b=b, kv=kv: e.matmul(
                    pr_ps[b][:, :], wv_b[:, 128 * k + 64 * kv:128 * k + 64 * (kv + 1)],
                    x_ln[k][:, sl], start=(k == 0), stop=(k == 3)),
                    waits=w, inc=(st, 1) if k == 3 else None)
            p.ins("vector", lambda e, s, b=b: e.tensor_copy(
                VaT[b][:, :], pr_ps[b][:, :]), waits=[(st, p.count(st))], inc=(sv, 1))
            pr_free[b] = p.count(sv)
            vdone = p.count(sv)
            for j in range(4):
                tc = 4 * g + j
                w = [(sv, vdone)] if j == 0 else [(sv, p.count(sv))]
                p.ins("tensor", lambda e, s, j=j, b=b: e.transpose(
                    vt_ps[:, :], VaT[b][:, 128 * j:128 * (j + 1)],
                    ident_bf[:, :]), waits=w, inc=(st, 1))
                p.ins("vector", lambda e, s, tc=tc, kv=kv: e.tensor_copy(
                    Va[kv][:].rearrange("p (c w) -> p c w", w=68)[:, tc, 0:64],
                    vt_ps[:, :]), waits=[(st, p.count(st))], inc=(sv, 1))
        p.ins("vector", lambda e, s, kv=kv: e.memset(
            Va[kv][:].rearrange("p (c w) -> p c w", w=68)[:, :, 64:65], 1.0),
            inc=(sv, 1))

    def proj_q(h):
        p.ins("tensor", lambda e, s, h=h: e.matmul(
            bc_ps[0:32, 0:2], cst["c_ones_row"][0:1, 0:32],
            cs_row[0:1, :].rearrange("o (two h) -> o two h", h=NH)[:, :, h],
            start=True, stop=True),
            waits=trig_done + [(sv, p.count(sv))], inc=(st, 1))
        p.ins("vector", lambda e, s: e.tensor_copy(csc[:, :], bc_ps[0:32, 0:2]),
              waits=[(st, p.count(st))], inc=(sv, 1))
        for g in range(nseg):
            sl = slice(g * TSEG, (g + 1) * TSEG)
            b = g % 2
            for k in range(4):
                w = [(sv, pr_free.get(b, 0))] if k == 0 else []
                p.ins("tensor", lambda e, s, k=k, sl=sl, b=b, h=h: e.matmul(
                    pr_ps[b][:, :], wq_b[:, DH * k + 64 * h:DH * k + 64 * (h + 1)],
                    x_ln[4 + k][:, sl], start=(k == 0), stop=(k == 3)),
                    waits=w, inc=(st, 1) if k == 3 else None)
            w = [(st, p.count(st))]
            p.ins("vector", lambda e, s, b=b: e.tensor_copy(
                rot[0][:, :], pr_ps[b][0:32, :]), waits=w)
            p.ins("vector", lambda e, s, b=b: e.tensor_copy(
                rot[1][:, :], pr_ps[b][32:64, :]))
            p.ins("vector", lambda e, s: e.tensor_single_scalar(
                rot[2][:, :], rot[1][:, :], csc[:, 0:1], ALU.mult))
            p.ins("vector", lambda e, s, sl=sl: e.scalar_tensor_tensor(
                Qh[0:32, sl], rot[0][:, :], csc[:, 1:2], rot[2][:, :],
                op0=ALU.mult, op1=ALU.subtract))
            p.ins("vector", lambda e, s: e.tensor_single_scalar(
                rot[3][:, :], rot[1][:, :], csc[:, 1:2], ALU.mult))
            p.ins("vector", lambda e, s, sl=sl: e.scalar_tensor_tensor(
                Qh[32:64, sl], rot[0][:, :], csc[:, 0:1], rot[3][:, :],
                op0=ALU.mult, op1=ALU.add), inc=(sv, 1))
            pr_free[b] = p.count(sv)

    def pass_a(h, kv):
        prep = p.count(sv)
        p.ins("vector", lambda e, s, h=h: e.tensor_scalar(
            qaug[:, :], cst["c_iota_T"][:, :], sl_row[0:1, 8 * h:8 * h + 1], 8.0,
            op0=ALU.mult, op1=ALU.mult),
            waits=[(sv, prep)] + trig_done, inc=(sv, 1))
        p.ins("vector", lambda e, s, h=h: e.tensor_scalar(
            kaug[:, :], cst["c_iota_T"][:, :], sl_row[0:1, 8 * h:8 * h + 1], -8.0,
            op0=ALU.mult, op1=ALU.mult), inc=(sv, 1))
        p.ins("vector", lambda e, s: e.memset(onesr[:, :], 1.0), inc=(sv, 1))
        prep2 = p.count(sv)
        for c in range(32):
            nkt = (c * 128 + 128 + TSEG - 1) // TSEG
            for kt in range(nkt):
                b = kt % 2
                kw = min(TSEG, T - kt * TSEG)
                w = [(sv, max(S_free_a.get(b, 0), prep2)), (ss, S_free.get(b, 0))]
                p.ins("tensor", lambda e, s, c=c, kt=kt, b=b, kw=kw, kv=kv: e.matmul(
                    S_ps[b][:, 0:kw], Qh[0:64, 128 * c:128 * (c + 1)],
                    Kh[kv][0:64, kt * TSEG:kt * TSEG + kw],
                    start=True, stop=False), waits=w)
                p.ins("tensor", lambda e, s, c=c, kt=kt, b=b, kw=kw: e.matmul(
                    S_ps[b][:, 0:kw], qaug[0:1, 128 * c:128 * (c + 1)],
                    onesr[0:1, 0:kw], start=False, stop=False))
                p.ins("tensor", lambda e, s, c=c, kt=kt, b=b, kw=kw: e.matmul(
                    S_ps[b][:, 0:kw], cst["c_ones_row"][0:1, :],
                    kaug[0:1, kt * TSEG:kt * TSEG + kw],
                    start=False, stop=True), inc=(st, 1))
                p.ins("vector", lambda e, s, b=b, kw=kw: e.reduce_max(
                    mtmp[:, 0:1], S_ps[b][:, 0:kw], axis=AX.X),
                    waits=[(st, p.count(st))], inc=(sv, 1))
                if kt == 0:
                    p.ins("vector", lambda e, s, c=c: e.tensor_copy(
                        Mp[:, c:c + 1], mtmp[:, 0:1]),
                        waits=[(sv, p.count(sv))], inc=(sv, 1))
                else:
                    p.ins("vector", lambda e, s, c=c: e.tensor_max(
                        Mp[:, c:c + 1], Mp[:, c:c + 1], mtmp[:, 0:1]),
                        waits=[(sv, p.count(sv))], inc=(sv, 1))
                S_free_a[b] = p.count(sv)
        for g8 in range(nseg):
            for j in range(4):
                c = 4 * g8 + j
                p.ins("tensor", lambda e, s, c=c, j=j: e.matmul(
                    row_ps[0:1, 128 * j:128 * (j + 1)], Mp[:, c:c + 1],
                    cst["c_ident"][:, :], start=(j == 0), stop=(j == 3)),
                    waits=[(sv, p.count(sv))] if j == 0 else [],
                    inc=(st, 1) if j == 3 else None)
            p.ins("vector", lambda e, s, g8=g8: e.tensor_sub(
                qaug[0:1, TSEG * g8:TSEG * (g8 + 1)],
                qaug[0:1, TSEG * g8:TSEG * (g8 + 1)], row_ps[0:1, :]),
                waits=[(st, p.count(st))], inc=(sv, 1))

    def attend(h, kv):
        prep_done = p.count(sv)
        for qi in range(nseg):
            qsl = slice(qi * TSEG, (qi + 1) * TSEG)
            nk = (qi + 1) * (TSEG // 128)
            for ki in range(nk):
                b = ki % 2
                ksl = slice(ki * 128, (ki + 1) * 128)
                w = [(ss, S_free.get(b, 0)), (sv, S_free_a.get(b, 0))]
                if qi == 0 and ki == 0:
                    w += [(sv, prep_done)]
                p.ins("tensor", lambda e, s, ksl=ksl, qsl=qsl, b=b, kv=kv: e.matmul(
                    S_ps[b][:, :], Kh[kv][0:64, ksl], Qh[0:64, qsl],
                    start=True, stop=False), waits=w)
                p.ins("tensor", lambda e, s, ksl=ksl, qsl=qsl, b=b: e.matmul(
                    S_ps[b][:, :], cst["c_ones_row"][0:1, :], qaug[0:1, qsl],
                    start=False, stop=False))
                p.ins("tensor", lambda e, s, ksl=ksl, qsl=qsl, b=b: e.matmul(
                    S_ps[b][:, :], kaug[0:1, ksl], onesr[0:1, :],
                    start=False, stop=True), inc=(st, 1))
                w2 = [(st, p.count(st)), (st, P_free.get(b, 0))]
                p.ins("scalar", lambda e, s, b=b: e.activation(
                    P_b[b][:], S_ps[b][:, :], AF.Exp, scale=0.125),
                    waits=w2, inc=(ss, 1))
                S_free[b] = p.count(ss)
                diag = (ki * 128 + 127 >= qi * TSEG)
                if diag:
                    p.ins("gpsimd", lambda e, s, b=b, ki=ki, qi=qi: e.affine_select(
                        P_b[b][:], P_b[b][:], [[1, TSEG]], ALU.is_ge, 0.0,
                        base=qi * TSEG - ki * 128, channel_multiplier=-1),
                        waits=[(ss, p.count(ss))], inc=(sg, 1))
                    avw = [(sg, p.count(sg))]
                else:
                    avw = [(ss, p.count(ss))]
                p.ins("tensor", lambda e, s, ki=ki, b=b, nk=nk, kv=kv: e.matmul(
                    acc_ps[:, :],
                    Va[kv][:].rearrange("p (c w) -> p c w", w=68)[:, ki, 0:65],
                    P_b[b][:], start=(ki == 0), stop=(ki == nk - 1)),
                    waits=avw, inc=(st, 1) if ki == nk - 1 else None)
                P_free[b] = p.count(st)
            p.ins("vector", lambda e, s: e.tensor_scalar_add(
                drow[:], acc_ps[64:65, :], 1.0),
                waits=[(st, p.count(st))], inc=(sv, 1))
            p.ins("vector", lambda e, s: e.reciprocal(drec[:], drow[:]),
                  waits=[(sv, p.count(sv))], inc=(sv, 1))
            p.ins("tensor", lambda e, s: e.matmul(
                bc_ps[:, :], cst["c_ones_row"][0:1, 0:64], drec[:, :],
                start=True, stop=True), waits=[(sv, p.count(sv))], inc=(st, 1))
            p.ins("scalar", lambda e, s: e.activation(
                outb[0:64, :], acc_ps[0:64, :], AF.Copy),
                waits=[(st, p.count(st))], inc=(ss, 1))
            p.ins("vector", lambda e, s, h=h, qsl=qsl: e.tensor_mul(
                ctxT[h // 2][64 * (h % 2):64 * (h % 2) + 64, qsl],
                outb[0:64, :], bc_ps[:, :]),
                waits=[(ss, p.count(ss))], inc=(sv, 1))

    for kv in (0, 1):
        proj_kv(kv)
        for h in range(4 * kv, 4 * kv + 4):
            proj_q(h)
            pass_a(h, kv)
            attend(h, kv)

    ps_sc.close()
    hd_sc.close()
    attn_all_done = p.count(sv)
    wo_b = ctx.sb("a0_wo", [128, (DH // 128) * D], BF16)
    p.ins("gpsimd", lambda e, s: e.dma_start(
        out=wo_b[:].rearrange("p (k d) -> p k d", k=DH // 128),
        in_=waps["woR"].rearrange("(k p) d -> p k d", p=128)),
        waits=[("a0_v", attn_all_done)], inc=(wl, 16))
    wo_rdy = p.count(wl)
    y_ps = [ctx.ps(f"a0_y{i}", [128, TSEG]) for i in range(2)]
    so = p.sem("a0_o")
    ob = [outb, outc]
    y_free2 = {}
    o_free = {}
    for g in range(nseg):
        sl = slice(g * TSEG, (g + 1) * TSEG)
        for dc in range(8):
            i = g * 8 + dc
            bk_ = i % 2
            for kc in range(DH // 128):
                w = []
                if kc == 0:
                    w = [(sv, y_free2.get(bk_, 0))]
                    if i == 0:
                        w += [(wl, wo_rdy), (sv, attn_all_done)]
                p.ins("tensor", lambda e, s, kc=kc, dc=dc, sl=sl, bk_=bk_: e.matmul(
                    y_ps[bk_][:, :], wo_b[:, D * kc + 128 * dc:D * kc + 128 * (dc + 1)],
                    ctxT[kc][:, sl], start=(kc == 0), stop=(kc == DH // 128 - 1)),
                    waits=w, inc=(st, 1) if kc == DH // 128 - 1 else None)
            p.ins("vector", lambda e, s, bk_=bk_: e.tensor_copy(
                ob[bk_][:], y_ps[bk_][:, :]),
                waits=[(st, p.count(st)), (so, p.count(so))], inc=(sv, 1))
            y_free2[bk_] = p.count(sv)
            p.ins("gpsimd", lambda e, s, dc=dc, sl=sl, bk_=bk_: e.dma_start(
                out=yT_ap[128 * dc:128 * (dc + 1), sl], in_=ob[bk_][:]),
                waits=[(sv, p.count(sv))], inc=(so, 16))
            o_free[bk_] = p.count(so)
    p.barrier("gpsimd", [(so, p.count(so))])


def build_l0attn_launch(NH=8):
    nc = bass.Bass("TRN2", target_bir_lowering=False)
    T = ATT_T
    DH = 64 * NH
    xT = nc.dram_tensor("xT", [D, T], F32, kind="ExternalInput").ap()
    waps = {
        "wqT": nc.dram_tensor("wqT", [512, DH], F32, kind="ExternalInput").ap(),
        "wkT": nc.dram_tensor("wkT", [512, 128], F32, kind="ExternalInput").ap(),
        "wvT": nc.dram_tensor("wvT", [512, 128], F32, kind="ExternalInput").ap(),
        "woR": nc.dram_tensor("woR", [DH, D], F32, kind="ExternalInput").ap(),
        "phase": nc.dram_tensor("phase", [1, NH], F32, kind="ExternalInput").ap(),
        "slopes": nc.dram_tensor("slopes", [1, 8 * NH], F32, kind="ExternalInput").ap(),
    }
    yT = nc.dram_tensor("yT", [D, T], F32, kind="ExternalOutput").ap()
    prog = Prog(nc)
    with ExitStack() as stack:
        ctx = Ctx(nc, prog, stack, T)
        emit_l0attn(ctx, xT, waps, yT, NH)
        block = stack.enter_context(nc.Block())
        prog.run(block, stack)
    return nc, ctx



def emit_l2attn(ctx, xT_ap, waps, yT_ap, NH=8):
    nc, p, T = ctx.nc, ctx.prog, ctx.T
    nseg = T // TSEG
    ntc = T // 128
    M_ = 256
    TWO_PI = 2.0 * PI
    cst = ctx.use_consts(["c_ones_col", "c_ones_row", "c_ident"])

    x_ln = [ctx.sb(f"xln{k}", [128, T], BF16) for k in range(8)]
    ready = [("c_ld", p.count("c_ld"))]
    ln_done = emit_ln_stream(ctx, xT_ap, x_ln, "a2ln", ready)

    DH = 64 * NH
    wq_b = ctx.sb("a2_wq", [128, 8 * DH], BF16)
    wk_b = ctx.sb("a2_wk", [128, 8 * DH], BF16)
    wv_b = ctx.sb("a2_wv", [128, 8 * DH], BF16)
    rf_b = ctx.sb("a2_rf", [64, M_], BF16)
    wl = p.sem("a2_wl")
    for nm, buf in (("wqT", wq_b), ("wkT", wk_b), ("wvT", wv_b)):
        p.ins("gpsimd", lambda e, s, nm=nm, buf=buf: e.dma_start(
            out=buf[:].rearrange("p (k h) -> p k h", k=8),
            in_=waps[nm].rearrange("(k p) h -> p k h", p=128)),
            waits=ln_done, inc=(wl, 16))
    p.ins("gpsimd", lambda e, s: e.dma_start(out=rf_b[:], in_=waps["rfT"]),
          waits=ln_done, inc=(wl, 16))

    outb = ctx.sb("a2_outb", [128, TSEG], F32)
    outc = ctx.sb("a2_outc", [128, TSEG], F32)
    ctxT = [ctx.sb(f"a2_ctx{i}", [128, T], BF16) for i in range(NH // 2)]
    ident_bf = ctx.sb("a2_idbf", [64, 64], BF16)
    ident65 = ctx.sb("a2_id65", [65, 65], BF16)

    hd_sc = ctx.scope()
    QT = hd_sc.sb("a2_QT", [64, T], BF16)
    KT = hd_sc.sb("a2_KT", [64, T], BF16)
    Qf_t = [hd_sc.sb(f"a2_Qf{i}", [128, TSEG], BF16) for i in range(4)]
    VaT = [hd_sc.sb(f"a2_VaT{i}", [64, TSEG], BF16) for i in range(2)]
    Va = hd_sc.sb("a2_Va", [128, ntc * 68], BF16)
    Kf = [hd_sc.sb(f"a2_Kf{i}", [128, TSEG], BF16) for i in range(2)]
    KVf = hd_sc.sb("a2_KVf", [128, 4 * 72], BF16)
    kvsb = hd_sc.sb("a2_kvsb", [65, TSEG], BF16)
    skseg = hd_sc.sb("a2_sks", [1, TSEG], F32)
    sclq = hd_sc.sb("a2_sq", [1, T], F32)
    zrow = hd_sc.sb("a2_z", [1, TSEG], F32)
    crow = hd_sc.sb("a2_c", [1, TSEG], F32)
    sqs = [hd_sc.sb(f"a2_sqs{i}", [64, TSEG], F32) for i in range(2)]
    tr1 = hd_sc.sb("a2_tr1", [128, TSEG], F32)
    tr2 = hd_sc.sb("a2_tr2", [128, TSEG], I32)
    tr3 = hd_sc.sb("a2_tr3", [128, TSEG], F32)

    ps_sc = ctx.scope()
    pj_ps = [ps_sc.ps(f"a2_pj{i}", [128, TSEG]) for i in range(2)]
    pr_ps = [pj_ps[0], pj_ps[1]]
    kv_ps = ps_sc.ps("a2_kv", [65, TSEG])
    acc_ps = ps_sc.ps("a2_acc", [65, TSEG])
    bc_ps = ps_sc.ps("a2_bc", [128, TSEG])
    row_ps = ps_sc.ps("a2_row", [1, TSEG])
    vtb_ps = ps_sc.ps("a2_vtb", [128, 128], BF16)

    st = p.sem("a2_t")
    ss = p.sem("a2_s")
    sv = p.sem("a2_v")
    p.ins("vector", lambda e, s: e.tensor_copy(ident_bf[:], cst["c_ident"][0:64, 0:64]),
          waits=ln_done, inc=(sv, 1))
    W_READY = p.count(wl)

    pr_free = {}
    pj_free = {}
    sq_free = {}
    kf_free = {}


    def head(h):
        for which, wbuf, dst in (("q", wq_b, QT), ("k", wk_b, KT)):
            for g in range(nseg):
                sl = slice(g * TSEG, (g + 1) * TSEG)
                b = g % 2
                for k in range(8):
                    w = [(sv, pr_free.get(b, 0))] if k == 0 else []
                    if h == 0 and g == 0 and k == 0 and which == "q":
                        w += [(wl, W_READY), (sv, 1)]
                    p.ins("tensor", lambda e, s, k=k, sl=sl, b=b, h=h, wbuf=wbuf: e.matmul(
                        pr_ps[b][0:64, :], wbuf[:, DH * k + 64 * h:DH * k + 64 * (h + 1)],
                        x_ln[k][:, sl], start=(k == 0), stop=(k == 7)),
                        waits=w, inc=(st, 1) if k == 7 else None)
                p.ins("vector", lambda e, s, sl=sl, b=b, dst=dst: e.tensor_copy(
                    dst[:, sl], pr_ps[b][0:64, :]), waits=[(st, p.count(st))], inc=(sv, 1))
                pr_free[b] = p.count(sv)
        for g in range(nseg):
            sl = slice(g * TSEG, (g + 1) * TSEG)
            b = g % 2
            for k in range(8):
                w = [(sv, pr_free.get(b, 0))] if k == 0 else []
                p.ins("tensor", lambda e, s, k=k, sl=sl, b=b, h=h: e.matmul(
                    pr_ps[b][0:64, :], wv_b[:, DH * k + 64 * h:DH * k + 64 * (h + 1)],
                    x_ln[k][:, sl], start=(k == 0), stop=(k == 7)),
                    waits=w, inc=(st, 1) if k == 7 else None)
            p.ins("vector", lambda e, s, b=b: e.tensor_copy(
                VaT[b][:, :], pr_ps[b][0:64, :]), waits=[(st, p.count(st))], inc=(sv, 1))
            pr_free[b] = p.count(sv)
            vdone = p.count(sv)
            for j in range(4):
                tc = 4 * g + j
                w = [(sv, vdone)] if j == 0 else [(sv, p.count(sv))]
                p.ins("tensor", lambda e, s, j=j, b=b: e.transpose(
                    vtb_ps[0:128, 0:64], VaT[b][:, 128 * j:128 * (j + 1)],
                    ident_bf[:, :]), waits=w, inc=(st, 1))
                p.ins("vector", lambda e, s, tc=tc: e.tensor_copy(
                    Va[:].rearrange("p (c w) -> p c w", w=68)[:, tc, 0:64],
                    vtb_ps[0:128, 0:64]), waits=[(st, p.count(st))], inc=(sv, 1))
        for which, srct in (("q", QT), ("k", KT)):
            for g in range(nseg):
                sl = slice(g * TSEG, (g + 1) * TSEG)
                b = g % 2
                w = [(st, sq_free.get(b, 0))] if b in sq_free else []
                if g == 0 and which == "q":
                    w += [(sv, p.count(sv))]
                p.ins("scalar", lambda e, s, sl=sl, b=b, srct=srct: e.activation(
                    sqs[b][:, :], srct[0:64, sl], AF.Square), waits=w, inc=(ss, 1))
                p.ins("tensor", lambda e, s, b=b: e.matmul(
                    row_ps[:, :], cst["c_ones_col"][0:64, :], sqs[b][:, :],
                    start=True, stop=True),
                    waits=[(ss, p.count(ss)), (sv, p.count(sv))], inc=(st, 1))
                sq_free[b] = p.count(st)
                if which == "q":
                    p.ins("scalar", lambda e, s, sl=sl: e.activation(
                        sclq[0:1, sl], row_ps[:, :], AF.Exp, scale=-0.5),
                        waits=[(st, p.count(st))], inc=(ss, 1))
                    p.ins("vector", lambda e, s, sl=sl: e.tensor_scalar_mul(
                        sclq[0:1, sl], sclq[0:1, sl], 1.0 / 16),
                        waits=[(ss, p.count(ss))], inc=(sv, 1))
                else:
                    p.ins("scalar", lambda e, s: e.activation(
                        skseg[:, :], row_ps[:, :], AF.Exp, scale=-0.5),
                        waits=[(st, p.count(st))], inc=(ss, 1))
                    p.ins("vector", lambda e, s: e.tensor_scalar_mul(
                        skseg[:, :], skseg[:, :], 1.0 / 16),
                        waits=[(ss, p.count(ss))], inc=(sv, 1))
                    for j in range(4):
                        tc = 4 * g + j
                        p.ins("tensor", lambda e, s, j=j: e.matmul(
                            bc_ps[0:128, 0:1], skseg[0:1, 128 * j:128 * (j + 1)],
                            cst["c_ones_col"][0:1, 0:1], start=True, stop=True),
                            waits=[(sv, p.count(sv))], inc=(st, 1))
                        p.ins("vector", lambda e, s, tc=tc: e.tensor_copy(
                            Va[:].rearrange("p (c w) -> p c w", w=68)[:, tc, 64:65],
                            bc_ps[0:128, 0:1]), waits=[(st, p.count(st))], inc=(sv, 1))
                        p.ins("vector", lambda e, s, tc=tc: e.tensor_scalar(
                            Va[:].rearrange("p (c w) -> p c w", w=68)[:, tc, 0:64],
                            Va[:].rearrange("p (c w) -> p c w", w=68)[:, tc, 0:64],
                            bc_ps[0:128, 0:1], None, op0=ALU.mult),
                            waits=[(sv, p.count(sv))], inc=(sv, 1))
        kv_done0 = p.count(sv)
        for tc in range(ntc):
            b = tc % 2
            sl = slice(128 * tc, 128 * (tc + 1))
            w = [(sv, max(pj_free.get(b, 0), pr_free.get(b, 0)))]
            p.ins("tensor", lambda e, s, sl=sl, b=b: e.matmul(
                pj_ps[b][:, 0:M_], KT[0:64, sl], rf_b[:, :],
                start=True, stop=True), waits=w, inc=(st, 1))
            for half, colsl in ((True, slice(0, M_)), (False, slice(M_, 2 * M_))):
                off = PI / 2.0 if half else 0.0
                p.ins("vector", lambda e, s, b=b, off=off: e.tensor_scalar(
                    tr2[:, 0:M_], pj_ps[b][:, 0:M_], off, 1.0 / TWO_PI,
                    op0=ALU.add, op1=ALU.mult),
                    waits=[(st, p.count(st)), (ss, p.count(ss))])
                p.ins("vector", lambda e, s: e.tensor_copy(tr3[:, 0:M_], tr2[:, 0:M_]))
                p.ins("vector", lambda e, s, b=b, off=off: e.tensor_scalar(
                    tr1[:, 0:M_], pj_ps[b][:, 0:M_], off, 1.0,
                    op0=ALU.add, op1=ALU.mult))
                p.ins("vector", lambda e, s: e.scalar_tensor_tensor(
                    tr1[:, 0:M_], tr3[:, 0:M_], -TWO_PI, tr1[:, 0:M_],
                    op0=ALU.mult, op1=ALU.add), inc=(sv, 1))
                p.ins("scalar", lambda e, s, b=b, colsl=colsl: e.activation(
                    Kf[b][:, colsl], tr1[:, 0:M_], AF.Sin),
                    waits=[(sv, p.count(sv))] + ([(st, kf_free.get(b, 0))] if half else []),
                    inc=(ss, 1))
            p.ins("tensor", lambda e, s, tc=tc, b=b: e.matmul(
                kv_ps[:, :], Va[:].rearrange("p (c w) -> p c w", w=68)[:, tc, 0:65],
                Kf[b][:, :], start=(tc == 0), stop=(tc == ntc - 1)),
                waits=[(ss, p.count(ss))], inc=(st, 1))
            kf_free[b] = p.count(st)
        p.ins("vector", lambda e, s: e.tensor_copy(kvsb[:, :], kv_ps[:, :]),
              waits=[(st, p.count(st))], inc=(sv, 1))
        for j in range(4):
            w = [(sv, p.count(sv))] if j == 0 else [(sv, p.count(sv))]
            p.ins("tensor", lambda e, s, j=j: e.transpose(
                vtb_ps[0:128, 0:65], kvsb[:, 128 * j:128 * (j + 1)],
                ident65[:, :]), waits=w, inc=(st, 1))
            p.ins("vector", lambda e, s, j=j: e.tensor_copy(
                KVf[:].rearrange("p (c w) -> p c w", w=72)[:, j, 0:65],
                vtb_ps[0:128, 0:65]), waits=[(st, p.count(st))], inc=(sv, 1))
        qf_free = 0
        for qi in range(nseg):
            qsl = slice(qi * TSEG, (qi + 1) * TSEG)
            for mc in range(2):
                b = mc
                w = [(sv, max(pj_free.get(b, 0), pr_free.get(b, 0)))]
                if qi == 0:
                    w += [(sv, p.count(sv))]
                p.ins("tensor", lambda e, s, mc=mc, qsl=qsl, b=b: e.matmul(
                    pj_ps[b][:, :], rf_b[:, 128 * mc:128 * (mc + 1)], QT[0:64, qsl],
                    start=True, stop=True), waits=w, inc=(st, 1))
                pjc = p.count(st)
                for part, half in ((mc, True), (2 + mc, False)):
                    off = PI / 2.0 if half else 0.0
                    p.ins("vector", lambda e, s, b=b, off=off: e.tensor_scalar(
                        tr2[:, :], pj_ps[b][:, :], off, 1.0 / TWO_PI,
                        op0=ALU.add, op1=ALU.mult),
                        waits=[(st, pjc), (ss, p.count(ss))])
                    p.ins("vector", lambda e, s: e.tensor_copy(tr3[:, :], tr2[:, :]))
                    p.ins("vector", lambda e, s, b=b, off=off: e.tensor_scalar(
                        tr1[:, :], pj_ps[b][:, :], off, 1.0, op0=ALU.add, op1=ALU.mult))
                    p.ins("vector", lambda e, s: e.scalar_tensor_tensor(
                        tr1[:, :], tr3[:, :], -TWO_PI, tr1[:, :],
                        op0=ALU.mult, op1=ALU.add), inc=(sv, 1))
                    p.ins("scalar", lambda e, s, part=part: e.activation(
                        Qf_t[part][:, :], tr1[:, :], AF.Sin),
                        waits=[(sv, p.count(sv)), (st, qf_free)], inc=(ss, 1))
                p.barrier("vector", [(ss, p.count(ss))], sem="a2_v")
                pj_free[b] = p.count(sv)
            for fc in range(4):
                w = [(ss, p.count(ss)), (sv, p.count(sv))] if fc == 0 else []
                p.ins("tensor", lambda e, s, fc=fc: e.matmul(
                    acc_ps[:, :], KVf[:].rearrange("p (c w) -> p c w", w=72)[:, fc, 0:65],
                    Qf_t[fc][:, :], start=(fc == 0), stop=(fc == 3)),
                    waits=w, inc=(st, 1) if fc == 3 else None)
            qf_free = p.count(st)
            p.ins("vector", lambda e, s: e.tensor_copy(zrow[:], acc_ps[64:65, :]),
                  waits=[(st, p.count(st))], inc=(sv, 1))
            p.ins("vector", lambda e, s, qsl=qsl: e.tensor_mul(
                zrow[:], zrow[:], sclq[0:1, qsl]), waits=[(sv, p.count(sv))],
                inc=(sv, 1))
            p.ins("vector", lambda e, s: e.tensor_scalar_max(zrow[:], zrow[:], 1e-6),
                  waits=[(sv, p.count(sv))], inc=(sv, 1))
            p.ins("vector", lambda e, s: e.reciprocal(crow[:], zrow[:]),
                  waits=[(sv, p.count(sv))], inc=(sv, 1))
            p.ins("vector", lambda e, s, qsl=qsl: e.tensor_mul(
                crow[:], crow[:], sclq[0:1, qsl]), waits=[(sv, p.count(sv))],
                inc=(sv, 1))
            p.ins("tensor", lambda e, s: e.matmul(
                bc_ps[0:64, :], cst["c_ones_row"][0:1, 0:64], crow[:, :],
                start=True, stop=True), waits=[(sv, p.count(sv))], inc=(st, 1))
            p.ins("scalar", lambda e, s: e.activation(
                outb[0:64, :], acc_ps[0:64, :], AF.Copy),
                waits=[(st, p.count(st))], inc=(ss, 1))
            p.ins("vector", lambda e, s, h=h, qsl=qsl: e.tensor_mul(
                ctxT[h // 2][64 * (h % 2):64 * (h % 2) + 64, qsl],
                outb[0:64, :], bc_ps[0:64, :]),
                waits=[(ss, p.count(ss))], inc=(sv, 1))

    p.ins("vector", lambda e, s: e.tensor_copy(ident65[:], cst["c_ident"][0:65, 0:65]),
          waits=ln_done, inc=(sv, 1))

    for h in range(NH):
        head(h)

    ps_sc.close()
    hd_sc.close()
    attn_all_done = p.count(sv)
    wo_b = ctx.sb("a2_wo", [128, (DH // 128) * D], BF16)
    p.ins("gpsimd", lambda e, s: e.dma_start(
        out=wo_b[:].rearrange("p (k d) -> p k d", k=DH // 128),
        in_=waps["woR"].rearrange("(k p) d -> p k d", p=128)),
        waits=[("a2_v", attn_all_done)], inc=(wl, 16))
    wo_rdy = p.count(wl)
    y_ps = [ctx.ps(f"a2_y{i}", [128, TSEG]) for i in range(2)]
    so = p.sem("a2_o")
    ob = [outb, outc]
    y_free2 = {}
    for g in range(nseg):
        sl = slice(g * TSEG, (g + 1) * TSEG)
        for dc in range(8):
            i = g * 8 + dc
            bk_ = i % 2
            for kc in range(DH // 128):
                w = []
                if kc == 0:
                    w = [(sv, y_free2.get(bk_, 0))]
                    if i == 0:
                        w += [(wl, wo_rdy), (sv, attn_all_done)]
                p.ins("tensor", lambda e, s, kc=kc, dc=dc, sl=sl, bk_=bk_: e.matmul(
                    y_ps[bk_][:, :], wo_b[:, D * kc + 128 * dc:D * kc + 128 * (dc + 1)],
                    ctxT[kc][:, sl], start=(kc == 0), stop=(kc == DH // 128 - 1)),
                    waits=w, inc=(st, 1) if kc == DH // 128 - 1 else None)
            p.ins("vector", lambda e, s, bk_=bk_: e.tensor_copy(
                ob[bk_][:], y_ps[bk_][:, :]),
                waits=[(st, p.count(st)), (so, p.count(so))], inc=(sv, 1))
            y_free2[bk_] = p.count(sv)
            p.ins("gpsimd", lambda e, s, dc=dc, sl=sl, bk_=bk_: e.dma_start(
                out=yT_ap[128 * dc:128 * (dc + 1), sl], in_=ob[bk_][:]),
                waits=[(sv, p.count(sv))], inc=(so, 16))
    p.barrier("gpsimd", [(so, p.count(so))])


def build_l2attn_launch(NH=8):
    nc = bass.Bass("TRN2", target_bir_lowering=False)
    T = ATT_T
    DH = 64 * NH
    xT = nc.dram_tensor("xT", [D, T], F32, kind="ExternalInput").ap()
    waps = {
        "wqT": nc.dram_tensor("wqT", [D, DH], F32, kind="ExternalInput").ap(),
        "wkT": nc.dram_tensor("wkT", [D, DH], F32, kind="ExternalInput").ap(),
        "wvT": nc.dram_tensor("wvT", [D, DH], F32, kind="ExternalInput").ap(),
        "woR": nc.dram_tensor("woR", [DH, D], F32, kind="ExternalInput").ap(),
        "rfT": nc.dram_tensor("rfT", [64, 256], F32, kind="ExternalInput").ap(),
    }
    yT = nc.dram_tensor("yT", [D, T], F32, kind="ExternalOutput").ap()
    prog = Prog(nc)
    with ExitStack() as stack:
        ctx = Ctx(nc, prog, stack, T)
        emit_l2attn(ctx, xT, waps, yT, NH)
        block = stack.enter_context(nc.Block())
        prog.run(block, stack)
    return nc, ctx




def emit_l3attn(ctx, xT_ap, waps, yT_ap, NH=8):
    nc, p, T = ctx.nc, ctx.prog, ctx.T
    assert T == ATT_T
    nseg = T // TSEG
    ntc = T // 128
    U = 50
    NITER = 38
    cst = ctx.use_consts(["c_ones_col", "c_ones_row", "c_ident", "c_iota_col"])

    x_ln = [ctx.sb(f"xln{k}", [128, T], BF16) for k in range(8)]
    ready = [("c_ld", p.count("c_ld"))]
    ln_done = emit_ln_stream(ctx, xT_ap, x_ln, "a3ln", ready)

    DH = 64 * NH
    wq_b = ctx.sb("a3_wq", [128, 8 * DH], BF16)
    wk_b = ctx.sb("a3_wk", [128, 8 * DH], BF16)
    wv_b = ctx.sb("a3_wv", [128, 8 * DH], BF16)
    bq_b = ctx.sb("a3_bq", [1, DH], BF16)
    bk_b = ctx.sb("a3_bk", [1, DH], BF16)
    bv_b = ctx.sb("a3_bv", [1, DH], BF16)
    ident_bf = ctx.sb("a3_idbf", [128, 128], BF16)
    ones_seg = ctx.sb("a3_ones", [1, TSEG], BF16)
    wl = p.sem("a3_wl")
    for nm, buf in (("wqT", wq_b), ("wkT", wk_b), ("wvT", wv_b),
                    ("bq", bq_b), ("bk", bk_b), ("bv", bv_b)):
        if nm.startswith("w"):
            p.ins("gpsimd", lambda e, s, nm=nm, buf=buf: e.dma_start(
                out=buf[:].rearrange("p (k h) -> p k h", k=8),
                in_=waps[nm].rearrange("(k p) h -> p k h", p=128)),
                waits=ln_done, inc=(wl, 16))
        else:
            p.ins("gpsimd", lambda e, s, nm=nm, buf=buf: e.dma_start(
                out=buf[:], in_=waps[nm]), waits=ln_done, inc=(wl, 16))
    sv = p.sem("a3_v")
    st = p.sem("a3_t")
    ss = p.sem("a3_s")
    p.ins("vector", lambda e, s: e.memset(ones_seg[:], 1.0),
          waits=ln_done, inc=(sv, 1))
    p.ins("vector", lambda e, s: e.tensor_copy(ident_bf[:], cst["c_ident"][:, :]),
          inc=(sv, 1))

    idx_i = ctx.sb("a3_idxi", [1, U], I32)
    idxf = ctx.sb("a3_idxf", [1, U], F32)
    selsc = ctx.sb("a3_ssc", [128, U], F32)
    SELb = ctx.sb("a3_SEL", [128, ntc * U], BF16)
    il = p.sem("a3_il")
    p.ins("sync", lambda e, s: e.dma_start(out=idx_i[:], in_=waps["idx"]),
          waits=ln_done, inc=(il, 16))
    p.ins("vector", lambda e, s: e.tensor_copy(idxf[:], idx_i[:]),
          waits=[(il, 16), (sv, p.count(sv))], inc=(sv, 1))
    p.ins("vector", lambda e, s: e.tensor_scalar(
        idxf[:], idxf[:], float(T - 1), None, op0=ALU.min),
        waits=[(sv, p.count(sv))], inc=(sv, 1))

    outb = ctx.sb("a3_outb", [128, TSEG], F32)
    outc = ctx.sb("a3_outc", [128, TSEG], F32)
    ctxT = [ctx.sb(f"a3_ctx{i}", [128, T], BF16) for i in range(NH // 2)]
    vm_sb = ctx.sb("a3_vm", [64, TSEG], F32)
    arow = ctx.sb("a3_ar", [64, TSEG], F32)
    mrow = ctx.sb("a3_mr", [1, T], BF16)
    Mall = ctx.sb("a3_M", [128, 32], F32)
    bmsk = ctx.sb("a3_bm", [128, 32], F32)
    bcnt = ctx.sb("a3_bc1", [128, 1], F32)
    vmacc = ctx.sb("a3_vma", [64, 1], F32)
    vmr = ctx.sb("a3_vmr", [1, 64], F32)
    onesf = ctx.sb("a3_onef", [1, TSEG], F32)
    ksmp = ctx.sb("a3_ks", [64, U], BF16)
    ktok = ctx.sb("a3_kt", [128, 64], BF16)
    sc1 = ctx.sb("a3_sc1", [128, 1], F32)
    sc2 = ctx.sb("a3_sc2", [128, 1], F32)
    blo = ctx.sb("a3_lo", [1, 1], F32)
    bhi = ctx.sb("a3_hi", [1, 1], F32)
    bmid = ctx.sb("a3_mid", [1, 1], F32)
    bt1 = ctx.sb("a3_t1", [1, 1], F32)
    bt2 = ctx.sb("a3_t2", [1, 1], F32)
    bcs = ctx.sb("a3_cs", [1, 1], F32)
    p.ins("vector", lambda e, s: e.memset(onesf[:], 1.0), inc=(sv, 1))

    hd_sc = ctx.scope()
    Qa = hd_sc.sb("a3_Qa", [65, T], BF16)
    Ka = hd_sc.sb("a3_Ka", [65, T], BF16)
    VaT = [hd_sc.sb(f"a3_VaT{i}", [64, TSEG], BF16) for i in range(2)]
    Va = hd_sc.sb("a3_Va", [128, ntc * 72], BF16)
    q2r = hd_sc.sb("a3_q2", [1, T], BF16)
    k2r = hd_sc.sb("a3_k2", [1, T], BF16)
    sqs = [hd_sc.sb(f"a3_sqs{i}", [64, TSEG], F32) for i in range(2)]
    kmax = hd_sc.sb("a3_km", [1, 1], F32)
    P_b = [hd_sc.sb(f"a3_P{i}", [128, TSEG], BF16) for i in range(2)]
    drow = hd_sc.sb("a3_dr", [1, TSEG], F32)
    drec = hd_sc.sb("a3_drec", [1, TSEG], F32)

    ps_sc = ctx.scope()
    pr_ps = [ps_sc.ps(f"a3_pr{i}", [64, TSEG]) for i in range(2)]
    S_ps = [ps_sc.ps(f"a3_S{i}", [128, TSEG]) for i in range(2)]
    acc_ps = ps_sc.ps("a3_acc", [65, TSEG])
    bc_ps = ps_sc.ps("a3_bc", [64, TSEG])
    row_ps = ps_sc.ps("a3_row", [1, TSEG])
    vt_ps = ps_sc.ps("a3_vt", [128, 64], BF16)

    W_READY = p.count(wl)
    pr_free = {}
    S_free = {}
    P_free = {}
    sq_free = {}

    p.ins("tensor", lambda e, s: e.matmul(
        S_ps[0][:, 0:U], cst["c_ones_row"][0:1, :], idxf[:, :],
        start=True, stop=True), waits=[(sv, p.count(sv))], inc=(st, 1))
    for c in range(ntc):
        w = [(st, p.count(st))] if c == 0 else [(sv, p.count(sv))]
        p.ins("vector", lambda e, s, c=c: e.tensor_scalar(
            selsc[:, :], S_ps[0][:, 0:U], float(-128 * c), None, op0=ALU.add),
            waits=w, inc=(sv, 1))
        p.ins("vector", lambda e, s, c=c: e.tensor_scalar(
            SELb[:, U * c:U * (c + 1)], selsc[:, :], cst["c_iota_col"][:, 0:1],
            None, op0=ALU.is_equal), waits=[(sv, p.count(sv))], inc=(sv, 1))
    sel_done = p.count(sv)

    def proj_qk(h, wbuf, bbuf, dst):
        for g in range(nseg):
            sl = slice(g * TSEG, (g + 1) * TSEG)
            b = g % 2
            for k in range(8):
                w = []
                if k == 0:
                    w = [(sv, pr_free.get(b, 0))]
                    if h == 0 and g == 0:
                        w += [(wl, W_READY), (sv, sel_done)]
                p.ins("tensor", lambda e, s, k=k, sl=sl, b=b, h=h, wbuf=wbuf: e.matmul(
                    pr_ps[b][:, :], wbuf[:, DH * k + 64 * h:DH * k + 64 * (h + 1)],
                    x_ln[k][:, sl], start=(k == 0), stop=False), waits=w)
            p.ins("tensor", lambda e, s, h=h, b=b, bbuf=bbuf: e.matmul(
                pr_ps[b][:, :], bbuf[0:1, 64 * h:64 * (h + 1)], ones_seg[0:1, :],
                start=False, stop=True), inc=(st, 1))
            p.ins("vector", lambda e, s, sl=sl, b=b, dst=dst: e.tensor_copy(
                dst[0:64, sl], pr_ps[b][:, :]), waits=[(st, p.count(st))],
                inc=(sv, 1))
            pr_free[b] = p.count(sv)

    def proj_v(h):
        p.ins("vector", lambda e, s: e.memset(vmacc[:], 0.0), inc=(sv, 1))
        for g in range(nseg):
            sl = slice(g * TSEG, (g + 1) * TSEG)
            b = g % 2
            for k in range(8):
                w = [(sv, pr_free.get(b, 0))] if k == 0 else []
                p.ins("tensor", lambda e, s, k=k, sl=sl, b=b, h=h: e.matmul(
                    pr_ps[b][:, :], wv_b[:, DH * k + 64 * h:DH * k + 64 * (h + 1)],
                    x_ln[k][:, sl], start=(k == 0), stop=False), waits=w)
            p.ins("tensor", lambda e, s, h=h, b=b: e.matmul(
                pr_ps[b][:, :], bv_b[0:1, 64 * h:64 * (h + 1)], ones_seg[0:1, :],
                start=False, stop=True), inc=(st, 1))
            p.ins("vector", lambda e, s, b=b: e.tensor_copy(
                VaT[b][:, :], pr_ps[b][:, :]), waits=[(st, p.count(st))], inc=(sv, 1))
            vdone = p.count(sv)
            p.ins("vector", lambda e, s, b=b: e.tensor_reduce(
                sqs[0][:, 0:1], pr_ps[b][:, :], axis=AX.X, op=ALU.add),
                waits=[(sv, vdone)], inc=(sv, 1))
            pr_free[b] = p.count(sv)
            p.ins("vector", lambda e, s: e.tensor_add(
                vmacc[:], vmacc[:], sqs[0][0:64, 0:1]),
                waits=[(sv, p.count(sv))], inc=(sv, 1))
            for j in range(4):
                tc = 4 * g + j
                w = [(sv, p.count(sv))]
                p.ins("tensor", lambda e, s, j=j, b=b: e.transpose(
                    vt_ps[:, :], VaT[b][:, 128 * j:128 * (j + 1)],
                    ident_bf[0:64, 0:64]), waits=w, inc=(st, 1))
                p.ins("vector", lambda e, s, tc=tc: e.tensor_copy(
                    Va[:].rearrange("p (c w) -> p c w", w=72)[:, tc, 0:64], vt_ps[:, :]),
                    waits=[(st, p.count(st))], inc=(sv, 1))
        p.ins("vector", lambda e, s: e.memset(
            Va[:].rearrange("p (c w) -> p c w", w=72)[:, :, 64:65], 1.0), inc=(sv, 1))
        p.ins("tensor", lambda e, s: e.matmul(
            row_ps[:, 0:64], vmacc[:, :], cst["c_ident"][0:64, 0:64],
            start=True, stop=True), waits=[(sv, p.count(sv))], inc=(st, 1))
        p.ins("vector", lambda e, s: e.tensor_scalar_mul(
            vmr[:], row_ps[0:1, 0:64], 1.0 / T), waits=[(st, p.count(st))],
            inc=(sv, 1))
        p.ins("tensor", lambda e, s: e.matmul(
            bc_ps[:, :], vmr[:, :], onesf[:, :], start=True, stop=True),
            waits=[(sv, p.count(sv))], inc=(st, 1))
        p.ins("vector", lambda e, s: e.tensor_copy(vm_sb[:, :], bc_ps[:, :]),
              waits=[(st, p.count(st))], inc=(sv, 1))

    def norms(h):
        rows_done = p.count(sv)
        for which, src_t, dst_r in (("q", Qa, q2r), ("k", Ka, k2r)):
            for g in range(nseg):
                sl = slice(g * TSEG, (g + 1) * TSEG)
                b = g % 2
                w = [(sv, rows_done)] if (which == "q" and g == 0) else []
                if (b in sq_free):
                    w = w + [(st, sq_free[b])]
                p.ins("scalar", lambda e, s, sl=sl, b=b, src_t=src_t: e.activation(
                    sqs[b][:, :], src_t[0:64, sl], AF.Square), waits=w, inc=(ss, 1))
                p.ins("tensor", lambda e, s, b=b: e.matmul(
                    row_ps[:, :], cst["c_ones_col"][0:64, :], sqs[b][:, :],
                    start=True, stop=True),
                    waits=[(ss, p.count(ss)), (sv, p.count(sv))], inc=(st, 1))
                sq_free[b] = p.count(st)
                p.ins("vector", lambda e, s, sl=sl, dst_r=dst_r: e.tensor_copy(
                    dst_r[:, sl], row_ps[:, :]), waits=[(st, p.count(st))],
                    inc=(sv, 1))
        p.ins("scalar", lambda e, s: e.activation(q2r[:], q2r[:], AF.Sqrt),
              waits=[(sv, p.count(sv))], inc=(ss, 1))
        p.ins("scalar", lambda e, s: e.activation(k2r[:], k2r[:], AF.Sqrt),
              inc=(ss, 1))
        p.ins("vector", lambda e, s: e.reduce_max(kmax[:], k2r[:], axis=AX.X),
              waits=[(ss, p.count(ss))], inc=(sv, 1))
        p.ins("vector", lambda e, s: e.tensor_scalar(
            Qa[64:65, :], q2r[:], kmax[0:1, 0:1], -1.0, op0=ALU.mult, op1=ALU.mult),
            waits=[(sv, p.count(sv))], inc=(sv, 1))
        p.ins("vector", lambda e, s: e.memset(Ka[64:65, :], 1.0), inc=(sv, 1))

    def sel_scores(h):
        prep = p.count(sv)
        for c in range(ntc):
            w = [(sv, prep)] if c == 0 else [(sv, p.count(sv))]
            p.ins("tensor", lambda e, s, c=c: e.transpose(
                vt_ps[:, 0:64], Ka[0:64, 128 * c:128 * (c + 1)],
                ident_bf[0:64, 0:64]), waits=w, inc=(st, 1))
            p.ins("vector", lambda e, s: e.tensor_copy(ktok[:, :], vt_ps[:, 0:64]),
                  waits=[(st, p.count(st))], inc=(sv, 1))
            p.ins("tensor", lambda e, s, c=c: e.matmul(
                pr_ps[0][:, 0:U], ktok[:, :], SELb[:, U * c:U * (c + 1)],
                start=(c == 0), stop=(c == ntc - 1)),
                waits=[(sv, p.count(sv))], inc=(st, 1) if c == ntc - 1 else None)
        p.ins("vector", lambda e, s: e.tensor_copy(ksmp[:, :], pr_ps[0][:, 0:U]),
              waits=[(st, p.count(st))], inc=(sv, 1))
        ks_done = p.count(sv)
        for c in range(ntc):
            w = [(sv, ks_done)] if c == 0 else [(sv, p.count(sv))]
            p.ins("tensor", lambda e, s, c=c: e.matmul(
                S_ps[0][:, 0:U], Qa[0:64, 128 * c:128 * (c + 1)], ksmp[:, :],
                start=True, stop=True), waits=w, inc=(st, 1))
            stc = p.count(st)
            p.ins("vector", lambda e, s: e.tensor_reduce(
                sc1[:, :], S_ps[0][:, 0:U], axis=AX.X, op=ALU.max),
                waits=[(st, stc)], inc=(sv, 1))
            p.ins("vector", lambda e, s: e.tensor_reduce(
                sc2[:, :], S_ps[0][:, 0:U], axis=AX.X, op=ALU.add),
                waits=[(sv, p.count(sv))], inc=(sv, 1))
            p.ins("vector", lambda e, s, c=c: e.scalar_tensor_tensor(
                Mall[:, c:c + 1], sc2[:, :], -1.0 / U, sc1[:, :],
                op0=ALU.mult, op1=ALU.add), waits=[(sv, p.count(sv))], inc=(sv, 1))
        step = _emit_small_chain(p, sv)
        step(lambda e, s: e.tensor_reduce(sc1[:, :], Mall[:, :], axis=AX.X,
                                          op=ALU.max))
        step(lambda e, s: e.tensor_scalar_mul(bmsk[:, :], Mall[:, :], -1.0))
        step(lambda e, s: e.tensor_reduce(sc2[:, :], bmsk[:, :], axis=AX.X,
                                          op=ALU.max))
        p.ins("tensor", lambda e, s: e.matmul(
            S_ps[0][0:1, 0:128], sc1[:, :], cst["c_ident"][:, :],
            start=True, stop=True), waits=[(sv, p.count(sv))], inc=(st, 1))
        p.ins("tensor", lambda e, s: e.matmul(
            S_ps[1][0:1, 0:128], sc2[:, :], cst["c_ident"][:, :],
            start=True, stop=True), inc=(st, 1))
        stc = p.count(st)
        step(lambda e, s: e.reduce_max(bhi[:, :], S_ps[0][0:1, 0:128], axis=AX.X),
             [(st, stc)])
        step(lambda e, s: e.reduce_max(blo[:, :], S_ps[1][0:1, 0:128], axis=AX.X))
        step(lambda e, s: e.tensor_scalar(blo[:, :], blo[:, :], -1.0, None,
                                          op0=ALU.mult))
        step(lambda e, s: e.tensor_scalar(bhi[:, :], bhi[:, :], 1.0, None,
                                          op0=ALU.add))
        for it in range(NITER):
            step(lambda e, s: e.tensor_add(bmid[:, :], blo[:, :], bhi[:, :]))
            step(lambda e, s: e.tensor_scalar_mul(bmid[:, :], bmid[:, :], 0.5))
            p.ins("tensor", lambda e, s: e.matmul(
                S_ps[0][:, 0:1], cst["c_ones_row"][0:1, :], bmid[:, :],
                start=True, stop=True), waits=[(sv, p.count(sv))], inc=(st, 1))
            stc = p.count(st)
            step(lambda e, s: e.tensor_scalar(
                bmsk[:, :], Mall[:, :], S_ps[0][:, 0:1], None, op0=ALU.is_ge),
                [(st, stc)])
            step(lambda e, s: e.tensor_reduce(bcnt[:, :], bmsk[:, :], axis=AX.X,
                                              op=ALU.add))
            p.ins("tensor", lambda e, s: e.matmul(
                row_ps[:, 0:1], cst["c_ones_col"][:, 0:1], bcnt[:, :],
                start=True, stop=True), waits=[(sv, p.count(sv))], inc=(st, 1))
            stc = p.count(st)
            step(lambda e, s: e.tensor_scalar(
                bcs[:, :], row_ps[0:1, 0:1], float(U) - 0.5, None, op0=ALU.is_ge),
                [(st, stc)])
            step(lambda e, s: e.tensor_sub(bt1[:, :], bmid[:, :], blo[:, :]))
            step(lambda e, s: e.tensor_mul(bt1[:, :], bt1[:, :], bcs[:, :]))
            step(lambda e, s: e.tensor_add(blo[:, :], blo[:, :], bt1[:, :]))
            step(lambda e, s: e.tensor_sub(bt2[:, :], bmid[:, :], bhi[:, :]))
            step(lambda e, s: e.tensor_mul(bt1[:, :], bt2[:, :], bcs[:, :]))
            step(lambda e, s: e.tensor_sub(bt2[:, :], bt2[:, :], bt1[:, :]))
            step(lambda e, s: e.tensor_add(bhi[:, :], bhi[:, :], bt2[:, :]))
        p.ins("tensor", lambda e, s: e.matmul(
            S_ps[0][:, 0:1], cst["c_ones_row"][0:1, :], blo[:, :],
            start=True, stop=True), waits=[(sv, p.count(sv))], inc=(st, 1))
        stc = p.count(st)
        step(lambda e, s: e.tensor_scalar(
            bmsk[:, :], Mall[:, :], S_ps[0][:, 0:1], None, op0=ALU.is_ge),
            [(st, stc)])
        for c in range(ntc):
            p.ins("tensor", lambda e, s, c=c: e.matmul(
                row_ps[:, 0:128], bmsk[:, c:c + 1], cst["c_ident"][:, :],
                start=True, stop=True), waits=[(sv, p.count(sv))], inc=(st, 1))
            p.ins("vector", lambda e, s, c=c: e.tensor_copy(
                mrow[:, 128 * c:128 * (c + 1)], row_ps[0:1, 0:128]),
                waits=[(st, p.count(st))], inc=(sv, 1))

    def attend(h):
        prep_done = p.count(sv)
        blend_done = 0
        for qi in range(nseg):
            qsl = slice(qi * TSEG, (qi + 1) * TSEG)
            for ki in range(ntc):
                b = ki % 2
                ksl = slice(ki * 128, (ki + 1) * 128)
                w = [(ss, S_free.get(b, 0))]
                if qi == 0 and ki == 0:
                    w += [(sv, prep_done)]
                p.ins("tensor", lambda e, s, ksl=ksl, qsl=qsl, b=b: e.matmul(
                    S_ps[b][:, :], Ka[0:65, ksl], Qa[0:65, qsl],
                    start=True, stop=True), waits=w, inc=(st, 1))
                w2 = [(st, p.count(st)), (st, P_free.get(b, 0))]
                p.ins("scalar", lambda e, s, b=b: e.activation(
                    P_b[b][:], S_ps[b][:, :], AF.Exp, scale=0.125),
                    waits=w2, inc=(ss, 1))
                S_free[b] = p.count(ss)
                avw = [(ss, p.count(ss))]
                if ki == 0:
                    avw += [(sv, blend_done)]
                p.ins("tensor", lambda e, s, ki=ki, b=b: e.matmul(
                    acc_ps[:, :],
                    Va[:].rearrange("p (c w) -> p c w", w=72)[:, ki, 0:65],
                    P_b[b][:], start=(ki == 0), stop=(ki == ntc - 1)),
                    waits=avw, inc=(st, 1) if ki == ntc - 1 else None)
                P_free[b] = p.count(st)
            p.ins("vector", lambda e, s: e.tensor_copy(drow[:], acc_ps[64:65, :]),
                  waits=[(st, p.count(st))], inc=(sv, 1))
            p.ins("vector", lambda e, s: e.reciprocal(drec[:], drow[:]),
                  waits=[(sv, p.count(sv))], inc=(sv, 1))
            p.ins("tensor", lambda e, s: e.matmul(
                bc_ps[:, :], cst["c_ones_row"][0:1, 0:64], drec[:, :],
                start=True, stop=True), waits=[(sv, p.count(sv))], inc=(st, 1))
            p.ins("scalar", lambda e, s: e.activation(
                outb[0:64, :], acc_ps[0:64, :], AF.Copy),
                waits=[(st, p.count(st))], inc=(ss, 1))
            p.ins("vector", lambda e, s, qsl=qsl: e.tensor_mul(
                arow[:, :], outb[0:64, :], bc_ps[:, :]),
                waits=[(ss, p.count(ss))], inc=(sv, 1))
            p.ins("tensor", lambda e, s, qsl=qsl: e.matmul(
                bc_ps[:, :], ones_seg[0:1, 0:64], mrow[:, qsl],
                start=True, stop=True), waits=[(sv, p.count(sv))], inc=(st, 1))
            stc = p.count(st)
            p.ins("vector", lambda e, s: e.tensor_sub(
                arow[:, :], arow[:, :], vm_sb[:, :]), waits=[(st, stc)],
                inc=(sv, 1))
            p.ins("vector", lambda e, s: e.tensor_mul(
                arow[:, :], arow[:, :], bc_ps[:, :]),
                waits=[(sv, p.count(sv))], inc=(sv, 1))
            p.ins("vector", lambda e, s, h=h, qsl=qsl: e.tensor_add(
                ctxT[h // 2][64 * (h % 2):64 * (h % 2) + 64, qsl],
                arow[:, :], vm_sb[:, :]), waits=[(sv, p.count(sv))], inc=(sv, 1))
            blend_done = p.count(sv)

    for h in range(NH):
        proj_qk(h, wq_b, bq_b, Qa)
        proj_qk(h, wk_b, bk_b, Ka)
        proj_v(h)
        norms(h)
        sel_scores(h)
        attend(h)

    ps_sc.close()
    hd_sc.close()
    attn_all_done = p.count(sv)
    wo_b = ctx.sb("a3_wo", [128, (DH // 128) * D], BF16)
    boC = ctx.sb("a3_boC", [128, 8], F32)
    p.ins("gpsimd", lambda e, s: e.dma_start(
        out=wo_b[:].rearrange("p (k d) -> p k d", k=DH // 128),
        in_=waps["woR"].rearrange("(k p) d -> p k d", p=128)),
        waits=[("a3_v", attn_all_done)], inc=(wl, 16))
    p.ins("sync", lambda e, s: e.dma_start(out=boC[:], in_=waps["boC"]),
          waits=[("a3_v", attn_all_done)], inc=(wl, 16))
    wo_rdy = p.count(wl)
    y_ps = [ctx.ps(f"a3_y{i}", [128, TSEG]) for i in range(2)]
    so = p.sem("a3_o")
    ctx_done = p.count(sv)
    ob = [outb, outc]
    y_free2 = {}
    o_free = {}
    for g in range(nseg):
        sl = slice(g * TSEG, (g + 1) * TSEG)
        for dc in range(8):
            i = g * 8 + dc
            bk_ = i % 2
            for kc in range(DH // 128):
                w = []
                if kc == 0:
                    w = [(sv, y_free2.get(bk_, 0))]
                    if i == 0:
                        w += [(wl, wo_rdy), (sv, ctx_done)]
                p.ins("tensor", lambda e, s, kc=kc, dc=dc, sl=sl, bk_=bk_: e.matmul(
                    y_ps[bk_][:, :], wo_b[:, D * kc + 128 * dc:D * kc + 128 * (dc + 1)],
                    ctxT[kc][:, sl], start=(kc == 0), stop=(kc == DH // 128 - 1)),
                    waits=w, inc=(st, 1) if kc == DH // 128 - 1 else None)
            p.ins("vector", lambda e, s, bk_=bk_, dc=dc: e.tensor_scalar(
                ob[bk_][:], y_ps[bk_][:, :], boC[:, dc:dc + 1], None, op0=ALU.add),
                waits=[(st, p.count(st)), (so, o_free.get(bk_, 0))], inc=(sv, 1))
            y_free2[bk_] = p.count(sv)
            p.ins("gpsimd", lambda e, s, dc=dc, sl=sl, bk_=bk_: e.dma_start(
                out=yT_ap[128 * dc:128 * (dc + 1), sl], in_=ob[bk_][:]),
                waits=[(sv, p.count(sv))], inc=(so, 16))
            o_free[bk_] = p.count(so)
    p.barrier("gpsimd", [(so, p.count(so))])


def build_l3attn_launch(NH=8):
    nc = bass.Bass("TRN2", target_bir_lowering=False)
    T = ATT_T
    DH = 64 * NH
    xT = nc.dram_tensor("xT", [D, T], F32, kind="ExternalInput").ap()
    waps = {
        "wqT": nc.dram_tensor("wqT", [D, DH], F32, kind="ExternalInput").ap(),
        "bq": nc.dram_tensor("bq", [1, DH], F32, kind="ExternalInput").ap(),
        "wkT": nc.dram_tensor("wkT", [D, DH], F32, kind="ExternalInput").ap(),
        "bk": nc.dram_tensor("bk", [1, DH], F32, kind="ExternalInput").ap(),
        "wvT": nc.dram_tensor("wvT", [D, DH], F32, kind="ExternalInput").ap(),
        "bv": nc.dram_tensor("bv", [1, DH], F32, kind="ExternalInput").ap(),
        "woR": nc.dram_tensor("woR", [DH, D], F32, kind="ExternalInput").ap(),
        "idx": nc.dram_tensor("idx", [1, 50], I32, kind="ExternalInput").ap(),
        "boC": nc.dram_tensor("boC", [128, 8], F32, kind="ExternalInput").ap(),
    }
    yT = nc.dram_tensor("yT", [D, T], F32, kind="ExternalOutput").ap()
    prog = Prog(nc)
    with ExitStack() as stack:
        ctx = Ctx(nc, prog, stack, T)
        emit_l3attn(ctx, xT, waps, yT, NH)
        block = stack.enter_context(nc.Block())
        prog.run(block, stack)
    return nc, ctx


# version 04 (final)
# speedup vs baseline: 1.0726x; 1.0726x over previous
import math
from contextlib import ExitStack

import numpy as np
import concourse.bass as bass
import concourse.mybir as mybir
from concourse.bass import IndirectOffsetOnAxis
from concourse.bass_utils import run_bass_kernel_spmd

F32 = mybir.dt.float32
BF16 = mybir.dt.bfloat16
F16 = mybir.dt.float16
I32 = mybir.dt.int32
U32 = mybir.dt.uint32
AF = mybir.ActivationFunctionType
ALU = mybir.AluOpType
AX = mybir.AxisListType

NCORES = 8
TPC = 2048
TSEG = 512
D = 1024
ATT_T = 4096
PI = float(np.pi)

ENGS = ("sync", "vector", "scalar", "tensor", "gpsimd")


class Prog:
    def __init__(self, nc):
        self.nc = nc
        self.ops = {e: [] for e in ENGS}
        self.sem_counts = {}

    def sem(self, name):
        if name not in self.sem_counts:
            self.sem_counts[name] = 0
        return name

    def count(self, name):
        return self.sem_counts[self.sem(name)]

    def bump(self, name, n):
        self.sem_counts[self.sem(name)] += n
        return self.sem_counts[name]

    def ins(self, eng, fn, waits=(), inc=None):
        if inc is not None:
            self.bump(inc[0], inc[1])
        self.ops[eng].append((fn, [(self.sem(s), v) for s, v in waits], inc))

    def barrier(self, eng, waits, sem=None):
        self.ins(eng, lambda e, s: e.engine_nop(), waits=waits,
                 inc=(sem, 1) if sem else None)

    def run(self, block, stack):
        handles = {n: stack.enter_context(self.nc.semaphore(n))
                   for n in self.sem_counts}

        def mk(name):
            def body(eng):
                for fn, waits, inc in self.ops[name]:
                    for (s, v) in waits:
                        if v > 0:
                            eng.wait_ge(handles[s], v)
                    ins = fn(eng, handles)
                    if inc is not None:
                        s, n = inc
                        assert ins is not None
                        ins.then_inc(handles[s], n)
            return body

        for e in ENGS:
            if self.ops[e]:
                getattr(block, e)(mk(e))



_CONSTS = None


def host_constants():
    global _CONSTS
    if _CONSTS is None:
        c = {}
        c["c_ones_col"] = np.ones((128, 1), np.float32)
        c["c_ones_row"] = np.ones((1, 128), np.float32)
        c["c_iota_col"] = np.arange(128, dtype=np.float32).reshape(128, 1)
        c["c_ident"] = np.eye(128, dtype=np.float32)
        T = ATT_T
        pos = np.arange(T, dtype=np.float32)
        inv = np.exp(np.arange(0, 64, 2, dtype=np.float32) * (-np.log(10000.0) / 64))
        ang = pos[None, :] * inv[:, None]
        c["c_rope_sin"] = np.sin(ang).astype(np.float16).astype(np.float32)
        c["c_rope_cos"] = np.cos(ang).astype(np.float16).astype(np.float32)
        c["c_iota_T"] = pos.reshape(1, T)
        ii = np.arange(128)[:, None]
        xx = np.arange(1024)[None, :]
        c["c_causal_big"] = np.where(ii - xx + 511 >= 0, 0.0, -4e30).astype(np.float32)
        for ne in (8, 16, 32):
            m = np.zeros((ne, ne * 128), np.float32)
            for e in range(ne):
                m[e, 128 * e:128 * (e + 1)] = 1.0
            c[f"c_sel{ne}"] = m
        c["c_lt128"] = (np.arange(128)[:, None] <
                        np.arange(128)[None, :]).astype(np.float32)
        _CONSTS = c
    return _CONSTS


class _Scope:

    def __init__(self, ctx):
        self.ctx = ctx
        self.stack = ExitStack()

    def __enter__(self):
        return self

    def __exit__(self, *a):
        return self.stack.__exit__(*a)

    def sb(self, name, shape, dtype):
        self.ctx.uid += 1
        return self.stack.enter_context(
            self.ctx.nc.sbuf_tensor(f"{name}_{self.ctx.uid}", shape, dtype))

    def ps(self, name, shape, dtype=F32):
        self.ctx.uid += 1
        return self.stack.enter_context(
            self.ctx.nc.psum_tensor(f"{name}_{self.ctx.uid}", shape, dtype))

    def close(self):
        self.stack.close()


class Ctx:
    def __init__(self, nc, prog, stack, T=TPC):
        self.nc = nc
        self.prog = prog
        self.stack = stack
        self.T = T
        self.uid = 0
        self.cst = {}
        self.cin = {}

    def sb(self, name, shape, dtype):
        self.uid += 1
        return self.stack.enter_context(
            self.nc.sbuf_tensor(f"{name}_{self.uid}", shape, dtype))

    def ps(self, name, shape, dtype=F32):
        self.uid += 1
        return self.stack.enter_context(
            self.nc.psum_tensor(f"{name}_{self.uid}", shape, dtype))

    def scope(self):
        s = _Scope(self)
        s.stack.__enter__()
        return s

    def load_const_after(self, name, waits, dtype=F32):
        hc = host_constants()
        bf = name.endswith("@bf")
        base = name[:-3] if bf else name
        arr = hc[base]
        if base not in self.cin:
            self.cin[base] = self.nc.dram_tensor(
                base, list(arr.shape), F32, kind="ExternalInput").ap()
        ap = self.cin[base]
        t = self.sb(name.replace("@", "_"), list(arr.shape), BF16 if bf else dtype)
        self.cst[name] = t
        self.prog.ins("gpsimd", lambda e, s, t=t, ap=ap: e.dma_start(
            out=t[:], in_=ap[:, :]), waits=waits, inc=("c_ld", 16))
        return t

    def use_consts(self, names):
        hc = host_constants()
        p = self.prog
        for n in names:
            if n in self.cst:
                continue
            bf = n.endswith("@bf")
            base = n[:-3] if bf else n
            arr = hc[base]
            if base not in self.cin:
                self.cin[base] = self.nc.dram_tensor(
                    base, list(arr.shape), F32, kind="ExternalInput").ap()
            ap = self.cin[base]
            t = self.sb(n.replace("@", "_"), list(arr.shape), BF16 if bf else F32)
            self.cst[n] = t
            eng = "gpsimd" if bf else "sync"
            p.ins(eng, lambda e, s, t=t, ap=ap: e.dma_start(out=t[:], in_=ap[:, :]),
                  inc=("c_ld", 16))
        return self.cst


def const_inmap(ctx):
    hc = host_constants()
    return {n: hc[n] for n in ctx.cin}




def _tt_b(e, out_ap, in_ap, small_ap, op):
    a, b = bass.broadcast_tensor_aps(in_ap, small_ap)
    return e.tensor_tensor(out_ap, a, b, op)

def emit_load_x(ctx, xT_ap, partials=None):
    p, T = ctx.prog, ctx.T
    xf = [ctx.sb(f"xf{k}", [128, T], F32) for k in range(8)]
    ld = p.sem("x_ld")
    for k in range(8):
        p.ins("sync", lambda e, s, k=k: e.dma_start(
            out=xf[k][:], in_=xT_ap[128 * k:128 * (k + 1), :]), inc=(ld, 16))
    if partials is None:
        return xf, [(ld, p.count(ld)), ("c_ld", p.count("c_ld"))]
    rv = p.sem("x_rdy")
    np_ = len(partials)
    psc = ctx.scope()
    pt = [psc.sb(f"part{i}", [128, T], F32) for i in range(np_)]
    pl = p.sem("p_ld")
    for k in range(8):
        for i, pap in enumerate(partials):
            w = [(rv, np_ * k)] if (k > 0 and i == 0) else []
            p.ins("sync", lambda e, s, k=k, i=i, pap=pap: e.dma_start(
                out=pt[i][:], in_=pap[128 * k:128 * (k + 1), :]), waits=w, inc=(pl, 16))
        for i in range(np_):
            w = [(pl, 16 * np_ * (k + 1)), (ld, 16 * (k + 1))] if i == 0 else []
            p.ins("vector", lambda e, s, k=k, i=i: e.tensor_add(
                xf[k][:], xf[k][:], pt[i][:]), waits=w, inc=(rv, 1))
    psc.close()
    return xf, [(rv, p.count(rv)), ("c_ld", p.count("c_ld"))]


def emit_ln(ctx, xf, x_ln, tag, ready):
    p, T = ctx.prog, ctx.T
    cst = ctx.cst
    nseg = T // TSEG
    keep = ctx.__dict__.get("keep_ln_stats")
    if keep:
        mu = ctx.sb(f"{tag}_mu", [1, T], F32)
        rstd = ctx.sb(f"{tag}_rstd", [1, T], F32)
    sc = ctx.scope()
    sq = sc.sb(f"{tag}_sq", [128, TSEG], F32)
    stat_q = sc.sb(f"{tag}_q", [1, T], F32)
    if not keep:
        mu = sc.sb(f"{tag}_mu", [1, T], F32)
        rstd = sc.sb(f"{tag}_rstd", [1, T], F32)
    ps_s = sc.ps(f"{tag}_pss", [1, TSEG])
    ps_q = sc.ps(f"{tag}_psq", [1, TSEG])
    ps_b = sc.ps(f"{tag}_psb", [128, TSEG])
    ps_b2 = sc.ps(f"{tag}_psb2", [128, TSEG])
    sv = p.sem(f"{tag}_v")
    ss = p.sem(f"{tag}_s")
    st = p.sem(f"{tag}_t")

    sq2b = sc.sb(f"{tag}_sq2b", [128, TSEG], F32)
    for g in range(nseg):
        sl = slice(g * TSEG, (g + 1) * TSEG)
        for k in range(8):
            b = sq if k % 2 == 0 else sq2b
            w = [(st, max(0, p.count(st) - 1))] if not (g == 0 and k <= 1) else (
                list(ready) if g == 0 else [(sv, p.count(sv))])
            p.ins("scalar", lambda e, s, k=k, sl=sl, b=b: e.activation(
                b[:], xf[k][:, sl], AF.Square), waits=w, inc=(ss, 1))
            w2 = []
            if g == 0 and k == 0:
                w2 = list(ready)
            elif k == 0:
                w2 = [(sv, p.count(sv))]
            p.ins("tensor", lambda e, s, k=k, sl=sl: e.matmul(
                ps_s[:, :], cst["c_ones_col"][:, :], xf[k][:, sl],
                start=(k == 0), stop=(k == 7)), waits=w2)
            p.ins("tensor", lambda e, s, k=k, b=b: e.matmul(
                ps_q[:, :], cst["c_ones_col"][:, :], b[:, :],
                start=(k == 0), stop=(k == 7)),
                waits=[(ss, p.count(ss))], inc=(st, 1))
        p.ins("vector", lambda e, s, sl=sl: e.tensor_scalar_mul(
            mu[:, sl], ps_s[:, :], 1.0 / D), waits=[(st, p.count(st))])
        p.ins("vector", lambda e, s, sl=sl: e.tensor_scalar_mul(
            stat_q[:, sl], ps_q[:, :], 1.0 / D), inc=(sv, 1))
    p.ins("vector", lambda e, s: e.tensor_mul(rstd[:], mu[:], mu[:]),
          waits=[(sv, p.count(sv))])
    p.ins("vector", lambda e, s: e.tensor_sub(stat_q[:], stat_q[:], rstd[:]))
    p.ins("vector", lambda e, s: e.tensor_scalar_add(stat_q[:], stat_q[:], 1e-5),
          inc=(sv, 1))
    p.ins("scalar", lambda e, s: e.activation(stat_q[:], stat_q[:], AF.Sqrt),
          waits=[(sv, p.count(sv))], inc=(ss, 1))
    p.ins("vector", lambda e, s: e.reciprocal(rstd[:], stat_q[:]),
          waits=[(ss, p.count(ss))], inc=(sv, 1))
    p.ins("vector", lambda e, s: e.engine_nop(), waits=[(sv, p.count(sv))])

    sq2 = sc.sb(f"{tag}_sq2", [128, TSEG], F32)
    for g in range(nseg):
        sl = slice(g * TSEG, (g + 1) * TSEG)
        w = [(sv, p.count(sv))]
        p.ins("tensor", lambda e, s, sl=sl: e.matmul(
            ps_b[:, :], cst["c_ones_row"][:, :], mu[:, sl], start=True, stop=True),
            waits=w)
        p.ins("tensor", lambda e, s, sl=sl: e.matmul(
            ps_b2[:, :], cst["c_ones_row"][:, :], rstd[:, sl], start=True, stop=True),
            inc=(st, 1))
        for k in range(8):
            w = [(st, p.count(st))] if k == 0 else []
            p.ins("vector", lambda e, s, k=k, sl=sl: e.tensor_sub(
                sq2[:, :], xf[k][:, sl], ps_b[:, :]), waits=w)
            p.ins("vector", lambda e, s, k=k, sl=sl: e.tensor_mul(
                x_ln[k][:, sl], sq2[:, :], ps_b2[:, :]),
                inc=(sv, 1) if k == 7 else None)
    sc.close()
    return [(sv, p.count(sv))], (mu if ctx.__dict__.get("keep_ln_stats") else None), (rstd if ctx.__dict__.get("keep_ln_stats") else None)


def emit_topk_gates(ctx, tag, probs, G_, E_, K, waits):
    p = ctx.prog
    gates = ctx.sb(f"{tag}_gates", [128, G_ * E_], F32)
    thr = ctx.sb(f"{tag}_thr", [128, G_], F32)
    ssum = ctx.sb(f"{tag}_gsum", [128, G_], F32)
    rsum = ctx.sb(f"{tag}_rsum", [128, G_], F32)
    keep = ctx.sb(f"{tag}_keep", [128, G_ * E_], F32)
    a3 = lambda t: t[:].rearrange("p (g e) -> p g e", e=E_)
    sv = p.sem(f"{tag}_v")

    def step(fn, w=()):
        p.ins("vector", fn, waits=list(w) + [(sv, p.count(sv))], inc=(sv, 1))

    step(lambda e, s: e.tensor_copy(keep[:], probs[:]), waits)
    for it in range(K - 1):
        step(lambda e, s: e.tensor_reduce(thr[:], a3(probs), axis=AX.X, op=ALU.max))
        step(lambda e, s: _tt_b(e, a3(gates), a3(probs), thr[:, :, None], ALU.is_ge))
        step(lambda e, s: e.scalar_tensor_tensor(
            probs[:], gates[:], -1e30, probs[:], op0=ALU.mult, op1=ALU.add))
    step(lambda e, s: e.tensor_reduce(thr[:], a3(probs), axis=AX.X, op=ALU.max))
    step(lambda e, s: _tt_b(e, a3(gates), a3(keep), thr[:, :, None], ALU.is_ge))
    step(lambda e, s: e.tensor_mul(gates[:], gates[:], keep[:]))
    step(lambda e, s: e.tensor_reduce(ssum[:], a3(gates), axis=AX.X, op=ALU.add))
    step(lambda e, s: e.reciprocal(rsum[:], ssum[:]))
    step(lambda e, s: _tt_b(e, a3(gates), a3(gates), rsum[:, :, None], ALU.mult))
    return gates, [(sv, p.count(sv))]


def emit_gate_rows(ctx, tag, gates, E_, waits):
    p, T = ctx.prog, ctx.T
    ntc = T // 128
    GT = ctx.sb(f"{tag}_GT", [E_, T], BF16)
    sc_ = ctx.scope()
    pt = sc_.ps(f"{tag}_pt", [E_, 128])
    st = p.sem(f"{tag}_t")
    sv = p.sem(f"{tag}_v")
    for tc in range(ntc):
        w = list(waits) if tc == 0 else [(sv, p.count(sv))]
        p.ins("tensor", lambda e, s, tc=tc: e.transpose(
            pt[:, :], gates[:, E_ * tc:E_ * (tc + 1)], ctx.cst["c_ident"][:, :]),
            waits=w, inc=(st, 1))
        p.ins("vector", lambda e, s, tc=tc: e.tensor_copy(
            GT[:, 128 * tc:128 * (tc + 1)], pt[:, :]),
            waits=[(st, p.count(st))], inc=(sv, 1))
    sc_.close()
    return GT, [(sv, p.count(sv))]


def emit_store_residual(ctx, tag, acc, xT_ap, partials, coef, yT_ap, waits):
    p, T = ctx.prog, ctx.T
    xre = ctx.sb(f"{tag}_xre", [128, T], F32)
    prt = [ctx.sb(f"{tag}_prt{i}", [128, T], F32)
           for i in range(len(partials or ()))]
    srl = p.sem(f"{tag}_rl")
    srv = p.sem(f"{tag}_rv")
    sro = p.sem(f"{tag}_ro")
    nl = 1 + len(partials or ())
    for k in range(8):
        w = list(waits) if k == 0 else [(srv, p.count(srv)), (sro, p.count(sro))]
        p.ins("sync", lambda e, s, k=k: e.dma_start(
            out=xre[:], in_=xT_ap[128 * k:128 * (k + 1), :]), waits=w, inc=(srl, 16))
        for i, pap in enumerate(partials or ()):
            p.ins("sync", lambda e, s, k=k, pap=pap, i=i: e.dma_start(
                out=prt[i][:], in_=pap[128 * k:128 * (k + 1), :]), inc=(srl, 16))
        p.ins("vector", lambda e, s, k=k: e.scalar_tensor_tensor(
            xre[:], acc[k][:], coef, xre[:], op0=ALU.mult, op1=ALU.add),
            waits=[(srl, p.count(srl))])
        for i in range(len(partials or ())):
            p.ins("vector", lambda e, s, k=k, i=i: e.tensor_add(
                xre[:], xre[:], prt[i][:]))
        p.barrier("vector", [], sem=srv)
        p.ins("gpsimd", lambda e, s, k=k: e.dma_start(
            out=yT_ap[128 * k:128 * (k + 1), :], in_=xre[:]),
            waits=[(srv, p.count(srv))], inc=(sro, 16))
    p.barrier("gpsimd", [(sro, p.count(sro))])



def emit_l0ffn(ctx, xT_ap, partials, waps, yT_ap, coef=0.5):
    p, T = ctx.prog, ctx.T
    E, H = 8, 2048
    nseg = T // TSEG
    ntc = T // 128
    nhc = H // 128
    cst = ctx.use_consts(["c_ones_col", "c_ones_row", "c_ident", "c_sel8@bf"])

    xf, ready = emit_load_x(ctx, xT_ap, partials)
    x_ln = [ctx.sb(f"xln{k}", [128, T], BF16) for k in range(8)]
    ln_done, _, _ = emit_ln(ctx, xf, x_ln, "l0ln", ready)

    wg_bf = ctx.sb("l0_wg", [128, 8 * 8], BF16)
    wld = p.sem("l0_wgld")
    p.ins("gpsimd", lambda e, s: e.dma_start(
        out=wg_bf[:].rearrange("p (k e) -> p k e", e=8),
        in_=waps["wg"].rearrange("(k p) e -> p k e", p=128)),
        waits=ln_done, inc=(wld, 16))
    ps_r = ctx.ps("l0_psr", [128, ntc * 8])
    sr = p.sem("l0_r")
    for tc in range(ntc):
        for k in range(8):
            w = ln_done + [(wld, 16)] if (tc == 0 and k == 0) else []
            p.ins("tensor", lambda e, s, tc=tc, k=k: e.matmul(
                ps_r[:, 8 * tc:8 * (tc + 1)],
                x_ln[k][:, 128 * tc:128 * (tc + 1)], wg_bf[:, 8 * k:8 * (k + 1)],
                start=(tc == 0 and k == 0), stop=(tc == ntc - 1 and k == 7)),
                waits=w, inc=(sr, 1) if (tc == ntc - 1 and k == 7) else None)

    r_exp = ctx.sb("l0_rexp", [128, ntc * 8], F32)
    r_m = ctx.sb("l0_rm", [128, ntc], F32)
    p.ins("vector", lambda e, s: e.tensor_reduce(
        r_m[:], ps_r[:].rearrange("p (g e) -> p g e", e=8), axis=AX.X, op=ALU.max,
        negate=True), waits=[(sr, p.count(sr))], inc=(p.sem("l0_rv"), 1))
    p.ins("vector", lambda e, s: _tt_b(
        e, r_exp[:].rearrange("p (g e) -> p g e", e=8),
        ps_r[:].rearrange("p (g e) -> p g e", e=8), r_m[:, :, None],
        ALU.add), waits=[("l0_rv", p.count("l0_rv"))], inc=(p.sem("l0_rv"), 1))
    p.ins("scalar", lambda e, s: e.activation(r_exp[:], r_exp[:], AF.Exp),
          waits=[("l0_rv", p.count("l0_rv"))], inc=(p.sem("l0_rs"), 1))
    gates, g_done = emit_topk_gates(ctx, "l0g", r_exp, ntc, 8, 2,
                                    [("l0_rs", p.count("l0_rs"))])
    GT, gt_done = emit_gate_rows(ctx, "l0gt", gates, 8, g_done)

    b2_bf = ctx.sb("l0_b2bf", [8, D], BF16)
    xsc = ctx.scope()
    w1_buf = xsc.sb("l0_w1", [128, 8 * H], BF16)
    w2_buf = xsc.sb("l0_w2", [128, nhc * D], BF16)
    b1_buf = xsc.sb("l0_b1", [128, nhc], F32)
    h_buf = xsc.sb("l0_h", [128, nhc * TSEG], BF16)
    gb_sb = xsc.sb("l0_gb", [128, TSEG], F32)
    sq_acc = xsc.sb("l0_sqacc", [128, TSEG], F32)
    acc = xf
    h_ps = [ctx.ps(f"l0_hps{i}", [128, TSEG]) for i in range(2)]
    y_ps = [ctx.ps(f"l0_yps{i}", [128, TSEG]) for i in range(2)]
    gb_ps = ctx.ps("l0_gbps", [128, TSEG])
    swl = p.sem("l0_wl")
    she = p.sem("l0_he")
    sht = p.sem("l0_ht")
    syt = p.sem("l0_yt")
    sgv = p.sem("l0_gv")
    sgt = p.sem("l0_gtp")

    def load_weights(eidx):
        p.ins("gpsimd", lambda e, s: e.engine_nop(),
              waits=[(syt, p.count(syt))] + (ln_done if eidx == 0 else []))
        p.ins("gpsimd", lambda e, s, eidx=eidx: e.dma_start(
            out=w1_buf[:].rearrange("p (k h) -> p k h", k=8),
            in_=waps["w1"][eidx].rearrange("(k p) h -> p k h", p=128)), inc=(swl, 16))
        p.ins("gpsimd", lambda e, s, eidx=eidx: e.dma_start(
            out=w2_buf[:].rearrange("p (k d) -> p k d", k=nhc),
            in_=waps["w2"][eidx].rearrange("(k p) d -> p k d", p=128)), inc=(swl, 16))
        p.ins("gpsimd", lambda e, s, eidx=eidx: e.dma_start(
            out=b1_buf[:],
            in_=waps["b1"][eidx]), inc=(swl, 16))

    h_free = {}
    y_free = {}
    gb_free = 0
    for e_i in range(E):
        load_weights(e_i)
        w_ready = p.count(swl)
        for g in range(nseg):
            sl = slice(g * TSEG, (g + 1) * TSEG)
            w = [(sgv, gb_free)]
            if e_i == 0 and g == 0:
                w += gt_done
            p.ins("tensor", lambda e, s, e_i=e_i, sl=sl: e.matmul(
                gb_ps[:, :], cst["c_sel8@bf"][:, 128 * e_i:128 * (e_i + 1)], GT[0:8, sl],
                start=True, stop=True), waits=w, inc=(sgt, 1))
            p.ins("vector", lambda e, s: e.tensor_copy(gb_sb[:], gb_ps[:, :]),
                  waits=[(sgt, p.count(sgt))], inc=(sgv, 1))
            for hc in range(nhc):
                for k in range(8):
                    w = []
                    if g == 0 and hc == 0 and k == 0:
                        w = [(swl, w_ready)]
                    if k == 0 and (hc % 2) in h_free:
                        w = w + [(she, h_free[hc % 2])]
                    p.ins("tensor", lambda e, s, hc=hc, k=k, sl=sl: e.matmul(
                        h_ps[hc % 2][:, :],
                        w1_buf[:, H * k + 128 * hc:H * k + 128 * (hc + 1)],
                        x_ln[k][:, sl], start=(k == 0), stop=(k == 7)),
                        waits=w, inc=(sht, 1) if k == 7 else None)
                w = [(sht, p.count(sht))]
                if hc == 0:
                    w += [(syt, p.count(syt))]
                p.ins("scalar", lambda e, s, hc=hc: e.activation(
                    h_buf[:, TSEG * hc:TSEG * (hc + 1)], h_ps[hc % 2][:, :],
                    AF.Relu, bias=b1_buf[:, hc:hc + 1]),
                    waits=w, inc=(she, 1))
                h_free[hc % 2] = p.count(she)
            for dc in range(8):
                for hc in range(nhc):
                    w = [(she, p.count(she))] if (dc == 0 and hc == 0) else []
                    if hc == 0 and (dc % 2) in y_free:
                        w = w + [(sgv, y_free[dc % 2])]
                    p.ins("tensor", lambda e, s, dc=dc, hc=hc: e.matmul(
                        y_ps[dc % 2][:, :],
                        w2_buf[:, D * hc + 128 * dc:D * hc + 128 * (dc + 1)],
                        h_buf[:, TSEG * hc:TSEG * (hc + 1)],
                        start=(hc == 0), stop=(hc == nhc - 1)),
                        waits=w, inc=(syt, 1) if hc == nhc - 1 else None)
                if e_i == 0:
                    p.ins("vector", lambda e, s, dc=dc, sl=sl: e.tensor_mul(
                        acc[dc][:, sl], y_ps[dc % 2][:, :], gb_sb[:]),
                        waits=[(syt, p.count(syt))], inc=(sgv, 1))
                else:
                    p.ins("vector", lambda e, s, dc=dc: e.tensor_mul(
                        sq_acc[:], y_ps[dc % 2][:, :], gb_sb[:]),
                        waits=[(syt, p.count(syt))])
                    p.ins("vector", lambda e, s, dc=dc, sl=sl: e.tensor_add(
                        acc[dc][:, sl], acc[dc][:, sl], sq_acc[:]), inc=(sgv, 1))
                y_free[dc % 2] = p.count(sgv)
            gb_free = p.count(sgv)

    xsc.close()
    b2_ps = ctx.ps("l0_b2ps", [128, TSEG])
    p.ins("gpsimd", lambda e, s: e.dma_start(out=b2_bf[:], in_=waps["b2"][:, :]),
          waits=ln_done, inc=(p.sem("l0_b2l"), 16))
    sbt = p.sem("l0_bt")
    for g in range(nseg):
        sl = slice(g * TSEG, (g + 1) * TSEG)
        for dc in range(8):
            w = [("l0_b2l", 16)] if (g == 0 and dc == 0) else []
            w = w + [(sgv, p.count(sgv))]
            p.ins("tensor", lambda e, s, dc=dc, sl=sl: e.matmul(
                b2_ps[:, :], b2_bf[:, 128 * dc:128 * (dc + 1)], GT[0:8, sl],
                start=True, stop=True), waits=w, inc=(sbt, 1))
            p.ins("vector", lambda e, s, dc=dc, sl=sl: e.tensor_add(
                acc[dc][:, sl], acc[dc][:, sl], b2_ps[:, :]),
                waits=[(sbt, p.count(sbt))], inc=(sgv, 1))

    emit_store_residual(ctx, "l0res", acc, xT_ap, partials, coef, yT_ap,
                        [(sgv, p.count(sgv))])


def build_l0ffn_launch(with_partials=False):
    nc = bass.Bass("TRN2", target_bir_lowering=False)
    T = TPC
    xT = nc.dram_tensor("xT", [D, T], F32, kind="ExternalInput").ap()
    partials = None
    if with_partials:
        partials = [nc.dram_tensor(f"p{i}", [D, T], F32, kind="ExternalInput").ap()
                    for i in range(2)]
    waps = {
        "wg": nc.dram_tensor("wg", [D, 8], F32, kind="ExternalInput").ap(),
        "w1": nc.dram_tensor("w1", [8, D, 2048], F32, kind="ExternalInput").ap(),
        "b1": nc.dram_tensor("b1", [8, 128, 16], F32, kind="ExternalInput").ap(),
        "w2": nc.dram_tensor("w2", [8, 2048, D], F32, kind="ExternalInput").ap(),
        "b2": nc.dram_tensor("b2", [8, D], F32, kind="ExternalInput").ap(),
    }
    yT = nc.dram_tensor("yT", [D, T], F32, kind="ExternalOutput").ap()
    prog = Prog(nc)
    with ExitStack() as stack:
        ctx = Ctx(nc, prog, stack, T)
        emit_l0ffn(ctx, xT, partials, waps, yT)
        block = stack.enter_context(nc.Block())
        prog.run(block, stack)
    return nc, ctx


_PROGS = {}


def _get_prog(tag, fn):
    if tag not in _PROGS:
        _PROGS[tag] = fn()
    return _PROGS[tag]


def kernel(**inputs):
    from concourse.bass_utils import run_bass_kernel_spmd
    x = np.asarray(inputs["L0_attn_x"], np.float32)
    B, S, Dm = x.shape
    TG = B * S
    TL = TG // 8
    cc = np.ascontiguousarray
    yg = cc(x.reshape(TG, Dm).T)
    zpart = np.zeros((Dm, TL), np.float32)

    def run8(prog, ims):
        nc, ctx = prog
        for im in ims:
            im.update(const_inmap(ctx))
        return run_bass_kernel_spmd(nc, ims, core_ids=list(range(8)))

    def ffn_pass(prog, wmap, yg, parts=None):
        ims = []
        for c in range(8):
            im = {"xT": cc(yg[:, TL * c:TL * (c + 1)])}
            im.update(wmap)
            if parts is None:
                im["p0"] = zpart
                im["p1"] = zpart
            else:
                pa, pb = parts
                b, hf = c // 2, c % 2
                im["p0"] = cc(pa[b][:, TL * hf:TL * hf + TL])
                im["p1"] = cc(pb[b][:, TL * hf:TL * hf + TL])
            ims.append(im)
        res = run8(prog, ims)
        return np.concatenate([res.results[c]["yT"] for c in range(8)], axis=1)

    def attn_pass(prog, wmap_fn, yg):
        ims = []
        for c in range(8):
            b, hf = c // 2, c % 2
            im = {"xT": cc(yg[:, S * b:S * (b + 1)])}
            im.update(wmap_fn(hf))
            ims.append(im)
        res = run8(prog, ims)
        pa = [res.results[2 * b]["yT"] for b in range(B)]
        pb = [res.results[2 * b + 1]["yT"] for b in range(B)]
        return pa, pb

    p_l0f = _get_prog("l0f", lambda: build_l0ffn_launch(True))
    w_l0f = {"wg": cc(inputs["L0_ffn_Wg"]), "w1": cc(inputs["L0_ffn_W1"]),
             "b1": cc(np.asarray(inputs["L0_ffn_b1"]).reshape(8, 16, 128)
                      .transpose(0, 2, 1)),
             "w2": cc(inputs["L0_ffn_W2"]), "b2": cc(inputs["L0_ffn_b2"])}
    yg = ffn_pass(p_l0f, w_l0f, yg)

    p_l0a = _get_prog("l0a", lambda: build_l0attn_launch(8))
    Wq0, Wk0 = np.asarray(inputs["L0_attn_Wq"]), np.asarray(inputs["L0_attn_Wk"])
    Wv0, Wo0 = np.asarray(inputs["L0_attn_Wv"]), np.asarray(inputs["L0_attn_Wo"])
    ph0 = np.asarray(inputs["L0_attn_q_phase_angle"], np.float32)
    ls0 = np.asarray(inputs["L0_attn_alibi_log_slopes"], np.float32)
    perm = np.concatenate([np.arange(0, 64, 2), np.arange(1, 64, 2)])

    def w_l0a(hf):
        heads = list(range(8 * hf, 8 * hf + 8))
        rows = np.concatenate([np.arange(64 * h, 64 * h + 64) for h in heads])
        return {"wqT": cc(np.concatenate([Wq0[:, 64 * h + perm] for h in heads],
                                         axis=1)),
                "wkT": cc(np.concatenate([Wk0[:, 64 * kv + perm]
                                          for kv in (2 * hf, 2 * hf + 1)], axis=1)),
                "wvT": cc(Wv0[:, 128 * hf:128 * hf + 128]),
                "woR": cc(Wo0[rows, :]),
                "phase": cc(ph0[None, 8 * hf:8 * hf + 8]),
                "slopes": cc(np.kron(ls0[8 * hf:8 * hf + 8],
                                     np.r_[1.0, np.zeros(7)]
                                     ).astype(np.float32)[None, :])}
    parts = attn_pass(p_l0a, w_l0a, yg)
    yg = ffn_pass(p_l0f, w_l0f, yg, parts)

    p_l1f = _get_prog("l1f", lambda: build_l1ffn_launch(True))
    w_l1f = {"gate_wT": cc(np.asarray(inputs["L1_ffn_gate_w"]).T),
             "corr_bias": cc(np.tile(np.asarray(inputs["L1_ffn_correction_bias"],
                                                np.float32), (128, 16))),
             "Wg": cc(inputs["L1_ffn_Wg"]), "Wu": cc(inputs["L1_ffn_Wu"]),
             "Wd": cc(inputs["L1_ffn_Wd"]), "Wg_s": cc(inputs["L1_ffn_Wg_s"]),
             "Wu_s": cc(inputs["L1_ffn_Wu_s"]), "Wd_s": cc(inputs["L1_ffn_Wd_s"])}
    yg = ffn_pass(p_l1f, w_l1f, yg)

    p_l1a = _get_prog("l1a", lambda: build_l1attn_launch(8))
    f32 = lambda k: np.asarray(inputs[k], np.float32)
    Wq1, bq1 = f32("L1_attn_Wq"), f32("L1_attn_bq")
    Wk1, bk1 = f32("L1_attn_Wk"), f32("L1_attn_bk")
    Wv1, bv1 = f32("L1_attn_Wv"), f32("L1_attn_bv")
    Wo1, bo1 = f32("L1_attn_Wo"), f32("L1_attn_bo")

    def w_l1a(hf):
        sl = slice(512 * hf, 512 * hf + 512)
        return {"wqT": cc(Wq1[:, sl]), "bq": cc(bq1[None, sl]),
                "wkT": cc(Wk1[:, sl]), "bk": cc(bk1[None, sl]),
                "wvT": cc(Wv1[:, sl]), "bv": cc(bv1[None, sl]),
                "woR": cc(Wo1[sl, :]),
                "boC": (cc(bo1.reshape(8, 128).T.astype(np.float32)) if hf == 0
                        else np.zeros((128, 8), np.float32))}
    parts = attn_pass(p_l1a, w_l1a, yg)
    yg = ffn_pass(p_l1f, w_l1f, yg, parts)

    p_l2c = _get_prog("l2c", lambda: build_l2cnt_launch(True))
    p_l2f = _get_prog("l2f", lambda: build_l2ffn_launch(True))
    w_router = cc(inputs["L2_ffn_w_router"])
    w_l2f = {"w_router": w_router, "w_gate": cc(inputs["L2_ffn_w_gate"]),
             "w_up": cc(inputs["L2_ffn_w_up"]),
             "w_down": cc(inputs["L2_ffn_w_down"])}

    def l2_pass(yg, parts):
        ims = []
        for c in range(8):
            im = {"xT": cc(yg[:, TL * c:TL * (c + 1)]), "w_router": w_router}
            if parts is None:
                im["p0"] = zpart
                im["p1"] = zpart
            else:
                pa, pb = parts
                b, hf = c // 2, c % 2
                im["p0"] = cc(pa[b][:, TL * hf:TL * hf + TL])
                im["p1"] = cc(pb[b][:, TL * hf:TL * hf + TL])
            ims.append(im)
        res = run8(p_l2c, [dict(im) for im in ims])
        counts_all = np.stack([res.results[c]["cnt"][0] for c in range(8)])
        ims2 = []
        for c in range(8):
            im = dict(ims[c])
            im.update(w_l2f)
            im["counts_all"] = cc(counts_all)
            pm = np.zeros((8, 1), np.float32)
            pm[:c] = 1.0
            im["prevmask"] = pm
            ims2.append(im)
        res2 = run8(p_l2f, ims2)
        return np.concatenate([res2.results[c]["yT"] for c in range(8)], axis=1)

    yg = l2_pass(yg, None)

    p_l2a = _get_prog("l2a", lambda: build_l2attn_launch(8))
    wq2, wk2 = np.asarray(inputs["L2_attn_wq"]), np.asarray(inputs["L2_attn_wk"])
    wv2, wo2 = np.asarray(inputs["L2_attn_wv"]), np.asarray(inputs["L2_attn_wo"])
    rf2 = np.asarray(inputs["L2_attn_rf"])

    def w_l2a(hf):
        rows = np.arange(512 * hf, 512 * hf + 512)
        return {"wqT": cc(wq2[rows, :].T), "wkT": cc(wk2[rows, :].T),
                "wvT": cc(wv2[rows, :].T), "woR": cc(wo2.T[rows, :]),
                "rfT": cc(rf2.T)}
    parts = attn_pass(p_l2a, w_l2a, yg)
    yg = l2_pass(yg, parts)

    p_l3f = _get_prog("l3f", lambda: build_l3ffn_launch(True))
    tr = lambda a: cc(np.swapaxes(np.asarray(a), -1, -2))
    w_l3f = {"gate_wT": tr(inputs["L3_ffn_gate_w"]),
             "w1T": tr(inputs["L3_ffn_w1"]), "w3T": tr(inputs["L3_ffn_w3"]),
             "w2T": tr(inputs["L3_ffn_w2"]), "sw1T": tr(inputs["L3_ffn_sw1"]),
             "sw3T": tr(inputs["L3_ffn_sw3"]), "sw2T": tr(inputs["L3_ffn_sw2"])}
    yg = ffn_pass(p_l3f, w_l3f, yg)

    p_l3a = _get_prog("l3a", lambda: build_l3attn_launch(8))
    Wq3, bq3 = np.asarray(inputs["L3_attn_Wq"]), np.asarray(inputs["L3_attn_bq"])
    Wk3, bk3 = np.asarray(inputs["L3_attn_Wk"]), np.asarray(inputs["L3_attn_bk"])
    Wv3, bv3 = np.asarray(inputs["L3_attn_Wv"]), np.asarray(inputs["L3_attn_bv"])
    Wo3, bo3 = np.asarray(inputs["L3_attn_Wo"]), np.asarray(inputs["L3_attn_bo"])
    idx3 = np.asarray(inputs["L3_attn_idx_k"], np.int32)

    def w_l3a(hf):
        sl = slice(512 * hf, 512 * hf + 512)
        return {"wqT": cc(Wq3.T[:, sl]), "bq": cc(bq3[None, sl]),
                "wkT": cc(Wk3.T[:, sl]), "bk": cc(bk3[None, sl]),
                "wvT": cc(Wv3.T[:, sl]), "bv": cc(bv3[None, sl]),
                "woR": cc(Wo3.T[sl, :]),
                "idx": cc(idx3[None, :]),
                "boC": (cc(bo3.reshape(8, 128).T.astype(np.float32)) if hf == 0
                        else np.zeros((128, 8), np.float32))}
    parts = attn_pass(p_l3a, w_l3a, yg)
    yg = ffn_pass(p_l3f, w_l3f, yg, parts)

    return np.ascontiguousarray(yg.T).reshape(B, S, Dm)


def build_ln_debug_launch():
    nc = bass.Bass("TRN2", target_bir_lowering=False)
    T = TPC
    xT = nc.dram_tensor("xT", [D, T], F32, kind="ExternalInput").ap()
    yT = nc.dram_tensor("yT", [D, T], F32, kind="ExternalOutput").ap()
    st_o = nc.dram_tensor("st_o", [2, T], F32, kind="ExternalOutput").ap()
    prog = Prog(nc)
    with ExitStack() as stack:
        ctx = Ctx(nc, prog, stack, T)
        ctx.keep_ln_stats = True
        p = prog
        ctx.use_consts(["c_ones_col", "c_ones_row"])
        xf, ready = emit_load_x(ctx, xT, None)
        x_ln = [ctx.sb(f"xln{k}", [128, T], BF16) for k in range(8)]
        ln_done, mu, rstd = emit_ln(ctx, xf, x_ln, "ln", ready)
        so = p.sem("out")
        p.ins("gpsimd", lambda e, s: e.dma_start(out=st_o[0:1, :], in_=mu[:]),
              waits=ln_done, inc=(so, 16))
        p.ins("gpsimd", lambda e, s: e.dma_start(out=st_o[1:2, :], in_=rstd[:]),
              inc=(so, 16))
        for k in range(8):
            p.ins("gpsimd", lambda e, s, k=k: e.dma_start(
                out=yT[128 * k:128 * (k + 1), :], in_=x_ln[k][:]), inc=(so, 16))
        p.barrier("gpsimd", [(so, p.count(so))])
        block = stack.enter_context(nc.Block())
        prog.run(block, stack)
    return nc, ctx


def build_l0router_debug_launch():
    nc = bass.Bass("TRN2", target_bir_lowering=False)
    T = TPC
    xT = nc.dram_tensor("xT", [D, T], F32, kind="ExternalInput").ap()
    wg_ap = nc.dram_tensor("wg", [D, 8], F32, kind="ExternalInput").ap()
    gt_o = nc.dram_tensor("gt_o", [8, T], F32, kind="ExternalOutput").ap()
    g_tok_o = nc.dram_tensor("g_tok_o", [128, (T // 128) * 8], F32, kind="ExternalOutput").ap()
    dbg_keep = nc.dram_tensor("dbg_keep", [128, (T // 128) * 8], F32, kind="ExternalOutput").ap()
    dbg_thr = nc.dram_tensor("dbg_thr", [128, T // 128], F32, kind="ExternalOutput").ap()
    dbg_ssum = nc.dram_tensor("dbg_ssum", [128, T // 128], F32, kind="ExternalOutput").ap()
    dbg_rsum = nc.dram_tensor("dbg_rsum", [128, T // 128], F32, kind="ExternalOutput").ap()
    prog = Prog(nc)
    with ExitStack() as stack:
        ctx = Ctx(nc, prog, stack, T)
        p = prog
        ntc = T // 128
        cst = ctx.use_consts(["c_ones_col", "c_ones_row", "c_ident"])
        xf, ready = emit_load_x(ctx, xT, None)
        x_ln = [ctx.sb(f"xln{k}", [128, T], BF16) for k in range(8)]
        ln_done, _, _ = emit_ln(ctx, xf, x_ln, "l0ln", ready)
        wg_bf = ctx.sb("l0_wg", [128, 8 * 8], BF16)
        wld = p.sem("l0_wgld")
        p.ins("gpsimd", lambda e, s: e.dma_start(
            out=wg_bf[:].rearrange("p (k e) -> p k e", e=8),
            in_=wg_ap.rearrange("(k p) e -> p k e", p=128)),
            waits=ln_done, inc=(wld, 16))
        ps_r = ctx.ps("l0_psr", [128, ntc * 8])
        sr = p.sem("l0_r")
        for tc in range(ntc):
            for k in range(8):
                w = ln_done + [(wld, 16)] if (tc == 0 and k == 0) else []
                p.ins("tensor", lambda e, s, tc=tc, k=k: e.matmul(
                    ps_r[:, 8 * tc:8 * (tc + 1)],
                    x_ln[k][:, 128 * tc:128 * (tc + 1)], wg_bf[:, 8 * k:8 * (k + 1)],
                    start=(tc == 0 and k == 0), stop=(tc == ntc - 1 and k == 7)),
                    waits=w, inc=(sr, 1) if (tc == ntc - 1 and k == 7) else None)
        r_exp = ctx.sb("l0_rexp", [128, ntc * 8], F32)
        r_m = ctx.sb("l0_rm", [128, ntc], F32)
        p.ins("vector", lambda e, s: e.tensor_reduce(
            r_m[:], ps_r[:].rearrange("p (g e) -> p g e", e=8), axis=AX.X, op=ALU.max,
            negate=True), waits=[(sr, p.count(sr))])
        p.ins("vector", lambda e, s: _tt_b(
            e, r_exp[:].rearrange("p (g e) -> p g e", e=8),
            ps_r[:].rearrange("p (g e) -> p g e", e=8), r_m[:, :, None],
            ALU.add), inc=(p.sem("l0_rv"), 1))
        p.ins("scalar", lambda e, s: e.activation(r_exp[:], r_exp[:], AF.Exp),
              waits=[("l0_rv", p.count("l0_rv"))], inc=(p.sem("l0_rs"), 1))
        gates, g_done = emit_topk_gates(ctx, "l0g", r_exp, ntc, 8, 2,
                                        [("l0_rs", p.count("l0_rs"))])
        GT, gt_done = emit_gate_rows(ctx, "l0gt", gates, 8, g_done)
        so = p.sem("out")
        p.ins("gpsimd", lambda e, s: e.dma_start(out=gt_o[:, :], in_=GT[:]),
              waits=gt_done, inc=(so, 16))
        p.ins("gpsimd", lambda e, s: e.dma_start(out=g_tok_o[:, :], in_=gates[:]),
              inc=(so, 16))
        for nm, ap in (("dbg_keep", dbg_keep), ("dbg_thr", dbg_thr),
                       ("dbg_ssum", dbg_ssum), ("dbg_rsum", dbg_rsum)):
            p.ins("gpsimd", lambda e, s, ap=ap: e.dma_start(
                out=ap[0:1, :], in_=ctx.cst["c_ident"][0:1, 0:ap.shape[1]]), inc=(so, 16))
        p.barrier("gpsimd", [(so, p.count(so))])
        block = stack.enter_context(nc.Block())
        prog.run(block, stack)
    return nc, ctx



def emit_glu_moe(ctx, tag, x_ln, GT, nE, experts, I, acc, waits_gates, sel_const,
                 seed_from=None):
    p, T = ctx.prog, ctx.T
    nseg = T // TSEG
    nic = I // 128
    cst = ctx.cst
    msc = ctx.scope()
    wg_buf = [msc.sb(f"{tag}_wg{i}", [128, 8 * I], BF16) for i in range(2)]
    wu_buf = [msc.sb(f"{tag}_wu{i}", [128, 8 * I], BF16) for i in range(2)]
    wd_buf = msc.sb(f"{tag}_wd", [128, nic * D], BF16)
    s_buf = [msc.sb(f"{tag}_s{i}", [128, TSEG], F32) for i in range(2)]
    act_buf = msc.sb(f"{tag}_act", [128, nic * TSEG], BF16)
    gb_sb = msc.sb(f"{tag}_gb", [128, TSEG], F32)
    g_ps = [ctx.ps(f"{tag}_gps{i}", [128, TSEG]) for i in range(2)]
    u_ps = [ctx.ps(f"{tag}_ups{i}", [128, TSEG]) for i in range(2)]
    y_ps = [ctx.ps(f"{tag}_yps{i}", [128, TSEG]) for i in range(2)]
    gb_ps = ctx.ps(f"{tag}_gbps", [128, TSEG])
    swl = p.sem(f"{tag}_wl")
    sgt = p.sem(f"{tag}_gt")
    sse = p.sem(f"{tag}_se")
    sae = p.sem(f"{tag}_ae")
    syt = p.sem(f"{tag}_yt")
    sav = p.sem(f"{tag}_av")
    sgb = p.sem(f"{tag}_gb")

    first_write = [True] * 8
    if seed_from is not None:
        for k in range(8):
            p.ins("vector", lambda e, s, k=k: e.tensor_copy(
                acc[k][:], seed_from[k][:]),
                waits=(list(waits_gates) if k == 0 else []), inc=(sav, 1))
        first_write = [False] * 8
    g_free = {}
    y_free = {}
    gb_free = 0
    act_free = 0
    wread = {}
    wd_read = 0
    for ei, ex in enumerate(experts):
        bi = ei % 2
        w0 = [(sgt, wread.get(bi, 0))] + (list(waits_gates) if ei == 0 else [])
        p.ins("gpsimd", lambda e, s, ex=ex, bi=bi: e.dma_start(
            out=wg_buf[bi][:].rearrange("p (k i) -> p k i", k=8),
            in_=ex["wg"].rearrange("(k p) i -> p k i", p=128)),
            waits=w0, inc=(swl, 16))
        p.ins("gpsimd", lambda e, s, ex=ex, bi=bi: e.dma_start(
            out=wu_buf[bi][:].rearrange("p (k i) -> p k i", k=8),
            in_=ex["wu"].rearrange("(k p) i -> p k i", p=128)), inc=(swl, 16))
        wgu_ready = p.count(swl)
        p.ins("gpsimd", lambda e, s, ex=ex: e.dma_start(
            out=wd_buf[:].rearrange("p (k d) -> p k d", k=nic),
            in_=ex["wd"].rearrange("(k p) d -> p k d", p=128)),
            waits=[(syt, wd_read)], inc=(swl, 16))
        w_ready = p.count(swl)
        for g in range(nseg):
            sl = slice(g * TSEG, (g + 1) * TSEG)
            gate = ex.get("gate")
            if gate is not None:
                p.ins("tensor", lambda e, s, gate=gate, sl=sl: e.matmul(
                    gb_ps[:, :], cst[sel_const][:, 128 * gate:128 * (gate + 1)],
                    GT[0:nE, sl], start=True, stop=True),
                    waits=[(sav, gb_free)] +
                          (list(waits_gates) if (ei == 0 and g == 0) else []),
                    inc=(sgb, 1))
                p.ins("vector", lambda e, s: e.tensor_copy(gb_sb[:], gb_ps[:, :]),
                      waits=[(sgb, p.count(sgb))], inc=(sav, 1))
            for ic in range(nic):
                for k in range(8):
                    w = [(swl, wgu_ready)] if (g == 0 and ic == 0 and k == 0) else []
                    if k == 0 and (ic % 2) in g_free:
                        w = w + [(sse, g_free[ic % 2])]
                    p.ins("tensor", lambda e, s, ic=ic, k=k, sl=sl, bi=bi: e.matmul(
                        g_ps[ic % 2][:, :],
                        wg_buf[bi][:, I * k + 128 * ic:I * k + 128 * (ic + 1)],
                        x_ln[k][:, sl], start=(k == 0), stop=(k == 7)), waits=w)
                    p.ins("tensor", lambda e, s, ic=ic, k=k, sl=sl, bi=bi: e.matmul(
                        u_ps[ic % 2][:, :],
                        wu_buf[bi][:, I * k + 128 * ic:I * k + 128 * (ic + 1)],
                        x_ln[k][:, sl], start=(k == 0), stop=(k == 7)),
                        inc=(sgt, 1) if k == 7 else None)
                w = [(sgt, p.count(sgt))]
                if (ic % 2) in g_free:
                    pass
                p.ins("scalar", lambda e, s, ic=ic: e.activation(
                    s_buf[ic % 2][:], g_ps[ic % 2][:, :], AF.Silu),
                    waits=w, inc=(sse, 1))
                w2 = [(sse, p.count(sse))]
                if ic == 0:
                    w2 += [(syt, act_free)]
                    if gate is not None:
                        w2 += [(sav, p.count(sav))]
                p.ins("vector", lambda e, s, ic=ic: e.tensor_mul(
                    s_buf[ic % 2][:], s_buf[ic % 2][:], u_ps[ic % 2][:, :]),
                    waits=w2)
                if gate is not None:
                    p.ins("vector", lambda e, s, ic=ic: e.tensor_mul(
                        act_buf[:, TSEG * ic:TSEG * (ic + 1)], s_buf[ic % 2][:],
                        gb_sb[:]), inc=(sae, 1))
                else:
                    p.ins("vector", lambda e, s, ic=ic: e.tensor_copy(
                        act_buf[:, TSEG * ic:TSEG * (ic + 1)], s_buf[ic % 2][:]),
                        inc=(sae, 1))
                g_free[ic % 2] = p.count(sse)
            for dc in range(8):
                for ic in range(nic):
                    w = [(sae, p.count(sae))] if (dc == 0 and ic == 0) else []
                    if g == 0 and dc == 0 and ic == 0:
                        w = w + [(swl, w_ready)]
                    if ic == 0 and (dc % 2) in y_free:
                        w = w + [(sav, y_free[dc % 2])]
                    p.ins("tensor", lambda e, s, dc=dc, ic=ic: e.matmul(
                        y_ps[dc % 2][:, :],
                        wd_buf[:, D * ic + 128 * dc:D * ic + 128 * (dc + 1)],
                        act_buf[:, TSEG * ic:TSEG * (ic + 1)],
                        start=(ic == 0), stop=(ic == nic - 1)),
                        waits=w, inc=(syt, 1) if ic == nic - 1 else None)
                if first_write[dc]:
                    p.ins("vector", lambda e, s, dc=dc, sl=sl: e.tensor_copy(
                        acc[dc][:, sl], y_ps[dc % 2][:, :]),
                        waits=[(syt, p.count(syt))], inc=(sav, 1))
                else:
                    p.ins("vector", lambda e, s, dc=dc, sl=sl: e.tensor_add(
                        acc[dc][:, sl], acc[dc][:, sl], y_ps[dc % 2][:, :]),
                        waits=[(syt, p.count(syt))], inc=(sav, 1))
                y_free[dc % 2] = p.count(sav)
            act_free = p.count(syt)
            gb_free = p.count(sav)
        first_write = [False] * 8
        wread[bi] = p.count(sgt)
        wd_read = p.count(syt)
    msc.close()
    return [(sav, p.count(sav))]


def _emit_small_chain(p, sem):
    def step(fn, w=()):
        p.ins("vector", fn, waits=list(w) + [(sem, p.count(sem))], inc=(sem, 1))
    return step


def emit_l1ffn(ctx, xT_ap, partials, waps, yT_ap, coef=0.5):
    p, T = ctx.prog, ctx.T
    E, I, NG = 8, 512, 4
    ntc = T // 128
    cst = ctx.use_consts(["c_ones_col", "c_ones_row", "c_ident", "c_sel8@bf"])

    xf, ready = emit_load_x(ctx, xT_ap, partials)
    x_ln = [ctx.sb(f"xln{k}", [128, T], BF16) for k in range(8)]
    ln_done, _, _ = emit_ln(ctx, xf, x_ln, "l1ln", ready)

    gwT_bf = ctx.sb("l1_gw", [128, 8 * 8], BF16)
    rsc = ctx.scope()
    wld = p.sem("l1_wgld")
    p.ins("gpsimd", lambda e, s: e.dma_start(
        out=gwT_bf[:].rearrange("p (k e) -> p k e", e=8),
        in_=waps["gate_wT"].rearrange("(k p) e -> p k e", p=128)),
        waits=ln_done, inc=(wld, 16))
    ps_r = rsc.ps("l1_psr", [128, ntc * 8])
    sr = p.sem("l1_r")
    for tc in range(ntc):
        for k in range(8):
            w = ln_done + [(wld, 16)] if (tc == 0 and k == 0) else []
            p.ins("tensor", lambda e, s, tc=tc, k=k: e.matmul(
                ps_r[:, 8 * tc:8 * (tc + 1)],
                x_ln[k][:, 128 * tc:128 * (tc + 1)], gwT_bf[:, 8 * k:8 * (k + 1)],
                start=(tc == 0 and k == 0), stop=(tc == ntc - 1 and k == 7)),
                waits=w, inc=(sr, 1) if (tc == ntc - 1 and k == 7) else None)

    cb_b = ctx.sb("l1_cbb", [128, 128], F32)
    p.ins("sync", lambda e, s: e.dma_start(out=cb_b[:], in_=waps["corr_bias"]),
          waits=ln_done, inc=(p.sem("l1_cbl"), 16))
    p.barrier("vector", [("l1_cbl", 16)], sem=p.sem("l1_cbv"))

    scores = ctx.sb("l1_sc", [128, ntc * 8], F32)
    sc_c = ctx.sb("l1_scc", [128, ntc * 8], F32)
    gsum = ctx.sb("l1_gs", [128, ntc * 4], F32)
    gthr = ctx.sb("l1_gthr", [128, ntc], F32)
    gmask = ctx.sb("l1_gm", [128, ntc * 4], F32)
    sfc = ctx.sb("l1_sfc", [128, ntc * 8], F32)
    sel = ctx.sb("l1_sel", [128, ntc * 8], F32)
    eq = ctx.sb("l1_eq", [128, ntc * 8], F32)
    pre = ctx.sb("l1_pre", [128, ntc * 8], F32)
    pre2 = ctx.sb("l1_pre2", [128, ntc * 8], F32)
    thr1 = ctx.sb("l1_t1", [128, ntc], F32)
    wsum = ctx.sb("l1_ws", [128, ntc], F32)
    rws = ctx.sb("l1_rws", [128, ntc], F32)
    a38 = lambda t: t[:].rearrange("p (g e) -> p g e", e=8)
    a34 = lambda t: t[:].rearrange("p (g e) -> p g e", e=4)
    sv = p.sem("l1_rv")
    step = _emit_small_chain(p, sv)

    p.ins("scalar", lambda e, s: e.activation(scores[:], ps_r[:], AF.Sigmoid),
          waits=[(sr, p.count(sr))], inc=(p.sem("l1_rs"), 1))
    step(lambda e, s: e.tensor_add(sc_c[:], scores[:], cb_b[:]),
         [("l1_rs", 1), ("l1_cbv", 1)])
    pair = lambda t, j: t[:].rearrange("p (x two) -> p x two", two=2)[:, :, j]
    step(lambda e, s: e.tensor_tensor(gsum[:], pair(sc_c, 0), pair(sc_c, 1), ALU.add))
    step(lambda e, s: e.tensor_reduce(gthr[:], a34(gsum), axis=AX.X, op=ALU.max))
    step(lambda e, s: _tt_b(e, a34(gmask), a34(gsum), gthr[:, :, None], ALU.is_ge))
    step(lambda e, s: e.scalar_tensor_tensor(
        gsum[:], gmask[:], -1e30, gsum[:], op0=ALU.mult, op1=ALU.add))
    step(lambda e, s: e.tensor_reduce(gthr[:], a34(gsum), axis=AX.X, op=ALU.max))
    step(lambda e, s: e.tensor_tensor(gsum[:], pair(sc_c, 0), pair(sc_c, 1), ALU.add))
    step(lambda e, s: _tt_b(e, a34(gmask), a34(gsum), gthr[:, :, None], ALU.is_ge))
    step(lambda e, s: e.tensor_copy(pair(sfc, 0), gmask[:]))
    step(lambda e, s: e.tensor_copy(pair(sfc, 1), gmask[:]))
    step(lambda e, s: e.tensor_mul(sfc[:], sfc[:], sc_c[:]))
    step(lambda e, s: e.memset(sel[:], 0.0))
    for it in range(4):
        step(lambda e, s: e.tensor_reduce(thr1[:], a38(sfc), axis=AX.X, op=ALU.max))
        step(lambda e, s: _tt_b(e, a38(eq), a38(sfc), thr1[:, :, None], ALU.is_ge))
        step(lambda e, s: e.tensor_copy(pre[:], eq[:]))
        cur, oth = pre, pre2
        for sh in (1, 2, 4):
            step(lambda e, s, cur=cur, oth=oth, sh=sh: e.tensor_copy(
                a38(oth)[:, :, 0:sh], a38(cur)[:, :, 0:sh]))
            step(lambda e, s, cur=cur, oth=oth, sh=sh: e.tensor_tensor(
                a38(oth)[:, :, sh:8], a38(cur)[:, :, sh:8], a38(cur)[:, :, 0:8 - sh],
                ALU.add))
            cur, oth = oth, cur
        step(lambda e, s, cur=cur: e.tensor_scalar(
            pre[:], cur[:], 1.0, None, op0=ALU.is_le))
        step(lambda e, s: e.tensor_mul(eq[:], eq[:], pre[:]))
        step(lambda e, s: e.tensor_add(sel[:], sel[:], eq[:]))
        step(lambda e, s: e.scalar_tensor_tensor(
            sfc[:], eq[:], -1e30, sfc[:], op0=ALU.mult, op1=ALU.add))
    step(lambda e, s: e.tensor_mul(sel[:], sel[:], scores[:]))
    step(lambda e, s: e.tensor_reduce(wsum[:], a38(sel), axis=AX.X, op=ALU.add))
    step(lambda e, s: e.tensor_scalar_add(wsum[:], wsum[:], 1e-20))
    step(lambda e, s: e.reciprocal(rws[:], wsum[:]))
    step(lambda e, s: _tt_b(e, a38(sel), a38(sel), rws[:, :, None], ALU.mult))
    step(lambda e, s: e.tensor_scalar_mul(sel[:], sel[:], 2.5))
    rsc.close()
    ctx.l1_gates_tok = sel
    ctx.l1_dbg = dict(sc_c=cb_b, gsum=gsum, gmask=gmask, sfc=sfc, scores=scores)
    GT, gt_done = emit_gate_rows(ctx, "l1gt", sel, 8, [(sv, p.count(sv))])
    ctx.l1_GT = GT
    ctx.l1_gt_done = gt_done

    acc = xf
    experts = [dict(wg=waps["Wg"][e_], wu=waps["Wu"][e_], wd=waps["Wd"][e_],
                    gate=e_) for e_ in range(E)]
    experts.append(dict(wg=waps["Wg_s"], wu=waps["Wu_s"], wd=waps["Wd_s"],
                        gate=None))
    moe_done = emit_glu_moe(ctx, "l1x", x_ln, GT, 8, experts, I, acc,
                            gt_done, "c_sel8@bf")
    emit_store_residual(ctx, "l1res", acc, xT_ap, partials, coef, yT_ap, moe_done)


def emit_l3ffn(ctx, xT_ap, partials, waps, yT_ap, coef=0.5):
    p, T = ctx.prog, ctx.T
    E, I = 16, 512
    ntc = T // 128
    cst = ctx.use_consts(["c_ones_col", "c_ones_row", "c_ident", "c_sel16@bf"])

    xf, ready = emit_load_x(ctx, xT_ap, partials)
    x_ln = [ctx.sb(f"xln{k}", [128, T], BF16) for k in range(8)]
    ln_done, _, _ = emit_ln(ctx, xf, x_ln, "l3ln", ready)

    gwT_bf = ctx.sb("l3_gw", [128, 8 * 16], BF16)
    wld = p.sem("l3_wgld")
    p.ins("gpsimd", lambda e, s: e.dma_start(
        out=gwT_bf[:].rearrange("p (k e) -> p k e", e=16),
        in_=waps["gate_wT"].rearrange("(k p) e -> p k e", p=128)),
        waits=ln_done, inc=(wld, 16))
    rsc3 = ctx.scope()
    ps_r = rsc3.ps("l3_psr", [128, ntc * 16])
    sr = p.sem("l3_r")
    for tc in range(ntc):
        for k in range(8):
            w = ln_done + [(wld, 16)] if (tc == 0 and k == 0) else []
            p.ins("tensor", lambda e, s, tc=tc, k=k: e.matmul(
                ps_r[:, 16 * tc:16 * (tc + 1)],
                x_ln[k][:, 128 * tc:128 * (tc + 1)], gwT_bf[:, 16 * k:16 * (k + 1)],
                start=(tc == 0 and k == 0), stop=(tc == ntc - 1 and k == 7)),
                waits=w, inc=(sr, 1) if (tc == ntc - 1 and k == 7) else None)
    r_exp = ctx.sb("l3_rexp", [128, ntc * 16], F32)
    r_m = ctx.sb("l3_rm", [128, ntc], F32)
    p.ins("vector", lambda e, s: e.tensor_reduce(
        r_m[:], ps_r[:].rearrange("p (g e) -> p g e", e=16), axis=AX.X, op=ALU.max,
        negate=True), waits=[(sr, p.count(sr))], inc=(p.sem("l3_rv"), 1))
    p.ins("vector", lambda e, s: _tt_b(
        e, r_exp[:].rearrange("p (g e) -> p g e", e=16),
        ps_r[:].rearrange("p (g e) -> p g e", e=16), r_m[:, :, None],
        ALU.add), waits=[("l3_rv", p.count("l3_rv"))], inc=(p.sem("l3_rv"), 1))
    p.ins("scalar", lambda e, s: e.activation(r_exp[:], r_exp[:], AF.Exp),
          waits=[("l3_rv", p.count("l3_rv"))], inc=(p.sem("l3_rs"), 1))
    gates, g_done = emit_topk_gates(ctx, "l3g", r_exp, ntc, 16, 4,
                                    [("l3_rs", p.count("l3_rs"))])
    rsc3.close()
    GT, gt_done = emit_gate_rows(ctx, "l3gt", gates, 16, g_done)

    acc = xf
    experts = [dict(wg=waps["w1T"][e_], wu=waps["w3T"][e_], wd=waps["w2T"][e_],
                    gate=e_) for e_ in range(E)]
    experts.append(dict(wg=waps["sw1T"], wu=waps["sw3T"], wd=waps["sw2T"],
                        gate=None))
    moe_done = emit_glu_moe(ctx, "l3x", x_ln, GT, 16, experts, I, acc,
                            gt_done, "c_sel16@bf")
    emit_store_residual(ctx, "l3res", acc, xT_ap, partials, coef, yT_ap, moe_done)


def build_l1ffn_launch(with_partials=False):
    nc = bass.Bass("TRN2", target_bir_lowering=False)
    T = TPC
    xT = nc.dram_tensor("xT", [D, T], F32, kind="ExternalInput").ap()
    partials = None
    if with_partials:
        partials = [nc.dram_tensor(f"p{i}", [D, T], F32, kind="ExternalInput").ap()
                    for i in range(2)]
    waps = {
        "gate_wT": nc.dram_tensor("gate_wT", [D, 8], F32, kind="ExternalInput").ap(),
        "corr_bias": nc.dram_tensor("corr_bias", [128, 128], F32, kind="ExternalInput").ap(),
        "Wg": nc.dram_tensor("Wg", [8, D, 512], F32, kind="ExternalInput").ap(),
        "Wu": nc.dram_tensor("Wu", [8, D, 512], F32, kind="ExternalInput").ap(),
        "Wd": nc.dram_tensor("Wd", [8, 512, D], F32, kind="ExternalInput").ap(),
        "Wg_s": nc.dram_tensor("Wg_s", [D, 512], F32, kind="ExternalInput").ap(),
        "Wu_s": nc.dram_tensor("Wu_s", [D, 512], F32, kind="ExternalInput").ap(),
        "Wd_s": nc.dram_tensor("Wd_s", [512, D], F32, kind="ExternalInput").ap(),
    }
    yT = nc.dram_tensor("yT", [D, T], F32, kind="ExternalOutput").ap()
    prog = Prog(nc)
    with ExitStack() as stack:
        ctx = Ctx(nc, prog, stack, T)
        emit_l1ffn(ctx, xT, partials, waps, yT)
        block = stack.enter_context(nc.Block())
        prog.run(block, stack)
    return nc, ctx


def build_l3ffn_launch(with_partials=False):
    nc = bass.Bass("TRN2", target_bir_lowering=False)
    T = TPC
    xT = nc.dram_tensor("xT", [D, T], F32, kind="ExternalInput").ap()
    partials = None
    if with_partials:
        partials = [nc.dram_tensor(f"p{i}", [D, T], F32, kind="ExternalInput").ap()
                    for i in range(2)]
    waps = {
        "gate_wT": nc.dram_tensor("gate_wT", [D, 16], F32, kind="ExternalInput").ap(),
        "w1T": nc.dram_tensor("w1T", [16, D, 512], F32, kind="ExternalInput").ap(),
        "w3T": nc.dram_tensor("w3T", [16, D, 512], F32, kind="ExternalInput").ap(),
        "w2T": nc.dram_tensor("w2T", [16, 512, D], F32, kind="ExternalInput").ap(),
        "sw1T": nc.dram_tensor("sw1T", [D, 512], F32, kind="ExternalInput").ap(),
        "sw3T": nc.dram_tensor("sw3T", [D, 512], F32, kind="ExternalInput").ap(),
        "sw2T": nc.dram_tensor("sw2T", [512, D], F32, kind="ExternalInput").ap(),
    }
    yT = nc.dram_tensor("yT", [D, T], F32, kind="ExternalOutput").ap()
    prog = Prog(nc)
    with ExitStack() as stack:
        ctx = Ctx(nc, prog, stack, T)
        emit_l3ffn(ctx, xT, partials, waps, yT)
        block = stack.enter_context(nc.Block())
        prog.run(block, stack)
    return nc, ctx




def emit_l2router(ctx, x_ln, waps, ln_done, rsc, tag):
    p, T = ctx.prog, ctx.T
    E = 32
    ntc = T // 128
    wr_buf = rsc.sb(f"{tag}_wr", [128, 8 * E], BF16)
    wld = p.sem(f"{tag}_wrld")
    p.ins("gpsimd", lambda e, s: e.dma_start(
        out=wr_buf[:].rearrange("p (k e) -> p k e", e=E),
        in_=waps["w_router"].rearrange("(k p) e -> p k e", p=128)),
        waits=ln_done, inc=(wld, 16))
    ps_r = rsc.ps(f"{tag}_psr", [128, ntc * E])
    sr = p.sem(f"{tag}_r")
    for tc in range(ntc):
        for k in range(8):
            w = ln_done + [(wld, 16)] if (tc == 0 and k == 0) else []
            p.ins("tensor", lambda e, s, tc=tc, k=k: e.matmul(
                ps_r[:, E * tc:E * (tc + 1)],
                x_ln[k][:, 128 * tc:128 * (tc + 1)], wr_buf[:, E * k:E * (k + 1)],
                start=(tc == 0 and k == 0), stop=(tc == ntc - 1 and k == 7)),
                waits=w, inc=(sr, 1) if (tc == ntc - 1 and k == 7) else None)
    logit = rsc.sb(f"{tag}_lg", [128, ntc * E], F32)
    lmax = rsc.sb(f"{tag}_lm", [128, ntc], F32)
    expd = rsc.sb(f"{tag}_ex", [128, ntc * E], F32)
    ssum = rsc.sb(f"{tag}_ssm", [128, ntc], F32)
    gcol = rsc.sb(f"{tag}_gc", [128, ntc], F32)
    oh = rsc.sb(f"{tag}_oh", [128, ntc * E], F32)
    ohb = rsc.sb(f"{tag}_ohb", [128, ntc * E], BF16)
    a3 = lambda t: t[:].rearrange("p (g e) -> p g e", e=E)
    sv = p.sem(f"{tag}_rv")
    step = _emit_small_chain(p, sv)
    p.ins("vector", lambda e, s: e.tensor_copy(logit[:], ps_r[:]),
          waits=[(sr, p.count(sr)), (sv, p.count(sv))], inc=(sv, 1))
    step(lambda e, s: e.tensor_reduce(lmax[:], a3(logit), axis=AX.X, op=ALU.max))
    step(lambda e, s: _tt_b(e, a3(expd), a3(logit), lmax[:, :, None], ALU.subtract))
    p.ins("scalar", lambda e, s: e.activation(expd[:], expd[:], AF.Exp),
          waits=[(sv, p.count(sv))], inc=(p.sem(f"{tag}_rs"), 1))
    step(lambda e, s: e.tensor_reduce(ssum[:], a3(expd), axis=AX.X, op=ALU.add),
         [(f"{tag}_rs", 1)])
    step(lambda e, s: e.reciprocal(gcol[:], ssum[:]))
    step(lambda e, s: _tt_b(e, a3(oh), a3(logit), lmax[:, :, None], ALU.is_ge))
    step(lambda e, s: e.tensor_copy(ohb[:], oh[:]))
    return dict(oh=oh, ohb=ohb, gcol=gcol, sv=sv, waits=[(sv, p.count(sv))])


def emit_l2cnt(ctx, xT_ap, partials, waps, cnt_ap):
    p, T = ctx.prog, ctx.T
    E = 32
    ntc = T // 128
    cst = ctx.use_consts(["c_ones_col", "c_ones_row"])
    xf, ready = emit_load_x(ctx, xT_ap, partials)
    x_ln = [ctx.sb(f"xln{k}", [128, T], BF16) for k in range(8)]
    ln_done, _, _ = emit_ln(ctx, xf, x_ln, "l2cln", ready)
    rsc = ctx.scope()
    r = emit_l2router(ctx, x_ln, waps, ln_done, rsc, "l2c")
    cnt_ps = rsc.ps("l2c_cps", [1, E])
    st = p.sem("l2c_t")
    for tc in range(ntc):
        w = list(r["waits"]) if tc == 0 else []
        p.ins("tensor", lambda e, s, tc=tc: e.matmul(
            cnt_ps[:, :], cst["c_ones_col"][:, :], r["oh"][:, E * tc:E * (tc + 1)],
            start=(tc == 0), stop=(tc == ntc - 1)),
            waits=w, inc=(st, 1) if tc == ntc - 1 else None)
    cnt_sb = rsc.sb("l2c_sb", [1, E], F32)
    p.ins("vector", lambda e, s: e.tensor_copy(cnt_sb[:], cnt_ps[:, :]),
          waits=[(st, 1)], inc=(p.sem("l2c_v"), 1))
    so = p.sem("l2c_o")
    p.ins("gpsimd", lambda e, s: e.dma_start(out=cnt_ap[:, :], in_=cnt_sb[:]),
          waits=[("l2c_v", 1)], inc=(so, 16))
    p.barrier("gpsimd", [(so, 16)])
    rsc.close()


def emit_l2ffn(ctx, xT_ap, partials, waps, yT_ap, coef=0.5):
    p, T = ctx.prog, ctx.T
    E, I = 32, 1024
    CAP = 640.0
    ntc = T // 128
    cst = ctx.use_consts(["c_ones_col", "c_ones_row", "c_ident",
                          "c_sel32@bf", "c_lt128@bf"])
    xf, ready = emit_load_x(ctx, xT_ap, partials)
    x_ln = [ctx.sb(f"xln{k}", [128, T], BF16) for k in range(8)]
    ln_done, _, _ = emit_ln(ctx, xf, x_ln, "l2ln", ready)

    run = ctx.sb("l2_run", [1, E], F32)
    possb = ctx.sb("l2_pos", [128, E], F32)
    accep = ctx.sb("l2_acp", [128, E], F32)
    gtok = ctx.sb("l2_gtk", [128, ntc * E], F32)
    pm_b = ctx.sb("l2_pm", [8, 1], F32)
    ca_b = ctx.sb("l2_ca", [8, E], F32)

    rsc = ctx.scope()
    r = emit_l2router(ctx, x_ln, waps, ln_done, rsc, "l2m")
    ld2 = p.sem("l2_bld")
    p.ins("sync", lambda e, s: e.dma_start(out=pm_b[:], in_=waps["prevmask"]),
          waits=ln_done, inc=(ld2, 16))
    p.ins("sync", lambda e, s: e.dma_start(out=ca_b[:], in_=waps["counts_all"]),
          waits=ln_done, inc=(ld2, 16))
    base_ps = rsc.ps("l2_bps", [1, E])
    pos_ps = rsc.ps("l2_pps", [128, E])
    big_ps = rsc.ps("l2_gps", [128, E])
    tot_ps = rsc.ps("l2_tps", [1, E])
    st = p.sem("l2_at")
    sv2 = p.sem("l2_av")
    step = _emit_small_chain(p, sv2)
    p.ins("tensor", lambda e, s: e.matmul(
        base_ps[:, :], pm_b[:, :], ca_b[:, :], start=True, stop=True),
        waits=[(ld2, 32)], inc=(st, 1))
    p.ins("vector", lambda e, s: e.tensor_copy(run[:], base_ps[:, :]),
          waits=[(st, p.count(st)), (sv2, p.count(sv2))], inc=(sv2, 1))
    for tc in range(ntc):
        w = (list(r["waits"]) if tc == 0 else []) + [(sv2, p.count(sv2))]
        p.ins("tensor", lambda e, s, tc=tc: e.matmul(
            pos_ps[:, :], cst["c_lt128@bf"][:, :], r["ohb"][:, E * tc:E * (tc + 1)],
            start=True, stop=True), waits=w, inc=(st, 1))
        p.ins("tensor", lambda e, s: e.matmul(
            big_ps[:, :], cst["c_ones_row"][:, :], run[:, :],
            start=True, stop=True), inc=(st, 1))
        p.ins("tensor", lambda e, s, tc=tc: e.matmul(
            tot_ps[:, :], cst["c_ones_col"][:, :], r["oh"][:, E * tc:E * (tc + 1)],
            start=True, stop=True), inc=(st, 1))
        stc = p.count(st)
        step(lambda e, s: e.tensor_copy(possb[:], pos_ps[:, :]), [(st, stc)])
        step(lambda e, s: e.tensor_add(possb[:], possb[:], big_ps[:, :]))
        step(lambda e, s: e.tensor_scalar(
            accep[:], possb[:], CAP - 0.5, None, op0=ALU.is_le))
        step(lambda e, s, tc=tc: e.tensor_mul(
            gtok[:, E * tc:E * (tc + 1)], r["oh"][:, E * tc:E * (tc + 1)],
            accep[:]))
        step(lambda e, s, tc=tc: e.tensor_scalar(
            gtok[:, E * tc:E * (tc + 1)], gtok[:, E * tc:E * (tc + 1)],
            r["gcol"][:, tc:tc + 1], None, op0=ALU.mult))
        step(lambda e, s: e.tensor_add(run[:], run[:], tot_ps[:, :]))
    rsc.close()
    GT, gt_done = emit_gate_rows(ctx, "l2gt", gtok, E, [(sv2, p.count(sv2))])

    acc = xf
    experts = [dict(wg=waps["w_gate"][e_], wu=waps["w_up"][e_],
                    wd=waps["w_down"][e_], gate=e_) for e_ in range(E)]
    moe_done = emit_glu_moe(ctx, "l2x", x_ln, GT, E, experts, I, acc,
                            gt_done, "c_sel32@bf", seed_from=x_ln)
    emit_store_residual(ctx, "l2res", acc, xT_ap, partials, coef, yT_ap, moe_done)


def build_l2cnt_launch(with_partials=False):
    nc = bass.Bass("TRN2", target_bir_lowering=False)
    T = TPC
    xT = nc.dram_tensor("xT", [D, T], F32, kind="ExternalInput").ap()
    partials = None
    if with_partials:
        partials = [nc.dram_tensor(f"p{i}", [D, T], F32, kind="ExternalInput").ap()
                    for i in range(2)]
    waps = {
        "w_router": nc.dram_tensor("w_router", [D, 32], F32,
                                   kind="ExternalInput").ap(),
    }
    cnt = nc.dram_tensor("cnt", [1, 32], F32, kind="ExternalOutput").ap()
    prog = Prog(nc)
    with ExitStack() as stack:
        ctx = Ctx(nc, prog, stack, T)
        emit_l2cnt(ctx, xT, partials, waps, cnt)
        block = stack.enter_context(nc.Block())
        prog.run(block, stack)
    return nc, ctx


def build_l2ffn_launch(with_partials=False):
    nc = bass.Bass("TRN2", target_bir_lowering=False)
    T = TPC
    xT = nc.dram_tensor("xT", [D, T], F32, kind="ExternalInput").ap()
    partials = None
    if with_partials:
        partials = [nc.dram_tensor(f"p{i}", [D, T], F32, kind="ExternalInput").ap()
                    for i in range(2)]
    waps = {
        "w_router": nc.dram_tensor("w_router", [D, 32], F32,
                                   kind="ExternalInput").ap(),
        "counts_all": nc.dram_tensor("counts_all", [8, 32], F32,
                                     kind="ExternalInput").ap(),
        "prevmask": nc.dram_tensor("prevmask", [8, 1], F32,
                                   kind="ExternalInput").ap(),
        "w_gate": nc.dram_tensor("w_gate", [32, D, D], F32,
                                 kind="ExternalInput").ap(),
        "w_up": nc.dram_tensor("w_up", [32, D, D], F32,
                               kind="ExternalInput").ap(),
        "w_down": nc.dram_tensor("w_down", [32, D, D], F32,
                                 kind="ExternalInput").ap(),
    }
    yT = nc.dram_tensor("yT", [D, T], F32, kind="ExternalOutput").ap()
    prog = Prog(nc)
    with ExitStack() as stack:
        ctx = Ctx(nc, prog, stack, T)
        emit_l2ffn(ctx, xT, partials, waps, yT)
        block = stack.enter_context(nc.Block())
        prog.run(block, stack)
    return nc, ctx





def emit_ln_stream(ctx, xT_ap, x_ln, tag, ready):
    p, T = ctx.prog, ctx.T
    cst = ctx.cst
    nseg = T // TSEG
    sc = ctx.scope()
    xs = [[sc.sb(f"{tag}_x{b}_{k}", [128, TSEG], F32) for k in range(8)]
          for b in range(2)]
    sq = [sc.sb(f"{tag}_sq{b}", [128, TSEG], F32) for b in range(2)]
    r_mu = sc.sb(f"{tag}_mu", [1, TSEG], F32)
    r_q = sc.sb(f"{tag}_q", [1, TSEG], F32)
    r_rs = sc.sb(f"{tag}_rs", [1, TSEG], F32)
    ps_s = sc.ps(f"{tag}_pss", [1, TSEG])
    ps_q = sc.ps(f"{tag}_psq", [1, TSEG])
    ps_b = sc.ps(f"{tag}_psb", [128, TSEG])
    ps_b2 = sc.ps(f"{tag}_psb2", [128, TSEG])
    sl_ = p.sem(f"{tag}_l")
    sv = p.sem(f"{tag}_v")
    ss = p.sem(f"{tag}_s")
    st = p.sem(f"{tag}_t")
    x_free = {}
    for g in range(nseg):
        b = g % 2
        sl = slice(g * TSEG, (g + 1) * TSEG)
        for k in range(8):
            w = [(sv, x_free.get(b, 0))] if k == 0 else []
            if g == 0 and k == 0:
                w = w + list(ready)
            p.ins("sync", lambda e, s, k=k, sl=sl, b=b: e.dma_start(
                out=xs[b][k][:], in_=xT_ap[128 * k:128 * (k + 1), sl]),
                waits=w, inc=(sl_, 16))
        ld_tgt = p.count(sl_)
        for k in range(8):
            w = [(sl_, ld_tgt)] if k == 0 else []
            wsq = [(sl_, ld_tgt)] if k == 0 else [(st, p.count(st))]
            p.ins("scalar", lambda e, s, k=k, b=b: e.activation(
                sq[b][:, :], xs[b][k][:, :], AF.Square), waits=wsq, inc=(ss, 1))
            p.ins("tensor", lambda e, s, k=k, b=b: e.matmul(
                ps_s[:, :], cst["c_ones_col"][:, :], xs[b][k][:, :],
                start=(k == 0), stop=(k == 7)), waits=w)
            p.ins("tensor", lambda e, s, k=k, b=b: e.matmul(
                ps_q[:, :], cst["c_ones_col"][:, :], sq[b][:, :],
                start=(k == 0), stop=(k == 7)),
                waits=[(ss, p.count(ss))], inc=(st, 1))
        p.ins("vector", lambda e, s: e.tensor_scalar_mul(
            r_mu[:], ps_s[:, :], 1.0 / D), waits=[(st, p.count(st))])
        p.ins("vector", lambda e, s: e.tensor_scalar_mul(
            r_q[:], ps_q[:, :], 1.0 / D))
        p.ins("vector", lambda e, s: e.tensor_mul(r_rs[:], r_mu[:], r_mu[:]))
        p.ins("vector", lambda e, s: e.tensor_sub(r_q[:], r_q[:], r_rs[:]))
        p.ins("vector", lambda e, s: e.tensor_scalar_add(r_q[:], r_q[:], 1e-5),
              inc=(sv, 1))
        p.ins("scalar", lambda e, s: e.activation(r_q[:], r_q[:], AF.Sqrt),
              waits=[(sv, p.count(sv))], inc=(ss, 1))
        p.ins("vector", lambda e, s: e.reciprocal(r_rs[:], r_q[:]),
              waits=[(ss, p.count(ss))], inc=(sv, 1))
        p.ins("tensor", lambda e, s: e.matmul(
            ps_b[:, :], cst["c_ones_row"][:, :], r_mu[:, :], start=True, stop=True),
            waits=[(sv, p.count(sv))])
        p.ins("tensor", lambda e, s: e.matmul(
            ps_b2[:, :], cst["c_ones_row"][:, :], r_rs[:, :], start=True, stop=True),
            inc=(st, 1))
        for k in range(8):
            w = [(st, p.count(st))] if k == 0 else []
            p.ins("vector", lambda e, s, k=k, b=b: e.tensor_sub(
                sq[b][:, :], xs[b][k][:, :], ps_b[:, :]), waits=w)
            p.ins("vector", lambda e, s, k=k, b=b, sl=sl: e.tensor_mul(
                x_ln[k][:, sl], sq[b][:, :], ps_b2[:, :]),
                inc=(sv, 1) if k == 7 else None)
        x_free[b] = p.count(sv)
    sc.close()
    return [(sv, p.count(sv))]


def emit_l1attn(ctx, xT_ap, waps, yT_ap, NH=8):
    nc, p, T = ctx.nc, ctx.prog, ctx.T
    assert T == ATT_T
    nseg = T // TSEG
    ntc = T // 128
    cst = ctx.use_consts(["c_ones_col", "c_ones_row", "c_ident"])

    x_ln = [ctx.sb(f"xln{k}", [128, T], BF16) for k in range(8)]
    ready = [("c_ld", p.count("c_ld"))]
    ln_done = emit_ln_stream(ctx, xT_ap, x_ln, "a1ln", ready)

    ctx.load_const_after("c_rope_sin@bf", ln_done)
    ctx.load_const_after("c_rope_cos@bf", ln_done)
    cst = ctx.cst
    DH = 64 * NH
    wq_b = ctx.sb("a1_wq", [128, 8 * DH], BF16)
    wk_b = ctx.sb("a1_wk", [128, 8 * DH], BF16)
    wv_b = ctx.sb("a1_wv", [128, 8 * DH], BF16)
    bq_b = ctx.sb("a1_bq", [1, DH], BF16)
    bk_b = ctx.sb("a1_bk", [1, DH], BF16)
    bv_b = ctx.sb("a1_bv", [1, DH], BF16)
    ident_bf = ctx.sb("a1_idbf", [128, 128], BF16)
    ones_seg = ctx.sb("a1_ones", [1, TSEG], BF16)
    wl = p.sem("a1_wl")
    for nm, buf in (("wqT", wq_b), ("wkT", wk_b), ("wvT", wv_b),
                    ("bq", bq_b), ("bk", bk_b), ("bv", bv_b)):
        if nm.startswith("w"):
            p.ins("gpsimd", lambda e, s, nm=nm, buf=buf: e.dma_start(
                out=buf[:].rearrange("p (k h) -> p k h", k=8),
                in_=waps[nm].rearrange("(k p) h -> p k h", p=128)),
                waits=ln_done, inc=(wl, 16))
        else:
            p.ins("gpsimd", lambda e, s, nm=nm, buf=buf: e.dma_start(
                out=buf[:], in_=waps[nm]), waits=ln_done, inc=(wl, 16))
    sv = p.sem("a1_v")
    p.ins("vector", lambda e, s: e.memset(ones_seg[:], 1.0),
          waits=ln_done, inc=(sv, 1))
    p.ins("vector", lambda e, s: e.tensor_copy(ident_bf[:], cst["c_ident"][:, :]),
          inc=(sv, 1))

    outb = ctx.sb("a1_outb", [128, TSEG], F32)
    outc = ctx.sb("a1_outc", [128, TSEG], F32)
    ctxT = [ctx.sb(f"a1_ctx{i}", [128, T], BF16) for i in range(NH // 2)]
    hd_sc = ctx.scope()
    Qa = hd_sc.sb("a1_Qa", [65, T], BF16)
    Ka = hd_sc.sb("a1_Ka", [65, T], BF16)
    VaT = [hd_sc.sb(f"a1_VaT{i}", [64, TSEG], BF16) for i in range(2)]
    Va = hd_sc.sb("a1_Va", [128, ntc * 72], BF16)
    q2r = hd_sc.sb("a1_q2", [1, T], BF16)
    k2r = hd_sc.sb("a1_k2", [1, T], BF16)
    sqs = [hd_sc.sb(f"a1_sqs{i}", [64, TSEG], F32) for i in range(2)]
    rsc4 = [hd_sc.sb(f"a1_r4_{i}", [32, TSEG], F32) for i in range(4)]
    kmax = hd_sc.sb("a1_km", [1, 1], F32)
    P_b = [hd_sc.sb(f"a1_P{i}", [128, TSEG], BF16) for i in range(2)]
    drow = hd_sc.sb("a1_dr", [1, TSEG], F32)
    drec = hd_sc.sb("a1_drec", [1, TSEG], F32)

    ps_sc = ctx.scope()
    pr_ps = [ps_sc.ps(f"a1_pr{i}", [64, TSEG]) for i in range(2)]
    S_ps = [ps_sc.ps(f"a1_S{i}", [128, TSEG]) for i in range(2)]
    acc_ps = ps_sc.ps("a1_acc", [65, TSEG])
    bc_ps = ps_sc.ps("a1_bc", [64, TSEG])
    row_ps = ps_sc.ps("a1_row", [1, TSEG])
    vt_ps = ps_sc.ps("a1_vt", [128, 64], BF16)

    st = p.sem("a1_t")
    ss = p.sem("a1_s")
    sg = p.sem("a1_g")

    W_READY = p.count(wl)
    pr_free = {}
    S_free = {}
    P_free = {}
    sq_free = {}

    def proj_qk(h, wbuf, bbuf, dst):
        for g in range(nseg):
            sl = slice(g * TSEG, (g + 1) * TSEG)
            b = g % 2
            for k in range(8):
                w = []
                if k == 0:
                    w = [(sv, pr_free.get(b, 0))]
                    if h == 0 and g == 0:
                        w += [(wl, W_READY), (sv, 2)]
                p.ins("tensor", lambda e, s, k=k, sl=sl, b=b, h=h, wbuf=wbuf: e.matmul(
                    pr_ps[b][:, :], wbuf[:, DH * k + 64 * h:DH * k + 64 * (h + 1)],
                    x_ln[k][:, sl], start=(k == 0), stop=False), waits=w)
            p.ins("tensor", lambda e, s, h=h, b=b, bbuf=bbuf: e.matmul(
                pr_ps[b][:, :], bbuf[0:1, 64 * h:64 * (h + 1)], ones_seg[0:1, :],
                start=False, stop=True), inc=(st, 1))
            w = [(st, p.count(st)), (st, sq_free.get(0, 0))]
            p.ins("vector", lambda e, s, sl=sl, b=b: e.tensor_mul(
                rsc4[0][:, :], cst["c_rope_sin@bf"][:, sl], pr_ps[b][32:64, :]),
                waits=w)
            p.ins("vector", lambda e, s, sl=sl, b=b: e.tensor_mul(
                rsc4[1][:, :], cst["c_rope_sin@bf"][:, sl], pr_ps[b][0:32, :]))
            p.ins("vector", lambda e, s, sl=sl, b=b: e.tensor_mul(
                rsc4[2][:, :], cst["c_rope_cos@bf"][:, sl], pr_ps[b][0:32, :]))
            p.ins("vector", lambda e, s, sl=sl, b=b: e.tensor_mul(
                rsc4[3][:, :], cst["c_rope_cos@bf"][:, sl], pr_ps[b][32:64, :]))
            p.ins("vector", lambda e, s, sl=sl, dst=dst: e.tensor_sub(
                dst[0:32, sl], rsc4[2][:, :], rsc4[0][:, :]))
            p.ins("vector", lambda e, s, sl=sl, dst=dst: e.tensor_add(
                dst[32:64, sl], rsc4[1][:, :], rsc4[3][:, :]), inc=(sv, 1))
            pr_free[b] = p.count(sv)

    def proj_v(h):
        for g in range(nseg):
            sl = slice(g * TSEG, (g + 1) * TSEG)
            b = g % 2
            for k in range(8):
                w = [(sv, pr_free.get(b, 0))] if k == 0 else []
                p.ins("tensor", lambda e, s, k=k, sl=sl, b=b, h=h: e.matmul(
                    pr_ps[b][:, :], wv_b[:, DH * k + 64 * h:DH * k + 64 * (h + 1)],
                    x_ln[k][:, sl], start=(k == 0), stop=False), waits=w)
            p.ins("tensor", lambda e, s, h=h, b=b: e.matmul(
                pr_ps[b][:, :], bv_b[0:1, 64 * h:64 * (h + 1)], ones_seg[0:1, :],
                start=False, stop=True), inc=(st, 1))
            p.ins("vector", lambda e, s, b=b: e.tensor_copy(
                VaT[b][:, :], pr_ps[b][:, :]), waits=[(st, p.count(st))], inc=(sv, 1))
            pr_free[b] = p.count(sv)
            vdone = p.count(sv)
            for j in range(4):
                tc = 4 * g + j
                w = [(sv, vdone)] if j == 0 else [(sv, p.count(sv))]
                p.ins("tensor", lambda e, s, j=j, b=b: e.transpose(
                    vt_ps[:, :], VaT[b][:, 128 * j:128 * (j + 1)],
                    ident_bf[0:64, 0:64]), waits=w, inc=(st, 1))
                p.ins("vector", lambda e, s, tc=tc: e.tensor_copy(
                    Va[:].rearrange("p (c w) -> p c w", w=72)[:, tc, 0:64], vt_ps[:, :]),
                    waits=[(st, p.count(st))], inc=(sv, 1))
        p.ins("vector", lambda e, s: e.memset(
            Va[:].rearrange("p (c w) -> p c w", w=72)[:, :, 64:65], 1.0), inc=(sv, 1))

    def norms(h):
        rows_done = p.count(sv)
        for which, src_t, dst_r in (("q", Qa, q2r), ("k", Ka, k2r)):
            for g in range(nseg):
                sl = slice(g * TSEG, (g + 1) * TSEG)
                b = g % 2
                w = [(sv, rows_done)] if (which == "q" and g == 0) else []
                if (b in sq_free):
                    w = w + [(st, sq_free[b])]
                p.ins("scalar", lambda e, s, sl=sl, b=b, src_t=src_t: e.activation(
                    sqs[b][:, :], src_t[0:64, sl], AF.Square), waits=w, inc=(ss, 1))
                p.ins("tensor", lambda e, s, b=b: e.matmul(
                    row_ps[:, :], cst["c_ones_col"][0:64, :], sqs[b][:, :],
                    start=True, stop=True),
                    waits=[(ss, p.count(ss)), (sv, p.count(sv))], inc=(st, 1))
                sq_free[b] = p.count(st)
                p.ins("vector", lambda e, s, sl=sl, dst_r=dst_r: e.tensor_copy(
                    dst_r[:, sl], row_ps[:, :]), waits=[(st, p.count(st))],
                    inc=(sv, 1))
        p.ins("scalar", lambda e, s: e.activation(q2r[:], q2r[:], AF.Sqrt),
              waits=[(sv, p.count(sv))], inc=(ss, 1))
        p.ins("scalar", lambda e, s: e.activation(k2r[:], k2r[:], AF.Sqrt),
              inc=(ss, 1))
        p.ins("vector", lambda e, s: e.reduce_max(kmax[:], k2r[:], axis=AX.X),
              waits=[(ss, p.count(ss))], inc=(sv, 1))
        p.ins("vector", lambda e, s: e.tensor_scalar(
            Qa[64:65, :], q2r[:], kmax[0:1, 0:1], -1.0, op0=ALU.mult, op1=ALU.mult),
            waits=[(sv, p.count(sv))], inc=(sv, 1))
        p.ins("vector", lambda e, s: e.memset(Ka[64:65, :], 1.0), inc=(sv, 1))

    def attend(h):
        prep_done = p.count(sv)
        for qi in range(nseg):
            qsl = slice(qi * TSEG, (qi + 1) * TSEG)
            nk = (qi + 1) * (TSEG // 128)
            for ki in range(nk):
                b = ki % 2
                ksl = slice(ki * 128, (ki + 1) * 128)
                w = [(ss, S_free.get(b, 0))]
                if qi == 0 and ki == 0:
                    w += [(sv, prep_done)]
                p.ins("tensor", lambda e, s, ksl=ksl, qsl=qsl, b=b: e.matmul(
                    S_ps[b][:, :], Ka[0:65, ksl], Qa[0:65, qsl],
                    start=True, stop=True), waits=w, inc=(st, 1))
                w2 = [(st, p.count(st)), (st, P_free.get(b, 0))]
                p.ins("scalar", lambda e, s, b=b: e.activation(
                    P_b[b][:], S_ps[b][:, :], AF.Exp, scale=0.125),
                    waits=w2, inc=(ss, 1))
                S_free[b] = p.count(ss)
                diag = (ki * 128 + 127 >= qi * TSEG)
                if diag:
                    p.ins("gpsimd", lambda e, s, b=b, ki=ki, qi=qi: e.affine_select(
                        P_b[b][:], P_b[b][:], [[1, TSEG]], ALU.is_ge, 0.0,
                        base=qi * TSEG - ki * 128, channel_multiplier=-1),
                        waits=[(ss, p.count(ss))], inc=(sg, 1))
                    avw = [(sg, p.count(sg))]
                else:
                    avw = [(ss, p.count(ss))]
                p.ins("tensor", lambda e, s, ki=ki, b=b, nk=nk: e.matmul(
                    acc_ps[:, :],
                    Va[:].rearrange("p (c w) -> p c w", w=72)[:, ki, 0:65],
                    P_b[b][:], start=(ki == 0), stop=(ki == nk - 1)),
                    waits=avw, inc=(st, 1) if ki == nk - 1 else None)
                P_free[b] = p.count(st)
            p.ins("vector", lambda e, s: e.tensor_copy(drow[:], acc_ps[64:65, :]),
                  waits=[(st, p.count(st))], inc=(sv, 1))
            p.ins("vector", lambda e, s: e.reciprocal(drec[:], drow[:]),
                  waits=[(sv, p.count(sv))], inc=(sv, 1))
            p.ins("tensor", lambda e, s: e.matmul(
                bc_ps[:, :], cst["c_ones_row"][0:1, 0:64], drec[:, :],
                start=True, stop=True), waits=[(sv, p.count(sv))], inc=(st, 1))
            p.ins("scalar", lambda e, s: e.activation(
                outb[0:64, :], acc_ps[0:64, :], AF.Copy),
                waits=[(st, p.count(st))], inc=(ss, 1))
            p.ins("vector", lambda e, s, h=h, qsl=qsl: e.tensor_mul(
                ctxT[h // 2][64 * (h % 2):64 * (h % 2) + 64, qsl],
                outb[0:64, :], bc_ps[:, :]),
                waits=[(ss, p.count(ss))], inc=(sv, 1))

    for h in range(NH):
        proj_qk(h, wq_b, bq_b, Qa)
        proj_qk(h, wk_b, bk_b, Ka)
        proj_v(h)
        norms(h)
        attend(h)

    ps_sc.close()
    hd_sc.close()
    attn_all_done = p.count(sv)
    wo_b = ctx.sb("a1_wo", [128, (DH // 128) * D], BF16)
    boC = ctx.sb("a1_boC", [128, 8], F32)
    p.ins("gpsimd", lambda e, s: e.dma_start(
        out=wo_b[:].rearrange("p (k d) -> p k d", k=DH // 128),
        in_=waps["woR"].rearrange("(k p) d -> p k d", p=128)),
        waits=[("a1_v", attn_all_done)], inc=(wl, 16))
    p.ins("sync", lambda e, s: e.dma_start(out=boC[:], in_=waps["boC"]),
          waits=[("a1_v", attn_all_done)], inc=(wl, 16))
    wo_rdy = p.count(wl)
    y_ps = [ctx.ps(f"a1_y{i}", [128, TSEG]) for i in range(2)]
    so = p.sem("a1_o")
    ctx_done = p.count(sv)
    ob = [outb, outc]
    y_free2 = {}
    o_free = {}
    for g in range(nseg):
        sl = slice(g * TSEG, (g + 1) * TSEG)
        for dc in range(8):
            i = g * 8 + dc
            bk_ = i % 2
            for kc in range(DH // 128):
                w = []
                if kc == 0:
                    w = [(sv, y_free2.get(bk_, 0))]
                    if i == 0:
                        w += [(wl, wo_rdy), (sv, ctx_done)]
                p.ins("tensor", lambda e, s, kc=kc, dc=dc, sl=sl, bk_=bk_: e.matmul(
                    y_ps[bk_][:, :], wo_b[:, D * kc + 128 * dc:D * kc + 128 * (dc + 1)],
                    ctxT[kc][:, sl], start=(kc == 0), stop=(kc == DH // 128 - 1)),
                    waits=w, inc=(st, 1) if kc == DH // 128 - 1 else None)
            p.ins("vector", lambda e, s, bk_=bk_, dc=dc: e.tensor_scalar(
                ob[bk_][:], y_ps[bk_][:, :], boC[:, dc:dc + 1], None, op0=ALU.add),
                waits=[(st, p.count(st)), (so, o_free.get(bk_, 0))], inc=(sv, 1))
            y_free2[bk_] = p.count(sv)
            p.ins("gpsimd", lambda e, s, dc=dc, sl=sl, bk_=bk_: e.dma_start(
                out=yT_ap[128 * dc:128 * (dc + 1), sl], in_=ob[bk_][:]),
                waits=[(sv, p.count(sv))], inc=(so, 16))
            o_free[bk_] = p.count(so)
    p.barrier("gpsimd", [(so, p.count(so))])


def build_l1attn_launch(NH=8):
    nc = bass.Bass("TRN2", target_bir_lowering=False)
    T = ATT_T
    DH = 64 * NH
    xT = nc.dram_tensor("xT", [D, T], F32, kind="ExternalInput").ap()
    waps = {
        "wqT": nc.dram_tensor("wqT", [D, DH], F32, kind="ExternalInput").ap(),
        "bq": nc.dram_tensor("bq", [1, DH], F32, kind="ExternalInput").ap(),
        "wkT": nc.dram_tensor("wkT", [D, DH], F32, kind="ExternalInput").ap(),
        "bk": nc.dram_tensor("bk", [1, DH], F32, kind="ExternalInput").ap(),
        "wvT": nc.dram_tensor("wvT", [D, DH], F32, kind="ExternalInput").ap(),
        "bv": nc.dram_tensor("bv", [1, DH], F32, kind="ExternalInput").ap(),
        "woR": nc.dram_tensor("woR", [DH, D], F32, kind="ExternalInput").ap(),
        "boC": nc.dram_tensor("boC", [128, 8], F32, kind="ExternalInput").ap(),
    }
    yT = nc.dram_tensor("yT", [D, T], F32, kind="ExternalOutput").ap()
    prog = Prog(nc)
    with ExitStack() as stack:
        ctx = Ctx(nc, prog, stack, T)
        emit_l1attn(ctx, xT, waps, yT, NH)
        block = stack.enter_context(nc.Block())
        prog.run(block, stack)
    return nc, ctx



def emit_l0attn(ctx, xT_ap, waps, yT_ap, NH=8):
    nc, p, T = ctx.nc, ctx.prog, ctx.T
    assert T == ATT_T
    nseg = T // TSEG
    ntc = T // 128
    cst = ctx.use_consts(["c_ones_col", "c_ones_row", "c_ident", "c_iota_T",
                          "c_causal_big@bf"])

    x_ln = [ctx.sb(f"xln{k}", [128, T], BF16) for k in range(8)]
    ready = [("c_ld", p.count("c_ld"))]
    ln_done = emit_ln_stream(ctx, xT_ap, x_ln, "a0ln", ready)

    DH = 64 * NH
    wq_b = ctx.sb("a0_wq", [128, 4 * DH], BF16)
    wk_b = ctx.sb("a0_wk", [128, 4 * 128], BF16)
    wv_b = ctx.sb("a0_wv", [128, 4 * 128], BF16)
    ph_row = ctx.sb("a0_ph", [1, NH], F32)
    ph2_row = ctx.sb("a0_ph2", [1, NH], F32)
    cs_row = ctx.sb("a0_cs", [1, 2 * NH], F32)
    lsl_row = ctx.sb("a0_lsl", [1, 8 * NH], F32)
    sl_row = ctx.sb("a0_sl", [1, 8 * NH], F32)
    wl = p.sem("a0_wl")
    for nm, buf, kk in (("wqT", wq_b, 4), ("wkT", wk_b, 4), ("wvT", wv_b, 4)):
        p.ins("gpsimd", lambda e, s, nm=nm, buf=buf, kk=kk: e.dma_start(
            out=buf[:].rearrange("p (k h) -> p k h", k=kk),
            in_=waps[nm].rearrange("(k p) h -> p k h", p=128)),
            waits=ln_done, inc=(wl, 16))
    p.ins("sync", lambda e, s: e.dma_start(out=ph_row[:], in_=waps["phase"]),
          waits=ln_done, inc=(wl, 16))
    p.ins("sync", lambda e, s: e.dma_start(out=lsl_row[:], in_=waps["slopes"]),
          inc=(wl, 16))
    ss = p.sem("a0_s")
    sv = p.sem("a0_v")
    p.ins("scalar", lambda e, s: e.activation(
        cs_row[0:1, 0:NH], ph_row[:], AF.Sin, bias=PI / 2.0 if False else 0.0),
        waits=[(wl, p.count(wl))], inc=(ss, 1))
    p.ins("vector", lambda e, s: e.tensor_scalar_add(
        ph2_row[:], ph_row[:], PI / 2.0),
        waits=[(wl, p.count(wl))], inc=(sv, 1))
    p.ins("scalar", lambda e, s: e.activation(
        cs_row[0:1, NH:2 * NH], ph2_row[:], AF.Sin),
        waits=[(sv, p.count(sv))], inc=(ss, 1))
    p.ins("scalar", lambda e, s: e.activation(sl_row[:], lsl_row[:], AF.Exp),
          inc=(ss, 1))
    trig_done = [(ss, p.count(ss))]

    outb = ctx.sb("a0_outb", [128, TSEG], F32)
    outc = outb
    ctxT = [ctx.sb(f"a0_ctx{i}", [128, T], BF16) for i in range(NH // 2)]
    ctx.a0_ctxT0 = ctxT
    ident_bf = ctx.sb("a0_idbf", [64, 64], BF16)
    hd_sc = ctx.scope()
    Qh = hd_sc.sb("a0_Qh", [64, T], BF16)
    Kh = [hd_sc.sb("a0_Kh", [64, T], BF16)] * 2
    qaug = hd_sc.sb("a0_qaug", [1, T], F32)
    kaug = hd_sc.sb("a0_kaug", [1, T], F32)
    onesr = hd_sc.sb("a0_onesr", [1, TSEG], F32)
    VaT = [hd_sc.sb(f"a0_VaT{i}", [64, TSEG], BF16) for i in range(2)]
    Va = [hd_sc.sb("a0_Va", [128, ntc * 68], BF16)] * 2
    Mp = hd_sc.sb("a0_Mp", [128, 32], F32)
    nseg_row = hd_sc.sb("a0_nr", [1, TSEG], F32)
    rot = [hd_sc.sb(f"a0_rot{i}", [32, TSEG], F32) for i in range(4)]
    csc = hd_sc.sb("a0_csc", [32, 2], F32)
    P_b = [hd_sc.sb(f"a0_P{i}", [128, TSEG], BF16) for i in range(2)]
    drec = hd_sc.sb("a0_drec", [1, TSEG], F32)
    drow = nseg_row
    mtmp = hd_sc.sb("a0_mtmp", [128, 2], F32)

    ps_sc = ctx.scope()
    pr_ps = [ps_sc.ps(f"a0_pr{i}", [64, TSEG]) for i in range(2)]
    S_ps = [ps_sc.ps(f"a0_S{i}", [128, TSEG]) for i in range(2)]
    acc_ps = ps_sc.ps("a0_acc", [65, TSEG])
    bc_ps = ps_sc.ps("a0_bc", [64, TSEG])
    row_ps = ps_sc.ps("a0_row", [1, TSEG])
    vt_ps = ps_sc.ps("a0_vt", [128, 64], BF16)

    st = p.sem("a0_t")
    sg = p.sem("a0_g")
    p.ins("vector", lambda e, s: e.tensor_copy(ident_bf[:], cst["c_ident"][0:64, 0:64]),
          waits=ln_done, inc=(sv, 1))

    pr_free = {}
    S_free = {}
    S_free_a = {}
    P_free = {}
    sq_free = {}
    W_READY = p.count(wl)

    def proj_kv(kv):
        for g in range(nseg):
            sl = slice(g * TSEG, (g + 1) * TSEG)
            b = g % 2
            for k in range(4):
                w = [(sv, pr_free.get(b, 0))]
                if kv == 0 and g == 0 and k == 0:
                    w += [(wl, W_READY), (sv, 2)]
                p.ins("tensor", lambda e, s, k=k, sl=sl, b=b, kv=kv: e.matmul(
                    pr_ps[b][:, :], wk_b[:, 128 * k + 64 * kv:128 * k + 64 * (kv + 1)],
                    x_ln[4 + k][:, sl], start=(k == 0), stop=(k == 3)),
                    waits=w if k == 0 else [], inc=(st, 1) if k == 3 else None)
            p.ins("vector", lambda e, s, sl=sl, b=b, kv=kv: e.tensor_copy(
                Kh[kv][:, sl], pr_ps[b][:, :]), waits=[(st, p.count(st))], inc=(sv, 1))
            pr_free[b] = p.count(sv)
        for g in range(nseg):
            sl = slice(g * TSEG, (g + 1) * TSEG)
            b = g % 2
            for k in range(4):
                w = [(sv, pr_free.get(b, 0))] if k == 0 else []
                p.ins("tensor", lambda e, s, k=k, sl=sl, b=b, kv=kv: e.matmul(
                    pr_ps[b][:, :], wv_b[:, 128 * k + 64 * kv:128 * k + 64 * (kv + 1)],
                    x_ln[k][:, sl], start=(k == 0), stop=(k == 3)),
                    waits=w, inc=(st, 1) if k == 3 else None)
            p.ins("vector", lambda e, s, b=b: e.tensor_copy(
                VaT[b][:, :], pr_ps[b][:, :]), waits=[(st, p.count(st))], inc=(sv, 1))
            pr_free[b] = p.count(sv)
            vdone = p.count(sv)
            for j in range(4):
                tc = 4 * g + j
                w = [(sv, vdone)] if j == 0 else [(sv, p.count(sv))]
                p.ins("tensor", lambda e, s, j=j, b=b: e.transpose(
                    vt_ps[:, :], VaT[b][:, 128 * j:128 * (j + 1)],
                    ident_bf[:, :]), waits=w, inc=(st, 1))
                p.ins("vector", lambda e, s, tc=tc, kv=kv: e.tensor_copy(
                    Va[kv][:].rearrange("p (c w) -> p c w", w=68)[:, tc, 0:64],
                    vt_ps[:, :]), waits=[(st, p.count(st))], inc=(sv, 1))
        p.ins("vector", lambda e, s, kv=kv: e.memset(
            Va[kv][:].rearrange("p (c w) -> p c w", w=68)[:, :, 64:65], 1.0),
            inc=(sv, 1))

    def proj_q(h):
        p.ins("tensor", lambda e, s, h=h: e.matmul(
            bc_ps[0:32, 0:2], cst["c_ones_row"][0:1, 0:32],
            cs_row[0:1, :].rearrange("o (two h) -> o two h", h=NH)[:, :, h],
            start=True, stop=True),
            waits=trig_done + [(sv, p.count(sv))], inc=(st, 1))
        p.ins("vector", lambda e, s: e.tensor_copy(csc[:, :], bc_ps[0:32, 0:2]),
              waits=[(st, p.count(st))], inc=(sv, 1))
        for g in range(nseg):
            sl = slice(g * TSEG, (g + 1) * TSEG)
            b = g % 2
            for k in range(4):
                w = [(sv, pr_free.get(b, 0))] if k == 0 else []
                p.ins("tensor", lambda e, s, k=k, sl=sl, b=b, h=h: e.matmul(
                    pr_ps[b][:, :], wq_b[:, DH * k + 64 * h:DH * k + 64 * (h + 1)],
                    x_ln[4 + k][:, sl], start=(k == 0), stop=(k == 3)),
                    waits=w, inc=(st, 1) if k == 3 else None)
            w = [(st, p.count(st))]
            p.ins("vector", lambda e, s, b=b: e.tensor_copy(
                rot[0][:, :], pr_ps[b][0:32, :]), waits=w)
            p.ins("vector", lambda e, s, b=b: e.tensor_copy(
                rot[1][:, :], pr_ps[b][32:64, :]))
            p.ins("vector", lambda e, s: e.tensor_single_scalar(
                rot[2][:, :], rot[1][:, :], csc[:, 0:1], ALU.mult))
            p.ins("vector", lambda e, s, sl=sl: e.scalar_tensor_tensor(
                Qh[0:32, sl], rot[0][:, :], csc[:, 1:2], rot[2][:, :],
                op0=ALU.mult, op1=ALU.subtract))
            p.ins("vector", lambda e, s: e.tensor_single_scalar(
                rot[3][:, :], rot[1][:, :], csc[:, 1:2], ALU.mult))
            p.ins("vector", lambda e, s, sl=sl: e.scalar_tensor_tensor(
                Qh[32:64, sl], rot[0][:, :], csc[:, 0:1], rot[3][:, :],
                op0=ALU.mult, op1=ALU.add), inc=(sv, 1))
            pr_free[b] = p.count(sv)

    def pass_a(h, kv):
        prep = p.count(sv)
        p.ins("vector", lambda e, s, h=h: e.tensor_scalar(
            qaug[:, :], cst["c_iota_T"][:, :], sl_row[0:1, 8 * h:8 * h + 1], 8.0,
            op0=ALU.mult, op1=ALU.mult),
            waits=[(sv, prep)] + trig_done, inc=(sv, 1))
        p.ins("vector", lambda e, s, h=h: e.tensor_scalar(
            kaug[:, :], cst["c_iota_T"][:, :], sl_row[0:1, 8 * h:8 * h + 1], -8.0,
            op0=ALU.mult, op1=ALU.mult), inc=(sv, 1))
        p.ins("vector", lambda e, s: e.memset(onesr[:, :], 1.0), inc=(sv, 1))
        prep2 = p.count(sv)
        for c in range(32):
            nkt = (c * 128 + 128 + TSEG - 1) // TSEG
            for kt in range(nkt):
                b = kt % 2
                kw = min(TSEG, T - kt * TSEG)
                w = [(sv, max(S_free_a.get(b, 0), prep2)), (ss, S_free.get(b, 0))]
                p.ins("tensor", lambda e, s, c=c, kt=kt, b=b, kw=kw, kv=kv: e.matmul(
                    S_ps[b][:, 0:kw], Qh[0:64, 128 * c:128 * (c + 1)],
                    Kh[kv][0:64, kt * TSEG:kt * TSEG + kw],
                    start=True, stop=False), waits=w)
                p.ins("tensor", lambda e, s, c=c, kt=kt, b=b, kw=kw: e.matmul(
                    S_ps[b][:, 0:kw], qaug[0:1, 128 * c:128 * (c + 1)],
                    onesr[0:1, 0:kw], start=False, stop=False))
                p.ins("tensor", lambda e, s, c=c, kt=kt, b=b, kw=kw: e.matmul(
                    S_ps[b][:, 0:kw], cst["c_ones_row"][0:1, :],
                    kaug[0:1, kt * TSEG:kt * TSEG + kw],
                    start=False, stop=True), inc=(st, 1))
                p.ins("vector", lambda e, s, b=b, kw=kw: e.reduce_max(
                    mtmp[:, 0:1], S_ps[b][:, 0:kw], axis=AX.X),
                    waits=[(st, p.count(st))], inc=(sv, 1))
                if kt == 0:
                    p.ins("vector", lambda e, s, c=c: e.tensor_copy(
                        Mp[:, c:c + 1], mtmp[:, 0:1]),
                        waits=[(sv, p.count(sv))], inc=(sv, 1))
                else:
                    p.ins("vector", lambda e, s, c=c: e.tensor_max(
                        Mp[:, c:c + 1], Mp[:, c:c + 1], mtmp[:, 0:1]),
                        waits=[(sv, p.count(sv))], inc=(sv, 1))
                S_free_a[b] = p.count(sv)
        for g8 in range(nseg):
            for j in range(4):
                c = 4 * g8 + j
                p.ins("tensor", lambda e, s, c=c, j=j: e.matmul(
                    row_ps[0:1, 128 * j:128 * (j + 1)], Mp[:, c:c + 1],
                    cst["c_ident"][:, :], start=(j == 0), stop=(j == 3)),
                    waits=[(sv, p.count(sv))] if j == 0 else [],
                    inc=(st, 1) if j == 3 else None)
            p.ins("vector", lambda e, s, g8=g8: e.tensor_sub(
                qaug[0:1, TSEG * g8:TSEG * (g8 + 1)],
                qaug[0:1, TSEG * g8:TSEG * (g8 + 1)], row_ps[0:1, :]),
                waits=[(st, p.count(st))], inc=(sv, 1))

    def attend(h, kv):
        prep_done = p.count(sv)
        for qi in range(nseg):
            qsl = slice(qi * TSEG, (qi + 1) * TSEG)
            nk = (qi + 1) * (TSEG // 128)
            for ki in range(nk):
                b = ki % 2
                ksl = slice(ki * 128, (ki + 1) * 128)
                w = [(ss, S_free.get(b, 0)), (sv, S_free_a.get(b, 0))]
                if qi == 0 and ki == 0:
                    w += [(sv, prep_done)]
                p.ins("tensor", lambda e, s, ksl=ksl, qsl=qsl, b=b, kv=kv: e.matmul(
                    S_ps[b][:, :], Kh[kv][0:64, ksl], Qh[0:64, qsl],
                    start=True, stop=False), waits=w)
                p.ins("tensor", lambda e, s, ksl=ksl, qsl=qsl, b=b: e.matmul(
                    S_ps[b][:, :], cst["c_ones_row"][0:1, :], qaug[0:1, qsl],
                    start=False, stop=False))
                p.ins("tensor", lambda e, s, ksl=ksl, qsl=qsl, b=b: e.matmul(
                    S_ps[b][:, :], kaug[0:1, ksl], onesr[0:1, :],
                    start=False, stop=True), inc=(st, 1))
                w2 = [(st, p.count(st)), (st, P_free.get(b, 0))]
                p.ins("scalar", lambda e, s, b=b: e.activation(
                    P_b[b][:], S_ps[b][:, :], AF.Exp, scale=0.125),
                    waits=w2, inc=(ss, 1))
                S_free[b] = p.count(ss)
                diag = (ki * 128 + 127 >= qi * TSEG)
                if diag:
                    p.ins("gpsimd", lambda e, s, b=b, ki=ki, qi=qi: e.affine_select(
                        P_b[b][:], P_b[b][:], [[1, TSEG]], ALU.is_ge, 0.0,
                        base=qi * TSEG - ki * 128, channel_multiplier=-1),
                        waits=[(ss, p.count(ss))], inc=(sg, 1))
                    avw = [(sg, p.count(sg))]
                else:
                    avw = [(ss, p.count(ss))]
                p.ins("tensor", lambda e, s, ki=ki, b=b, nk=nk, kv=kv: e.matmul(
                    acc_ps[:, :],
                    Va[kv][:].rearrange("p (c w) -> p c w", w=68)[:, ki, 0:65],
                    P_b[b][:], start=(ki == 0), stop=(ki == nk - 1)),
                    waits=avw, inc=(st, 1) if ki == nk - 1 else None)
                P_free[b] = p.count(st)
            p.ins("vector", lambda e, s: e.tensor_scalar_add(
                drow[:], acc_ps[64:65, :], 1.0),
                waits=[(st, p.count(st))], inc=(sv, 1))
            p.ins("vector", lambda e, s: e.reciprocal(drec[:], drow[:]),
                  waits=[(sv, p.count(sv))], inc=(sv, 1))
            p.ins("tensor", lambda e, s: e.matmul(
                bc_ps[:, :], cst["c_ones_row"][0:1, 0:64], drec[:, :],
                start=True, stop=True), waits=[(sv, p.count(sv))], inc=(st, 1))
            p.ins("scalar", lambda e, s: e.activation(
                outb[0:64, :], acc_ps[0:64, :], AF.Copy),
                waits=[(st, p.count(st))], inc=(ss, 1))
            p.ins("vector", lambda e, s, h=h, qsl=qsl: e.tensor_mul(
                ctxT[h // 2][64 * (h % 2):64 * (h % 2) + 64, qsl],
                outb[0:64, :], bc_ps[:, :]),
                waits=[(ss, p.count(ss))], inc=(sv, 1))

    for kv in (0, 1):
        proj_kv(kv)
        for h in range(4 * kv, 4 * kv + 4):
            proj_q(h)
            pass_a(h, kv)
            attend(h, kv)

    ps_sc.close()
    hd_sc.close()
    attn_all_done = p.count(sv)
    wo_b = ctx.sb("a0_wo", [128, (DH // 128) * D], BF16)
    p.ins("gpsimd", lambda e, s: e.dma_start(
        out=wo_b[:].rearrange("p (k d) -> p k d", k=DH // 128),
        in_=waps["woR"].rearrange("(k p) d -> p k d", p=128)),
        waits=[("a0_v", attn_all_done)], inc=(wl, 16))
    wo_rdy = p.count(wl)
    y_ps = [ctx.ps(f"a0_y{i}", [128, TSEG]) for i in range(2)]
    so = p.sem("a0_o")
    ob = [outb, outc]
    y_free2 = {}
    o_free = {}
    for g in range(nseg):
        sl = slice(g * TSEG, (g + 1) * TSEG)
        for dc in range(8):
            i = g * 8 + dc
            bk_ = i % 2
            for kc in range(DH // 128):
                w = []
                if kc == 0:
                    w = [(sv, y_free2.get(bk_, 0))]
                    if i == 0:
                        w += [(wl, wo_rdy), (sv, attn_all_done)]
                p.ins("tensor", lambda e, s, kc=kc, dc=dc, sl=sl, bk_=bk_: e.matmul(
                    y_ps[bk_][:, :], wo_b[:, D * kc + 128 * dc:D * kc + 128 * (dc + 1)],
                    ctxT[kc][:, sl], start=(kc == 0), stop=(kc == DH // 128 - 1)),
                    waits=w, inc=(st, 1) if kc == DH // 128 - 1 else None)
            p.ins("vector", lambda e, s, bk_=bk_: e.tensor_copy(
                ob[bk_][:], y_ps[bk_][:, :]),
                waits=[(st, p.count(st)), (so, p.count(so))], inc=(sv, 1))
            y_free2[bk_] = p.count(sv)
            p.ins("gpsimd", lambda e, s, dc=dc, sl=sl, bk_=bk_: e.dma_start(
                out=yT_ap[128 * dc:128 * (dc + 1), sl], in_=ob[bk_][:]),
                waits=[(sv, p.count(sv))], inc=(so, 16))
            o_free[bk_] = p.count(so)
    p.barrier("gpsimd", [(so, p.count(so))])


def build_l0attn_launch(NH=8):
    nc = bass.Bass("TRN2", target_bir_lowering=False)
    T = ATT_T
    DH = 64 * NH
    xT = nc.dram_tensor("xT", [D, T], F32, kind="ExternalInput").ap()
    waps = {
        "wqT": nc.dram_tensor("wqT", [512, DH], F32, kind="ExternalInput").ap(),
        "wkT": nc.dram_tensor("wkT", [512, 128], F32, kind="ExternalInput").ap(),
        "wvT": nc.dram_tensor("wvT", [512, 128], F32, kind="ExternalInput").ap(),
        "woR": nc.dram_tensor("woR", [DH, D], F32, kind="ExternalInput").ap(),
        "phase": nc.dram_tensor("phase", [1, NH], F32, kind="ExternalInput").ap(),
        "slopes": nc.dram_tensor("slopes", [1, 8 * NH], F32, kind="ExternalInput").ap(),
    }
    yT = nc.dram_tensor("yT", [D, T], F32, kind="ExternalOutput").ap()
    prog = Prog(nc)
    with ExitStack() as stack:
        ctx = Ctx(nc, prog, stack, T)
        emit_l0attn(ctx, xT, waps, yT, NH)
        block = stack.enter_context(nc.Block())
        prog.run(block, stack)
    return nc, ctx



def emit_l2attn(ctx, xT_ap, waps, yT_ap, NH=8):
    nc, p, T = ctx.nc, ctx.prog, ctx.T
    nseg = T // TSEG
    ntc = T // 128
    M_ = 256
    TWO_PI = 2.0 * PI
    cst = ctx.use_consts(["c_ones_col", "c_ones_row", "c_ident"])

    x_ln = [ctx.sb(f"xln{k}", [128, T], BF16) for k in range(8)]
    ready = [("c_ld", p.count("c_ld"))]
    ln_done = emit_ln_stream(ctx, xT_ap, x_ln, "a2ln", ready)

    DH = 64 * NH
    wq_b = ctx.sb("a2_wq", [128, 8 * DH], BF16)
    wk_b = ctx.sb("a2_wk", [128, 8 * DH], BF16)
    wv_b = ctx.sb("a2_wv", [128, 8 * DH], BF16)
    rf_b = ctx.sb("a2_rf", [64, M_], BF16)
    wl = p.sem("a2_wl")
    for nm, buf in (("wqT", wq_b), ("wkT", wk_b), ("wvT", wv_b)):
        p.ins("gpsimd", lambda e, s, nm=nm, buf=buf: e.dma_start(
            out=buf[:].rearrange("p (k h) -> p k h", k=8),
            in_=waps[nm].rearrange("(k p) h -> p k h", p=128)),
            waits=ln_done, inc=(wl, 16))
    p.ins("gpsimd", lambda e, s: e.dma_start(out=rf_b[:], in_=waps["rfT"]),
          waits=ln_done, inc=(wl, 16))

    outb = ctx.sb("a2_outb", [128, TSEG], F32)
    outc = ctx.sb("a2_outc", [128, TSEG], F32)
    ctxT = [ctx.sb(f"a2_ctx{i}", [128, T], BF16) for i in range(NH // 2)]
    ident_bf = ctx.sb("a2_idbf", [64, 64], BF16)
    ident65 = ctx.sb("a2_id65", [65, 65], BF16)

    hd_sc = ctx.scope()
    QT = hd_sc.sb("a2_QT", [64, T], BF16)
    KT = hd_sc.sb("a2_KT", [64, T], BF16)
    Qf_t = [hd_sc.sb(f"a2_Qf{i}", [128, TSEG], BF16) for i in range(4)]
    VaT = [hd_sc.sb(f"a2_VaT{i}", [64, TSEG], BF16) for i in range(2)]
    Va = hd_sc.sb("a2_Va", [128, ntc * 68], BF16)
    Kf = [hd_sc.sb(f"a2_Kf{i}", [128, TSEG], BF16) for i in range(2)]
    KVf = hd_sc.sb("a2_KVf", [128, 4 * 72], BF16)
    kvsb = hd_sc.sb("a2_kvsb", [65, TSEG], BF16)
    skseg = hd_sc.sb("a2_sks", [1, TSEG], F32)
    sclq = hd_sc.sb("a2_sq", [1, T], F32)
    zrow = hd_sc.sb("a2_z", [1, TSEG], F32)
    crow = hd_sc.sb("a2_c", [1, TSEG], F32)
    sqs = [hd_sc.sb(f"a2_sqs{i}", [64, TSEG], F32) for i in range(2)]
    tr1 = hd_sc.sb("a2_tr1", [128, TSEG], F32)
    tr2 = hd_sc.sb("a2_tr2", [128, TSEG], I32)
    tr3 = hd_sc.sb("a2_tr3", [128, TSEG], F32)

    ps_sc = ctx.scope()
    pj_ps = [ps_sc.ps(f"a2_pj{i}", [128, TSEG]) for i in range(2)]
    pr_ps = [pj_ps[0], pj_ps[1]]
    kv_ps = ps_sc.ps("a2_kv", [65, TSEG])
    acc_ps = ps_sc.ps("a2_acc", [65, TSEG])
    bc_ps = ps_sc.ps("a2_bc", [128, TSEG])
    row_ps = ps_sc.ps("a2_row", [1, TSEG])
    vtb_ps = ps_sc.ps("a2_vtb", [128, 128], BF16)

    st = p.sem("a2_t")
    ss = p.sem("a2_s")
    sv = p.sem("a2_v")
    p.ins("vector", lambda e, s: e.tensor_copy(ident_bf[:], cst["c_ident"][0:64, 0:64]),
          waits=ln_done, inc=(sv, 1))
    W_READY = p.count(wl)

    pr_free = {}
    pj_free = {}
    sq_free = {}
    kf_free = {}


    def head(h):
        for which, wbuf, dst in (("q", wq_b, QT), ("k", wk_b, KT)):
            for g in range(nseg):
                sl = slice(g * TSEG, (g + 1) * TSEG)
                b = g % 2
                for k in range(8):
                    w = [(sv, pr_free.get(b, 0))] if k == 0 else []
                    if h == 0 and g == 0 and k == 0 and which == "q":
                        w += [(wl, W_READY), (sv, 1)]
                    p.ins("tensor", lambda e, s, k=k, sl=sl, b=b, h=h, wbuf=wbuf: e.matmul(
                        pr_ps[b][0:64, :], wbuf[:, DH * k + 64 * h:DH * k + 64 * (h + 1)],
                        x_ln[k][:, sl], start=(k == 0), stop=(k == 7)),
                        waits=w, inc=(st, 1) if k == 7 else None)
                p.ins("vector", lambda e, s, sl=sl, b=b, dst=dst: e.tensor_copy(
                    dst[:, sl], pr_ps[b][0:64, :]), waits=[(st, p.count(st))], inc=(sv, 1))
                pr_free[b] = p.count(sv)
        for g in range(nseg):
            sl = slice(g * TSEG, (g + 1) * TSEG)
            b = g % 2
            for k in range(8):
                w = [(sv, pr_free.get(b, 0))] if k == 0 else []
                p.ins("tensor", lambda e, s, k=k, sl=sl, b=b, h=h: e.matmul(
                    pr_ps[b][0:64, :], wv_b[:, DH * k + 64 * h:DH * k + 64 * (h + 1)],
                    x_ln[k][:, sl], start=(k == 0), stop=(k == 7)),
                    waits=w, inc=(st, 1) if k == 7 else None)
            p.ins("vector", lambda e, s, b=b: e.tensor_copy(
                VaT[b][:, :], pr_ps[b][0:64, :]), waits=[(st, p.count(st))], inc=(sv, 1))
            pr_free[b] = p.count(sv)
            vdone = p.count(sv)
            for j in range(4):
                tc = 4 * g + j
                w = [(sv, vdone)] if j == 0 else [(sv, p.count(sv))]
                p.ins("tensor", lambda e, s, j=j, b=b: e.transpose(
                    vtb_ps[0:128, 0:64], VaT[b][:, 128 * j:128 * (j + 1)],
                    ident_bf[:, :]), waits=w, inc=(st, 1))
                p.ins("vector", lambda e, s, tc=tc: e.tensor_copy(
                    Va[:].rearrange("p (c w) -> p c w", w=68)[:, tc, 0:64],
                    vtb_ps[0:128, 0:64]), waits=[(st, p.count(st))], inc=(sv, 1))
        for which, srct in (("q", QT), ("k", KT)):
            for g in range(nseg):
                sl = slice(g * TSEG, (g + 1) * TSEG)
                b = g % 2
                w = [(st, sq_free.get(b, 0))] if b in sq_free else []
                if g == 0 and which == "q":
                    w += [(sv, p.count(sv))]
                p.ins("scalar", lambda e, s, sl=sl, b=b, srct=srct: e.activation(
                    sqs[b][:, :], srct[0:64, sl], AF.Square), waits=w, inc=(ss, 1))
                p.ins("tensor", lambda e, s, b=b: e.matmul(
                    row_ps[:, :], cst["c_ones_col"][0:64, :], sqs[b][:, :],
                    start=True, stop=True),
                    waits=[(ss, p.count(ss)), (sv, p.count(sv))], inc=(st, 1))
                sq_free[b] = p.count(st)
                if which == "q":
                    p.ins("scalar", lambda e, s, sl=sl: e.activation(
                        sclq[0:1, sl], row_ps[:, :], AF.Exp, scale=-0.5),
                        waits=[(st, p.count(st))], inc=(ss, 1))
                    p.ins("vector", lambda e, s, sl=sl: e.tensor_scalar_mul(
                        sclq[0:1, sl], sclq[0:1, sl], 1.0 / 16),
                        waits=[(ss, p.count(ss))], inc=(sv, 1))
                else:
                    p.ins("scalar", lambda e, s: e.activation(
                        skseg[:, :], row_ps[:, :], AF.Exp, scale=-0.5),
                        waits=[(st, p.count(st))], inc=(ss, 1))
                    p.ins("vector", lambda e, s: e.tensor_scalar_mul(
                        skseg[:, :], skseg[:, :], 1.0 / 16),
                        waits=[(ss, p.count(ss))], inc=(sv, 1))
                    for j in range(4):
                        tc = 4 * g + j
                        p.ins("tensor", lambda e, s, j=j: e.matmul(
                            bc_ps[0:128, 0:1], skseg[0:1, 128 * j:128 * (j + 1)],
                            cst["c_ones_col"][0:1, 0:1], start=True, stop=True),
                            waits=[(sv, p.count(sv))], inc=(st, 1))
                        p.ins("vector", lambda e, s, tc=tc: e.tensor_copy(
                            Va[:].rearrange("p (c w) -> p c w", w=68)[:, tc, 64:65],
                            bc_ps[0:128, 0:1]), waits=[(st, p.count(st))], inc=(sv, 1))
                        p.ins("vector", lambda e, s, tc=tc: e.tensor_scalar(
                            Va[:].rearrange("p (c w) -> p c w", w=68)[:, tc, 0:64],
                            Va[:].rearrange("p (c w) -> p c w", w=68)[:, tc, 0:64],
                            bc_ps[0:128, 0:1], None, op0=ALU.mult),
                            waits=[(sv, p.count(sv))], inc=(sv, 1))
        kv_done0 = p.count(sv)
        for tc in range(ntc):
            b = tc % 2
            sl = slice(128 * tc, 128 * (tc + 1))
            w = [(sv, max(pj_free.get(b, 0), pr_free.get(b, 0)))]
            p.ins("tensor", lambda e, s, sl=sl, b=b: e.matmul(
                pj_ps[b][:, 0:M_], KT[0:64, sl], rf_b[:, :],
                start=True, stop=True), waits=w, inc=(st, 1))
            for half, colsl in ((True, slice(0, M_)), (False, slice(M_, 2 * M_))):
                off = PI / 2.0 if half else 0.0
                p.ins("vector", lambda e, s, b=b, off=off: e.tensor_scalar(
                    tr2[:, 0:M_], pj_ps[b][:, 0:M_], off, 1.0 / TWO_PI,
                    op0=ALU.add, op1=ALU.mult),
                    waits=[(st, p.count(st)), (ss, p.count(ss))])
                p.ins("vector", lambda e, s: e.tensor_copy(tr3[:, 0:M_], tr2[:, 0:M_]))
                p.ins("vector", lambda e, s, b=b, off=off: e.tensor_scalar(
                    tr1[:, 0:M_], pj_ps[b][:, 0:M_], off, 1.0,
                    op0=ALU.add, op1=ALU.mult))
                p.ins("vector", lambda e, s: e.scalar_tensor_tensor(
                    tr1[:, 0:M_], tr3[:, 0:M_], -TWO_PI, tr1[:, 0:M_],
                    op0=ALU.mult, op1=ALU.add), inc=(sv, 1))
                p.ins("scalar", lambda e, s, b=b, colsl=colsl: e.activation(
                    Kf[b][:, colsl], tr1[:, 0:M_], AF.Sin),
                    waits=[(sv, p.count(sv))] + ([(st, kf_free.get(b, 0))] if half else []),
                    inc=(ss, 1))
            p.ins("tensor", lambda e, s, tc=tc, b=b: e.matmul(
                kv_ps[:, :], Va[:].rearrange("p (c w) -> p c w", w=68)[:, tc, 0:65],
                Kf[b][:, :], start=(tc == 0), stop=(tc == ntc - 1)),
                waits=[(ss, p.count(ss))], inc=(st, 1))
            kf_free[b] = p.count(st)
        p.ins("vector", lambda e, s: e.tensor_copy(kvsb[:, :], kv_ps[:, :]),
              waits=[(st, p.count(st))], inc=(sv, 1))
        for j in range(4):
            w = [(sv, p.count(sv))] if j == 0 else [(sv, p.count(sv))]
            p.ins("tensor", lambda e, s, j=j: e.transpose(
                vtb_ps[0:128, 0:65], kvsb[:, 128 * j:128 * (j + 1)],
                ident65[:, :]), waits=w, inc=(st, 1))
            p.ins("vector", lambda e, s, j=j: e.tensor_copy(
                KVf[:].rearrange("p (c w) -> p c w", w=72)[:, j, 0:65],
                vtb_ps[0:128, 0:65]), waits=[(st, p.count(st))], inc=(sv, 1))
        qf_free = 0
        for qi in range(nseg):
            qsl = slice(qi * TSEG, (qi + 1) * TSEG)
            for mc in range(2):
                b = mc
                w = [(sv, max(pj_free.get(b, 0), pr_free.get(b, 0)))]
                if qi == 0:
                    w += [(sv, p.count(sv))]
                p.ins("tensor", lambda e, s, mc=mc, qsl=qsl, b=b: e.matmul(
                    pj_ps[b][:, :], rf_b[:, 128 * mc:128 * (mc + 1)], QT[0:64, qsl],
                    start=True, stop=True), waits=w, inc=(st, 1))
                pjc = p.count(st)
                for part, half in ((mc, True), (2 + mc, False)):
                    off = PI / 2.0 if half else 0.0
                    p.ins("vector", lambda e, s, b=b, off=off: e.tensor_scalar(
                        tr2[:, :], pj_ps[b][:, :], off, 1.0 / TWO_PI,
                        op0=ALU.add, op1=ALU.mult),
                        waits=[(st, pjc), (ss, p.count(ss))])
                    p.ins("vector", lambda e, s: e.tensor_copy(tr3[:, :], tr2[:, :]))
                    p.ins("vector", lambda e, s, b=b, off=off: e.tensor_scalar(
                        tr1[:, :], pj_ps[b][:, :], off, 1.0, op0=ALU.add, op1=ALU.mult))
                    p.ins("vector", lambda e, s: e.scalar_tensor_tensor(
                        tr1[:, :], tr3[:, :], -TWO_PI, tr1[:, :],
                        op0=ALU.mult, op1=ALU.add), inc=(sv, 1))
                    p.ins("scalar", lambda e, s, part=part: e.activation(
                        Qf_t[part][:, :], tr1[:, :], AF.Sin),
                        waits=[(sv, p.count(sv)), (st, qf_free)], inc=(ss, 1))
                p.barrier("vector", [(ss, p.count(ss))], sem="a2_v")
                pj_free[b] = p.count(sv)
            for fc in range(4):
                w = [(ss, p.count(ss)), (sv, p.count(sv))] if fc == 0 else []
                p.ins("tensor", lambda e, s, fc=fc: e.matmul(
                    acc_ps[:, :], KVf[:].rearrange("p (c w) -> p c w", w=72)[:, fc, 0:65],
                    Qf_t[fc][:, :], start=(fc == 0), stop=(fc == 3)),
                    waits=w, inc=(st, 1) if fc == 3 else None)
            qf_free = p.count(st)
            p.ins("vector", lambda e, s: e.tensor_copy(zrow[:], acc_ps[64:65, :]),
                  waits=[(st, p.count(st))], inc=(sv, 1))
            p.ins("vector", lambda e, s, qsl=qsl: e.tensor_mul(
                zrow[:], zrow[:], sclq[0:1, qsl]), waits=[(sv, p.count(sv))],
                inc=(sv, 1))
            p.ins("vector", lambda e, s: e.tensor_scalar_max(zrow[:], zrow[:], 1e-6),
                  waits=[(sv, p.count(sv))], inc=(sv, 1))
            p.ins("vector", lambda e, s: e.reciprocal(crow[:], zrow[:]),
                  waits=[(sv, p.count(sv))], inc=(sv, 1))
            p.ins("vector", lambda e, s, qsl=qsl: e.tensor_mul(
                crow[:], crow[:], sclq[0:1, qsl]), waits=[(sv, p.count(sv))],
                inc=(sv, 1))
            p.ins("tensor", lambda e, s: e.matmul(
                bc_ps[0:64, :], cst["c_ones_row"][0:1, 0:64], crow[:, :],
                start=True, stop=True), waits=[(sv, p.count(sv))], inc=(st, 1))
            p.ins("scalar", lambda e, s: e.activation(
                outb[0:64, :], acc_ps[0:64, :], AF.Copy),
                waits=[(st, p.count(st))], inc=(ss, 1))
            p.ins("vector", lambda e, s, h=h, qsl=qsl: e.tensor_mul(
                ctxT[h // 2][64 * (h % 2):64 * (h % 2) + 64, qsl],
                outb[0:64, :], bc_ps[0:64, :]),
                waits=[(ss, p.count(ss))], inc=(sv, 1))

    p.ins("vector", lambda e, s: e.tensor_copy(ident65[:], cst["c_ident"][0:65, 0:65]),
          waits=ln_done, inc=(sv, 1))

    for h in range(NH):
        head(h)

    ps_sc.close()
    hd_sc.close()
    attn_all_done = p.count(sv)
    wo_b = ctx.sb("a2_wo", [128, (DH // 128) * D], BF16)
    p.ins("gpsimd", lambda e, s: e.dma_start(
        out=wo_b[:].rearrange("p (k d) -> p k d", k=DH // 128),
        in_=waps["woR"].rearrange("(k p) d -> p k d", p=128)),
        waits=[("a2_v", attn_all_done)], inc=(wl, 16))
    wo_rdy = p.count(wl)
    y_ps = [ctx.ps(f"a2_y{i}", [128, TSEG]) for i in range(2)]
    so = p.sem("a2_o")
    ob = [outb, outc]
    y_free2 = {}
    for g in range(nseg):
        sl = slice(g * TSEG, (g + 1) * TSEG)
        for dc in range(8):
            i = g * 8 + dc
            bk_ = i % 2
            for kc in range(DH // 128):
                w = []
                if kc == 0:
                    w = [(sv, y_free2.get(bk_, 0))]
                    if i == 0:
                        w += [(wl, wo_rdy), (sv, attn_all_done)]
                p.ins("tensor", lambda e, s, kc=kc, dc=dc, sl=sl, bk_=bk_: e.matmul(
                    y_ps[bk_][:, :], wo_b[:, D * kc + 128 * dc:D * kc + 128 * (dc + 1)],
                    ctxT[kc][:, sl], start=(kc == 0), stop=(kc == DH // 128 - 1)),
                    waits=w, inc=(st, 1) if kc == DH // 128 - 1 else None)
            p.ins("vector", lambda e, s, bk_=bk_: e.tensor_copy(
                ob[bk_][:], y_ps[bk_][:, :]),
                waits=[(st, p.count(st)), (so, p.count(so))], inc=(sv, 1))
            y_free2[bk_] = p.count(sv)
            p.ins("gpsimd", lambda e, s, dc=dc, sl=sl, bk_=bk_: e.dma_start(
                out=yT_ap[128 * dc:128 * (dc + 1), sl], in_=ob[bk_][:]),
                waits=[(sv, p.count(sv))], inc=(so, 16))
    p.barrier("gpsimd", [(so, p.count(so))])


def build_l2attn_launch(NH=8):
    nc = bass.Bass("TRN2", target_bir_lowering=False)
    T = ATT_T
    DH = 64 * NH
    xT = nc.dram_tensor("xT", [D, T], F32, kind="ExternalInput").ap()
    waps = {
        "wqT": nc.dram_tensor("wqT", [D, DH], F32, kind="ExternalInput").ap(),
        "wkT": nc.dram_tensor("wkT", [D, DH], F32, kind="ExternalInput").ap(),
        "wvT": nc.dram_tensor("wvT", [D, DH], F32, kind="ExternalInput").ap(),
        "woR": nc.dram_tensor("woR", [DH, D], F32, kind="ExternalInput").ap(),
        "rfT": nc.dram_tensor("rfT", [64, 256], F32, kind="ExternalInput").ap(),
    }
    yT = nc.dram_tensor("yT", [D, T], F32, kind="ExternalOutput").ap()
    prog = Prog(nc)
    with ExitStack() as stack:
        ctx = Ctx(nc, prog, stack, T)
        emit_l2attn(ctx, xT, waps, yT, NH)
        block = stack.enter_context(nc.Block())
        prog.run(block, stack)
    return nc, ctx




def emit_l3attn(ctx, xT_ap, waps, yT_ap, NH=8):
    nc, p, T = ctx.nc, ctx.prog, ctx.T
    assert T == ATT_T
    nseg = T // TSEG
    ntc = T // 128
    U = 50
    NITER = 24
    cst = ctx.use_consts(["c_ones_col", "c_ones_row", "c_ident", "c_iota_col"])

    x_ln = [ctx.sb(f"xln{k}", [128, T], BF16) for k in range(8)]
    ready = [("c_ld", p.count("c_ld"))]
    ln_done = emit_ln_stream(ctx, xT_ap, x_ln, "a3ln", ready)

    DH = 64 * NH
    wq_b = ctx.sb("a3_wq", [128, 8 * DH], BF16)
    wk_b = ctx.sb("a3_wk", [128, 8 * DH], BF16)
    wv_b = ctx.sb("a3_wv", [128, 8 * DH], BF16)
    bq_b = ctx.sb("a3_bq", [1, DH], BF16)
    bk_b = ctx.sb("a3_bk", [1, DH], BF16)
    bv_b = ctx.sb("a3_bv", [1, DH], BF16)
    ident_bf = ctx.sb("a3_idbf", [128, 128], BF16)
    ones_seg = ctx.sb("a3_ones", [1, TSEG], BF16)
    wl = p.sem("a3_wl")
    for nm, buf in (("wqT", wq_b), ("wkT", wk_b), ("wvT", wv_b),
                    ("bq", bq_b), ("bk", bk_b), ("bv", bv_b)):
        if nm.startswith("w"):
            p.ins("gpsimd", lambda e, s, nm=nm, buf=buf: e.dma_start(
                out=buf[:].rearrange("p (k h) -> p k h", k=8),
                in_=waps[nm].rearrange("(k p) h -> p k h", p=128)),
                waits=ln_done, inc=(wl, 16))
        else:
            p.ins("gpsimd", lambda e, s, nm=nm, buf=buf: e.dma_start(
                out=buf[:], in_=waps[nm]), waits=ln_done, inc=(wl, 16))
    sv = p.sem("a3_v")
    st = p.sem("a3_t")
    ss = p.sem("a3_s")
    p.ins("vector", lambda e, s: e.memset(ones_seg[:], 1.0),
          waits=ln_done, inc=(sv, 1))
    p.ins("vector", lambda e, s: e.tensor_copy(ident_bf[:], cst["c_ident"][:, :]),
          inc=(sv, 1))

    idx_i = ctx.sb("a3_idxi", [1, U], I32)
    idxf = ctx.sb("a3_idxf", [1, U], F32)
    selsc = ctx.sb("a3_ssc", [128, U], F32)
    SELb = ctx.sb("a3_SEL", [128, ntc * U], BF16)
    il = p.sem("a3_il")
    p.ins("sync", lambda e, s: e.dma_start(out=idx_i[:], in_=waps["idx"]),
          waits=ln_done, inc=(il, 16))
    p.ins("vector", lambda e, s: e.tensor_copy(idxf[:], idx_i[:]),
          waits=[(il, 16), (sv, p.count(sv))], inc=(sv, 1))
    p.ins("vector", lambda e, s: e.tensor_scalar(
        idxf[:], idxf[:], float(T - 1), None, op0=ALU.min),
        waits=[(sv, p.count(sv))], inc=(sv, 1))

    outb = ctx.sb("a3_outb", [128, TSEG], F32)
    outc = ctx.sb("a3_outc", [128, TSEG], F32)
    ctxT = [ctx.sb(f"a3_ctx{i}", [128, T], BF16) for i in range(NH // 2)]
    vm_sb = ctx.sb("a3_vm", [64, TSEG], F32)
    arow = ctx.sb("a3_ar", [64, TSEG], F32)
    mrow = ctx.sb("a3_mr", [1, T], BF16)
    Mall = ctx.sb("a3_M", [128, 32], F32)
    bmsk = ctx.sb("a3_bm", [128, 32], F32)
    bcnt = ctx.sb("a3_bc1", [128, 1], F32)
    vmacc = ctx.sb("a3_vma", [64, 1], F32)
    vmr = ctx.sb("a3_vmr", [1, 64], F32)
    onesf = ctx.sb("a3_onef", [1, TSEG], F32)
    ksmp = ctx.sb("a3_ks", [64, U], BF16)
    ktok = ctx.sb("a3_kt", [128, 64], BF16)
    sc1 = ctx.sb("a3_sc1", [128, 1], F32)
    sc2 = ctx.sb("a3_sc2", [128, 1], F32)
    blo = ctx.sb("a3_lo", [1, 1], F32)
    bhi = ctx.sb("a3_hi", [1, 1], F32)
    bmid = ctx.sb("a3_mid", [1, 1], F32)
    bt1 = ctx.sb("a3_t1", [1, 1], F32)
    bt2 = ctx.sb("a3_t2", [1, 1], F32)
    bcs = ctx.sb("a3_cs", [1, 1], F32)
    p.ins("vector", lambda e, s: e.memset(onesf[:], 1.0), inc=(sv, 1))

    hd_sc = ctx.scope()
    Qa = hd_sc.sb("a3_Qa", [65, T], BF16)
    Ka = hd_sc.sb("a3_Ka", [65, T], BF16)
    VaT = [hd_sc.sb(f"a3_VaT{i}", [64, TSEG], BF16) for i in range(2)]
    Va = hd_sc.sb("a3_Va", [128, ntc * 72], BF16)
    q2r = hd_sc.sb("a3_q2", [1, T], BF16)
    k2r = hd_sc.sb("a3_k2", [1, T], BF16)
    sqs = [hd_sc.sb(f"a3_sqs{i}", [64, TSEG], F32) for i in range(2)]
    kmax = hd_sc.sb("a3_km", [1, 1], F32)
    P_b = [hd_sc.sb(f"a3_P{i}", [128, TSEG], BF16) for i in range(2)]
    drow = hd_sc.sb("a3_dr", [1, TSEG], F32)
    drec = hd_sc.sb("a3_drec", [1, TSEG], F32)

    ps_sc = ctx.scope()
    pr_ps = [ps_sc.ps(f"a3_pr{i}", [64, TSEG]) for i in range(2)]
    S_ps = [ps_sc.ps(f"a3_S{i}", [128, TSEG]) for i in range(2)]
    acc_ps = ps_sc.ps("a3_acc", [65, TSEG])
    bc_ps = ps_sc.ps("a3_bc", [64, TSEG])
    row_ps = ps_sc.ps("a3_row", [1, TSEG])
    vt_ps = ps_sc.ps("a3_vt", [128, 64], BF16)

    W_READY = p.count(wl)
    pr_free = {}
    S_free = {}
    P_free = {}
    sq_free = {}

    p.ins("tensor", lambda e, s: e.matmul(
        S_ps[0][:, 0:U], cst["c_ones_row"][0:1, :], idxf[:, :],
        start=True, stop=True), waits=[(sv, p.count(sv))], inc=(st, 1))
    for c in range(ntc):
        w = [(st, p.count(st))] if c == 0 else [(sv, p.count(sv))]
        p.ins("vector", lambda e, s, c=c: e.tensor_scalar(
            selsc[:, :], S_ps[0][:, 0:U], float(-128 * c), None, op0=ALU.add),
            waits=w, inc=(sv, 1))
        p.ins("vector", lambda e, s, c=c: e.tensor_scalar(
            SELb[:, U * c:U * (c + 1)], selsc[:, :], cst["c_iota_col"][:, 0:1],
            None, op0=ALU.is_equal), waits=[(sv, p.count(sv))], inc=(sv, 1))
    sel_done = p.count(sv)

    def proj_qk(h, wbuf, bbuf, dst):
        for g in range(nseg):
            sl = slice(g * TSEG, (g + 1) * TSEG)
            b = g % 2
            for k in range(8):
                w = []
                if k == 0:
                    w = [(sv, pr_free.get(b, 0))]
                    if h == 0 and g == 0:
                        w += [(wl, W_READY), (sv, sel_done)]
                p.ins("tensor", lambda e, s, k=k, sl=sl, b=b, h=h, wbuf=wbuf: e.matmul(
                    pr_ps[b][:, :], wbuf[:, DH * k + 64 * h:DH * k + 64 * (h + 1)],
                    x_ln[k][:, sl], start=(k == 0), stop=False), waits=w)
            p.ins("tensor", lambda e, s, h=h, b=b, bbuf=bbuf: e.matmul(
                pr_ps[b][:, :], bbuf[0:1, 64 * h:64 * (h + 1)], ones_seg[0:1, :],
                start=False, stop=True), inc=(st, 1))
            p.ins("vector", lambda e, s, sl=sl, b=b, dst=dst: e.tensor_copy(
                dst[0:64, sl], pr_ps[b][:, :]), waits=[(st, p.count(st))],
                inc=(sv, 1))
            pr_free[b] = p.count(sv)

    def proj_v(h):
        p.ins("vector", lambda e, s: e.memset(vmacc[:], 0.0), inc=(sv, 1))
        for g in range(nseg):
            sl = slice(g * TSEG, (g + 1) * TSEG)
            b = g % 2
            for k in range(8):
                w = [(sv, pr_free.get(b, 0))] if k == 0 else []
                p.ins("tensor", lambda e, s, k=k, sl=sl, b=b, h=h: e.matmul(
                    pr_ps[b][:, :], wv_b[:, DH * k + 64 * h:DH * k + 64 * (h + 1)],
                    x_ln[k][:, sl], start=(k == 0), stop=False), waits=w)
            p.ins("tensor", lambda e, s, h=h, b=b: e.matmul(
                pr_ps[b][:, :], bv_b[0:1, 64 * h:64 * (h + 1)], ones_seg[0:1, :],
                start=False, stop=True), inc=(st, 1))
            p.ins("vector", lambda e, s, b=b: e.tensor_copy(
                VaT[b][:, :], pr_ps[b][:, :]), waits=[(st, p.count(st))], inc=(sv, 1))
            vdone = p.count(sv)
            p.ins("vector", lambda e, s, b=b: e.tensor_reduce(
                sqs[0][:, 0:1], pr_ps[b][:, :], axis=AX.X, op=ALU.add),
                waits=[(sv, vdone)], inc=(sv, 1))
            pr_free[b] = p.count(sv)
            p.ins("vector", lambda e, s: e.tensor_add(
                vmacc[:], vmacc[:], sqs[0][0:64, 0:1]),
                waits=[(sv, p.count(sv))], inc=(sv, 1))
            for j in range(4):
                tc = 4 * g + j
                w = [(sv, p.count(sv))]
                p.ins("tensor", lambda e, s, j=j, b=b: e.transpose(
                    vt_ps[:, :], VaT[b][:, 128 * j:128 * (j + 1)],
                    ident_bf[0:64, 0:64]), waits=w, inc=(st, 1))
                p.ins("vector", lambda e, s, tc=tc: e.tensor_copy(
                    Va[:].rearrange("p (c w) -> p c w", w=72)[:, tc, 0:64], vt_ps[:, :]),
                    waits=[(st, p.count(st))], inc=(sv, 1))
        p.ins("vector", lambda e, s: e.memset(
            Va[:].rearrange("p (c w) -> p c w", w=72)[:, :, 64:65], 1.0), inc=(sv, 1))
        p.ins("tensor", lambda e, s: e.matmul(
            row_ps[:, 0:64], vmacc[:, :], cst["c_ident"][0:64, 0:64],
            start=True, stop=True), waits=[(sv, p.count(sv))], inc=(st, 1))
        p.ins("vector", lambda e, s: e.tensor_scalar_mul(
            vmr[:], row_ps[0:1, 0:64], 1.0 / T), waits=[(st, p.count(st))],
            inc=(sv, 1))
        p.ins("tensor", lambda e, s: e.matmul(
            bc_ps[:, :], vmr[:, :], onesf[:, :], start=True, stop=True),
            waits=[(sv, p.count(sv))], inc=(st, 1))
        p.ins("vector", lambda e, s: e.tensor_copy(vm_sb[:, :], bc_ps[:, :]),
              waits=[(st, p.count(st))], inc=(sv, 1))

    def norms(h):
        rows_done = p.count(sv)
        for which, src_t, dst_r in (("q", Qa, q2r), ("k", Ka, k2r)):
            for g in range(nseg):
                sl = slice(g * TSEG, (g + 1) * TSEG)
                b = g % 2
                w = [(sv, rows_done)] if (which == "q" and g == 0) else []
                if (b in sq_free):
                    w = w + [(st, sq_free[b])]
                p.ins("scalar", lambda e, s, sl=sl, b=b, src_t=src_t: e.activation(
                    sqs[b][:, :], src_t[0:64, sl], AF.Square), waits=w, inc=(ss, 1))
                p.ins("tensor", lambda e, s, b=b: e.matmul(
                    row_ps[:, :], cst["c_ones_col"][0:64, :], sqs[b][:, :],
                    start=True, stop=True),
                    waits=[(ss, p.count(ss)), (sv, p.count(sv))], inc=(st, 1))
                sq_free[b] = p.count(st)
                p.ins("vector", lambda e, s, sl=sl, dst_r=dst_r: e.tensor_copy(
                    dst_r[:, sl], row_ps[:, :]), waits=[(st, p.count(st))],
                    inc=(sv, 1))
        p.ins("scalar", lambda e, s: e.activation(q2r[:], q2r[:], AF.Sqrt),
              waits=[(sv, p.count(sv))], inc=(ss, 1))
        p.ins("scalar", lambda e, s: e.activation(k2r[:], k2r[:], AF.Sqrt),
              inc=(ss, 1))
        p.ins("vector", lambda e, s: e.reduce_max(kmax[:], k2r[:], axis=AX.X),
              waits=[(ss, p.count(ss))], inc=(sv, 1))
        p.ins("vector", lambda e, s: e.tensor_scalar(
            Qa[64:65, :], q2r[:], kmax[0:1, 0:1], -1.0, op0=ALU.mult, op1=ALU.mult),
            waits=[(sv, p.count(sv))], inc=(sv, 1))
        p.ins("vector", lambda e, s: e.memset(Ka[64:65, :], 1.0), inc=(sv, 1))

    def sel_scores(h):
        prep = p.count(sv)
        for c in range(ntc):
            w = [(sv, prep)] if c == 0 else [(sv, p.count(sv))]
            p.ins("tensor", lambda e, s, c=c: e.transpose(
                vt_ps[:, 0:64], Ka[0:64, 128 * c:128 * (c + 1)],
                ident_bf[0:64, 0:64]), waits=w, inc=(st, 1))
            p.ins("vector", lambda e, s: e.tensor_copy(ktok[:, :], vt_ps[:, 0:64]),
                  waits=[(st, p.count(st))], inc=(sv, 1))
            p.ins("tensor", lambda e, s, c=c: e.matmul(
                pr_ps[0][:, 0:U], ktok[:, :], SELb[:, U * c:U * (c + 1)],
                start=(c == 0), stop=(c == ntc - 1)),
                waits=[(sv, p.count(sv))], inc=(st, 1) if c == ntc - 1 else None)
        p.ins("vector", lambda e, s: e.tensor_copy(ksmp[:, :], pr_ps[0][:, 0:U]),
              waits=[(st, p.count(st))], inc=(sv, 1))
        ks_done = p.count(sv)
        for c in range(ntc):
            w = [(sv, ks_done)] if c == 0 else [(sv, p.count(sv))]
            p.ins("tensor", lambda e, s, c=c: e.matmul(
                S_ps[0][:, 0:U], Qa[0:64, 128 * c:128 * (c + 1)], ksmp[:, :],
                start=True, stop=True), waits=w, inc=(st, 1))
            stc = p.count(st)
            p.ins("vector", lambda e, s: e.tensor_reduce(
                sc1[:, :], S_ps[0][:, 0:U], axis=AX.X, op=ALU.max),
                waits=[(st, stc)], inc=(sv, 1))
            p.ins("vector", lambda e, s: e.tensor_reduce(
                sc2[:, :], S_ps[0][:, 0:U], axis=AX.X, op=ALU.add),
                waits=[(sv, p.count(sv))], inc=(sv, 1))
            p.ins("vector", lambda e, s, c=c: e.scalar_tensor_tensor(
                Mall[:, c:c + 1], sc2[:, :], -1.0 / U, sc1[:, :],
                op0=ALU.mult, op1=ALU.add), waits=[(sv, p.count(sv))], inc=(sv, 1))
        step = _emit_small_chain(p, sv)
        step(lambda e, s: e.tensor_reduce(sc1[:, :], Mall[:, :], axis=AX.X,
                                          op=ALU.max))
        step(lambda e, s: e.tensor_scalar_mul(bmsk[:, :], Mall[:, :], -1.0))
        step(lambda e, s: e.tensor_reduce(sc2[:, :], bmsk[:, :], axis=AX.X,
                                          op=ALU.max))
        p.ins("tensor", lambda e, s: e.matmul(
            S_ps[0][0:1, 0:128], sc1[:, :], cst["c_ident"][:, :],
            start=True, stop=True), waits=[(sv, p.count(sv))], inc=(st, 1))
        p.ins("tensor", lambda e, s: e.matmul(
            S_ps[1][0:1, 0:128], sc2[:, :], cst["c_ident"][:, :],
            start=True, stop=True), inc=(st, 1))
        stc = p.count(st)
        step(lambda e, s: e.reduce_max(bhi[:, :], S_ps[0][0:1, 0:128], axis=AX.X),
             [(st, stc)])
        step(lambda e, s: e.reduce_max(blo[:, :], S_ps[1][0:1, 0:128], axis=AX.X))
        step(lambda e, s: e.tensor_scalar(blo[:, :], blo[:, :], -1.0, None,
                                          op0=ALU.mult))
        step(lambda e, s: e.tensor_scalar(bhi[:, :], bhi[:, :], 1.0, None,
                                          op0=ALU.add))
        for it in range(NITER):
            step(lambda e, s: e.tensor_add(bmid[:, :], blo[:, :], bhi[:, :]))
            step(lambda e, s: e.tensor_scalar_mul(bmid[:, :], bmid[:, :], 0.5))
            p.ins("tensor", lambda e, s: e.matmul(
                S_ps[0][:, 0:1], cst["c_ones_row"][0:1, :], bmid[:, :],
                start=True, stop=True), waits=[(sv, p.count(sv))], inc=(st, 1))
            stc = p.count(st)
            step(lambda e, s: e.tensor_scalar(
                bmsk[:, :], Mall[:, :], S_ps[0][:, 0:1], None, op0=ALU.is_ge),
                [(st, stc)])
            step(lambda e, s: e.tensor_reduce(bcnt[:, :], bmsk[:, :], axis=AX.X,
                                              op=ALU.add))
            p.ins("tensor", lambda e, s: e.matmul(
                row_ps[:, 0:1], cst["c_ones_col"][:, 0:1], bcnt[:, :],
                start=True, stop=True), waits=[(sv, p.count(sv))], inc=(st, 1))
            stc = p.count(st)
            step(lambda e, s: e.tensor_scalar(
                bcs[:, :], row_ps[0:1, 0:1], float(U) - 0.5, None, op0=ALU.is_ge),
                [(st, stc)])
            step(lambda e, s: e.tensor_sub(bt1[:, :], bmid[:, :], blo[:, :]))
            step(lambda e, s: e.tensor_mul(bt1[:, :], bt1[:, :], bcs[:, :]))
            step(lambda e, s: e.tensor_add(blo[:, :], blo[:, :], bt1[:, :]))
            step(lambda e, s: e.tensor_sub(bt2[:, :], bmid[:, :], bhi[:, :]))
            step(lambda e, s: e.tensor_mul(bt1[:, :], bt2[:, :], bcs[:, :]))
            step(lambda e, s: e.tensor_sub(bt2[:, :], bt2[:, :], bt1[:, :]))
            step(lambda e, s: e.tensor_add(bhi[:, :], bhi[:, :], bt2[:, :]))
        p.ins("tensor", lambda e, s: e.matmul(
            S_ps[0][:, 0:1], cst["c_ones_row"][0:1, :], blo[:, :],
            start=True, stop=True), waits=[(sv, p.count(sv))], inc=(st, 1))
        stc = p.count(st)
        step(lambda e, s: e.tensor_scalar(
            bmsk[:, :], Mall[:, :], S_ps[0][:, 0:1], None, op0=ALU.is_ge),
            [(st, stc)])
        for c in range(ntc):
            p.ins("tensor", lambda e, s, c=c: e.matmul(
                row_ps[:, 0:128], bmsk[:, c:c + 1], cst["c_ident"][:, :],
                start=True, stop=True), waits=[(sv, p.count(sv))], inc=(st, 1))
            p.ins("vector", lambda e, s, c=c: e.tensor_copy(
                mrow[:, 128 * c:128 * (c + 1)], row_ps[0:1, 0:128]),
                waits=[(st, p.count(st))], inc=(sv, 1))

    def attend(h):
        prep_done = p.count(sv)
        blend_done = 0
        for qi in range(nseg):
            qsl = slice(qi * TSEG, (qi + 1) * TSEG)
            for ki in range(ntc):
                b = ki % 2
                ksl = slice(ki * 128, (ki + 1) * 128)
                w = [(ss, S_free.get(b, 0))]
                if qi == 0 and ki == 0:
                    w += [(sv, prep_done)]
                p.ins("tensor", lambda e, s, ksl=ksl, qsl=qsl, b=b: e.matmul(
                    S_ps[b][:, :], Ka[0:65, ksl], Qa[0:65, qsl],
                    start=True, stop=True), waits=w, inc=(st, 1))
                w2 = [(st, p.count(st)), (st, P_free.get(b, 0))]
                p.ins("scalar", lambda e, s, b=b: e.activation(
                    P_b[b][:], S_ps[b][:, :], AF.Exp, scale=0.125),
                    waits=w2, inc=(ss, 1))
                S_free[b] = p.count(ss)
                avw = [(ss, p.count(ss))]
                if ki == 0:
                    avw += [(sv, blend_done)]
                p.ins("tensor", lambda e, s, ki=ki, b=b: e.matmul(
                    acc_ps[:, :],
                    Va[:].rearrange("p (c w) -> p c w", w=72)[:, ki, 0:65],
                    P_b[b][:], start=(ki == 0), stop=(ki == ntc - 1)),
                    waits=avw, inc=(st, 1) if ki == ntc - 1 else None)
                P_free[b] = p.count(st)
            p.ins("vector", lambda e, s: e.tensor_copy(drow[:], acc_ps[64:65, :]),
                  waits=[(st, p.count(st))], inc=(sv, 1))
            p.ins("vector", lambda e, s: e.reciprocal(drec[:], drow[:]),
                  waits=[(sv, p.count(sv))], inc=(sv, 1))
            p.ins("tensor", lambda e, s: e.matmul(
                bc_ps[:, :], cst["c_ones_row"][0:1, 0:64], drec[:, :],
                start=True, stop=True), waits=[(sv, p.count(sv))], inc=(st, 1))
            p.ins("scalar", lambda e, s: e.activation(
                outb[0:64, :], acc_ps[0:64, :], AF.Copy),
                waits=[(st, p.count(st))], inc=(ss, 1))
            p.ins("vector", lambda e, s, qsl=qsl: e.tensor_mul(
                arow[:, :], outb[0:64, :], bc_ps[:, :]),
                waits=[(ss, p.count(ss))], inc=(sv, 1))
            p.ins("tensor", lambda e, s, qsl=qsl: e.matmul(
                bc_ps[:, :], ones_seg[0:1, 0:64], mrow[:, qsl],
                start=True, stop=True), waits=[(sv, p.count(sv))], inc=(st, 1))
            stc = p.count(st)
            p.ins("vector", lambda e, s: e.tensor_sub(
                arow[:, :], arow[:, :], vm_sb[:, :]), waits=[(st, stc)],
                inc=(sv, 1))
            p.ins("vector", lambda e, s: e.tensor_mul(
                arow[:, :], arow[:, :], bc_ps[:, :]),
                waits=[(sv, p.count(sv))], inc=(sv, 1))
            p.ins("vector", lambda e, s, h=h, qsl=qsl: e.tensor_add(
                ctxT[h // 2][64 * (h % 2):64 * (h % 2) + 64, qsl],
                arow[:, :], vm_sb[:, :]), waits=[(sv, p.count(sv))], inc=(sv, 1))
            blend_done = p.count(sv)

    for h in range(NH):
        proj_qk(h, wq_b, bq_b, Qa)
        proj_qk(h, wk_b, bk_b, Ka)
        proj_v(h)
        norms(h)
        sel_scores(h)
        attend(h)

    ps_sc.close()
    hd_sc.close()
    attn_all_done = p.count(sv)
    wo_b = ctx.sb("a3_wo", [128, (DH // 128) * D], BF16)
    boC = ctx.sb("a3_boC", [128, 8], F32)
    p.ins("gpsimd", lambda e, s: e.dma_start(
        out=wo_b[:].rearrange("p (k d) -> p k d", k=DH // 128),
        in_=waps["woR"].rearrange("(k p) d -> p k d", p=128)),
        waits=[("a3_v", attn_all_done)], inc=(wl, 16))
    p.ins("sync", lambda e, s: e.dma_start(out=boC[:], in_=waps["boC"]),
          waits=[("a3_v", attn_all_done)], inc=(wl, 16))
    wo_rdy = p.count(wl)
    y_ps = [ctx.ps(f"a3_y{i}", [128, TSEG]) for i in range(2)]
    so = p.sem("a3_o")
    ctx_done = p.count(sv)
    ob = [outb, outc]
    y_free2 = {}
    o_free = {}
    for g in range(nseg):
        sl = slice(g * TSEG, (g + 1) * TSEG)
        for dc in range(8):
            i = g * 8 + dc
            bk_ = i % 2
            for kc in range(DH // 128):
                w = []
                if kc == 0:
                    w = [(sv, y_free2.get(bk_, 0))]
                    if i == 0:
                        w += [(wl, wo_rdy), (sv, ctx_done)]
                p.ins("tensor", lambda e, s, kc=kc, dc=dc, sl=sl, bk_=bk_: e.matmul(
                    y_ps[bk_][:, :], wo_b[:, D * kc + 128 * dc:D * kc + 128 * (dc + 1)],
                    ctxT[kc][:, sl], start=(kc == 0), stop=(kc == DH // 128 - 1)),
                    waits=w, inc=(st, 1) if kc == DH // 128 - 1 else None)
            p.ins("vector", lambda e, s, bk_=bk_, dc=dc: e.tensor_scalar(
                ob[bk_][:], y_ps[bk_][:, :], boC[:, dc:dc + 1], None, op0=ALU.add),
                waits=[(st, p.count(st)), (so, o_free.get(bk_, 0))], inc=(sv, 1))
            y_free2[bk_] = p.count(sv)
            p.ins("gpsimd", lambda e, s, dc=dc, sl=sl, bk_=bk_: e.dma_start(
                out=yT_ap[128 * dc:128 * (dc + 1), sl], in_=ob[bk_][:]),
                waits=[(sv, p.count(sv))], inc=(so, 16))
            o_free[bk_] = p.count(so)
    p.barrier("gpsimd", [(so, p.count(so))])


def build_l3attn_launch(NH=8):
    nc = bass.Bass("TRN2", target_bir_lowering=False)
    T = ATT_T
    DH = 64 * NH
    xT = nc.dram_tensor("xT", [D, T], F32, kind="ExternalInput").ap()
    waps = {
        "wqT": nc.dram_tensor("wqT", [D, DH], F32, kind="ExternalInput").ap(),
        "bq": nc.dram_tensor("bq", [1, DH], F32, kind="ExternalInput").ap(),
        "wkT": nc.dram_tensor("wkT", [D, DH], F32, kind="ExternalInput").ap(),
        "bk": nc.dram_tensor("bk", [1, DH], F32, kind="ExternalInput").ap(),
        "wvT": nc.dram_tensor("wvT", [D, DH], F32, kind="ExternalInput").ap(),
        "bv": nc.dram_tensor("bv", [1, DH], F32, kind="ExternalInput").ap(),
        "woR": nc.dram_tensor("woR", [DH, D], F32, kind="ExternalInput").ap(),
        "idx": nc.dram_tensor("idx", [1, 50], I32, kind="ExternalInput").ap(),
        "boC": nc.dram_tensor("boC", [128, 8], F32, kind="ExternalInput").ap(),
    }
    yT = nc.dram_tensor("yT", [D, T], F32, kind="ExternalOutput").ap()
    prog = Prog(nc)
    with ExitStack() as stack:
        ctx = Ctx(nc, prog, stack, T)
        emit_l3attn(ctx, xT, waps, yT, NH)
        block = stack.enter_context(nc.Block())
        prog.run(block, stack)
    return nc, ctx
